# version 02
# baseline (speedup 1.0000x reference)
import time
import numpy as np
import concourse.bass as bass
import concourse.mybir as mybir
from concourse.bass import AutoArena
from concourse.bass_utils import run_bass_kernel_spmd

F32 = mybir.dt.float32
BF16 = mybir.dt.bfloat16
U8 = mybir.dt.uint8
A = mybir.AluOpType
AF = mybir.ActivationFunctionType
AX = mybir.AxisListType

D = 1024
NTOK = 4096
NT = 32
NG = 8
H = 16
HD = 64
NOWN = 2048
NTO = 16
KB = 1024


def ts(i, n=128):
    return slice(i * n, (i + 1) * n)


class B:
    ENGINES = ("sync", "tensor", "vector", "scalar", "gpsimd")

    def __init__(self, nc):
        self.nc = nc
        self.ops = {e: [] for e in self.ENGINES}
        self.dirty = {e: False for e in self.ENGINES}
        self.used = {e: False for e in self.ENGINES}
        self.ph_cnt = 0
        self.phd_cnt = {"sync": 0, "gpsimd": 0, "scalar": 0, "vector": 0}

    def on(self, eng, fn, dma=False):
        if dma:
            self.ops[eng].append(["dma", fn, 16])
            self.phd_cnt[eng] += 16
        else:
            self.ops[eng].append(["op", fn, 1])
            self.dirty[eng] = True
        self.used[eng] = True

    def barrier(self):
        for eng in self.ENGINES:
            if self.dirty[eng]:
                for i in range(len(self.ops[eng]) - 1, -1, -1):
                    if self.ops[eng][i][0] == "op":
                        self.ops[eng][i][0] = "op_inc"
                        self.ph_cnt += 1
                        break
                self.dirty[eng] = False
        v = (self.ph_cnt, self.phd_cnt["sync"], self.phd_cnt["gpsimd"])
        for eng in self.ENGINES:
            self.ops[eng].append(["barrier", v, 0])

    def build_block(self):
        nc = self.nc
        c1 = nc.semaphore("ph"); ph = c1.__enter__()
        c2 = nc.semaphore("phds"); phds = c2.__enter__()
        c3 = nc.semaphore("phdg"); phdg = c3.__enter__()
        with nc.Block() as block:
            for eng in self.ENGINES:
                if not self.used[eng]:
                    continue
                lst = self.ops[eng]

                def make(lst, eng=None):
                    def body(e):
                        last_wait = (-1, -1, -1)
                        for it in lst:
                            kind, fn = it[0], it[1]
                            if kind == "barrier":
                                if fn[0] > 0 and fn[0] != last_wait[0]:
                                    e.wait_ge(ph, fn[0])
                                if fn[1] > 0 and fn[1] != last_wait[1]:
                                    e.wait_ge(phds, fn[1])
                                if fn[2] > 0 and fn[2] != last_wait[2]:
                                    e.wait_ge(phdg, fn[2])
                                last_wait = fn
                            elif kind == "op":
                                fn(e)
                            elif kind == "op_inc":
                                fn(e).then_inc(ph, 1)
                            elif kind == "dma":
                                fn(e).then_inc(phds if eng == "sync" else phdg, 16)
                    return body
                getattr(block, eng)(make(lst, eng))
        c3.__exit__(None, None, None)
        c2.__exit__(None, None, None)
        c1.__exit__(None, None, None)


def build_program(stage):
    NGL = NG
    PASS2 = True
    nc = bass.Bass("TRN2", target_bir_lowering=False)

    def din(name, shape, dtype=F32):
        return nc.dram_tensor(name, list(shape), dtype, kind="ExternalInput").ap()

    def dout(name, shape, dtype=F32):
        return nc.dram_tensor(name, list(shape), dtype, kind="ExternalOutput").ap()

    x_own = din("x_own", [128, NTO * D])
    x_part = din("x_part", [128, NTO * D])
    wqkvT = din("wqkvT", [128, 8 * 3 * D])
    woutT = din("woutT", [128, 8 * D])
    bout_in = din("bout", [1, D])
    cn_w = [din(f"cn_w{i}", s) for i, s in enumerate(
        [(128, 8 * D), (128, 8 * 512), (128, 4 * 256), (128, 2 * 128)], 1)]
    cn_b = [din(f"cn_b{i}", (128, s)) for i, s in enumerate([8, 4, 2, 1], 1)]
    gc_w = [din(f"gc_w{i}", s) for i, s in enumerate(
        [(128, 8 * 512), (128, 4 * 256), (128, 2 * 128)], 1)]
    gc_b = [din(f"gc_b{i}", (128, s)) for i, s in enumerate([4, 2, 1], 1)]
    ar_in = din("ar", [1, 1])
    wq_in = din("wq", [128, 8 * D])
    wk_in = din("wk", [128, 8 * D])
    wv_in = din("wv", [128, 8 * D])
    wo_in = din("wo", [128, 8 * D])
    bq_in = din("bq", [128, 8])
    bk_in = din("bk", [128, 8])
    bv_in = din("bvrow", [1, D])
    bo_in = din("borow", [1, D])
    w1e_in = [din(f"w1e{e}", [128, 8 * 4096]) for e in range(8)]
    w2e_in = [din(f"w2e{e}", [128, 16 * D]) for e in range(8)]
    b1g_in = din("b1g", [128, 128])
    b1l_in = din("b1l", [128, 128])
    b2mat_in = din("b2mat", [8, D])
    gatew_in = din("gatew", [128, 64])
    gbrow_in = din("gbrow", [1, 8])
    nsc_in = din("nsc", [128, 8])
    out_own = dout("out_own", [128, NTO * D])
    xsp1 = nc.dram_tensor("xsp1", [128, NTO * D], F32, kind="Internal").ap()
    xsp2 = nc.dram_tensor("xsp2", [128, NTO * D], F32, kind="Internal").ap()

    arena = AutoArena(nc)
    ARENA_SZ = 212480
    arena.size = ARENA_SZ
    arena.allocate()
    placed = {}

    def place(name, shape, dtype, off):
        nbytes = int(np.prod(shape[1:])) * mybir.dt.size(dtype)
        assert off + nbytes <= ARENA_SZ, (name, off, nbytes)
        hnd = nc.virtual_sbuf_tensor(name, list(shape), dtype)
        nc._tensor(hnd.name, hnd.shape, hnd.dtype, type="SB")
        mloc = nc.lookup_mloc(hnd)
        mloc.allocated = True
        mloc.addr = arena.offset + off
        placed[name] = (off, nbytes)
        return hnd.ap()

    R0, R1, R2, R3, R4, ST = 0, 65536, 98304, 163840, 188416, 206848
    X = place("X", [128, NTO * D], F32, R0)
    YS = place("YS", [128, NT * D], BF16, R0)
    XP = place("XP", [128, NTO * D], BF16, R1)
    WQKV = place("WQKV", [128, 8 * 3 * D], BF16, R2)
    WOUT = place("WOUT", [128, 8 * D], BF16, R2 + 49152)
    WCN1 = place("WCN1", [128, 8 * D], BF16, R2)
    WCN2 = place("WCN2", [128, 8 * 512], BF16, R2 + 16384)
    WCN3 = place("WCN3", [128, 4 * 256], BF16, R2 + 24576)
    WCN4 = place("WCN4", [128, 2 * 128], BF16, R2 + 26624)
    WGC1 = place("WGC1", [128, 8 * 512], BF16, R2 + 27136)
    WGC2 = place("WGC2", [128, 4 * 256], BF16, R2 + 35328)
    WGC3 = place("WGC3", [128, 2 * 128], BF16, R2 + 37376)
    KT = place("KT", [128, 4 * D], BF16, R3)
    VT = place("VT", [128, 4 * D], BF16, R3 + 8192)
    QT = place("QT", [128, 8 * 512], BF16, R3)
    AOT = place("AOT", [128, 8 * 512], BF16, R3 + 8192)
    OTM = place("OTM", [128, 4 * D], BF16, R3 + 16384)
    H1T = place("H1T", [128, 8 * 512], BF16, R3)
    H2T = place("H2T", [128, 4 * 512], BF16, R3 + 8192)
    H3T = place("H3T", [128, 2 * 512], BF16, R3 + 12288)
    YTT = place("YTT", [128, 8 * 512], BF16, R3 + 16384)
    XN = place("XN", [128, 4 * D], BF16, R4)
    XSTG = place("XSTG", [128, D], F32, R3)
    XSTG2 = place("XSTG2", [128, D], F32, R3 + 4096)
    TROW = place("TROW", [128, 512], BF16, R3 + 12288 + 2048)
    XT = place("XT", [128, 8 * 512], BF16, R4 + 8192)
    EW = place("EW", [128, 1024], BF16, R4 + 16384)
    IDENT = place("IDENT", [128, 128], BF16, ST)
    ONES = place("ONES", [128, 128], BF16, ST + 256)
    BOUT = place("BOUT", [128, D], BF16, ST + 512)
    SM = place("SM", [128, 64], F32, ST + 2560)
    ZT = place("ZT", [128, 64], F32, ST + 2816)
    CNB = place("CNB", [128, 32], F32, ST + 3072)
    SC2 = place("SC2", [128, 56], F32, ST + 3200)
    KVS = place("KVS", [128, 1024], BF16, ST + 3456)
    XT3 = place("XT3", [128, NT * D], BF16, R0)
    X3 = place("X3", [128, NTO * D], F32, R0)
    OACC = place("OACC", [128, 8 * NOWN], BF16, R1)
    WQ1 = place("WQ1", [128, 8 * D], BF16, R2)
    WK1 = place("WK1", [128, 8 * D], BF16, R2 + 16384)
    WV1 = place("WV1", [128, 8 * D], BF16, R2 + 32768)
    WO1 = place("WO1", [128, 8 * D], BF16, R2)
    SPAIR = place("SPAIR", [128, 2 * NTOK], BF16, R2 + 49152)
    KTP = place("KTP", [128, NTOK], BF16, R3)
    KTM = place("KTM", [128, NT * 128], BF16, R3 + 8192)
    VTM = place("VTM", [128, NT * 128], BF16, R3 + 16384)
    QTP = place("QTP", [128, NOWN], BF16, R4)
    YSEL = place("YSEL", [128, NTOK], BF16, R4 + 8192)
    SST = place("SST", [128, 64], F32, ST + 2816)
    DIAG = place("DIAG", [128, 128], BF16, ST + 3456)
    KKZ = place("KKZ", [128, 128], BF16, ST + 3712)
    QTMS = place("QTMS", [128, 128], BF16, ST + 3968)
    IOTA16 = place("IOTA16", [128, 16], F32, ST + 4224)
    KSZ = place("KSZ", [128, 2], BF16, ST + 4288)
    M16 = place("M16", [128, 16], BF16, ST + 4320)
    EX32 = place("EX32", [128, 32], F32, ST + 4352)
    BQ1 = place("BQ1", [128, 8], F32, ST + 4480)
    BK1 = place("BK1", [128, 8], F32, ST + 4512)
    TT2 = place("TT2", [128, 8 * NOWN], BF16, R1)
    W1E = place("W1E", [128, 8 * 4096], BF16, R2)
    W2E = place("W2E", [128, 16 * D], BF16, R3)
    SWIF = place("SWIF", [128, 16 * 256], BF16, R4 + 8192)
    GATEW = place("GATEW", [128, 64], BF16, ST + 3456)
    B1G = place("B1G", [128, 128], F32, ST + 2560)
    B1L = place("B1L", [128, 128], F32, ST + 5056)
    MEVAC = place("MEVAC", [128, 2048], F32, R4 + 8192)
    HLF = place("HLF", [128, 512], F32, R4 + 16384)
    WTOK = place("WTOK", [128, 128], F32, ST + 3712)
    NSC = place("NSC", [128, 8], F32, ST + 4224)
    W8TG = place("W8TG", [8, 256], BF16, ST + 4544)
    WTB = place("WTB", [128, 16], BF16, ST + 4288)
    MSK = place("MSK", [128, 6], U8, ST + 4544)
    MRV = place("MRV", [128, 8], F32, ST + 4480)
    KSUM = place("KSUM", [128, 8], BF16, ST + 5504)
    KSUMF = place("KSUMF", [128, 8], F32, ST + 5536)
    KSUMZ = place("KSUMZ", [128, 16], BF16, ST + 5568)
    ROWB = place("ROWB", [1, 1024], BF16, R4 + 16384)
    CONSTS = place("CONSTS", [128, 8], F32, ST + 3424)

    ctxs = []

    def psum_t(name, shape, dtype=F32):
        cm = nc.psum_tensor(name, list(shape), dtype)
        t = cm.__enter__()
        ctxs.append(cm)
        return t

    PS = [psum_t(f"PS{i}", (128, 512)) for i in range(4)]
    PSB = [psum_t(f"PSB{i}", (128, 1024), BF16) for i in range(4)]

    b = B(nc)

    for t in range(NTO):
        b.on("sync", lambda e, t=t: e.dma_start(out=X[:, ts(t, D)], in_=x_own[:, ts(t, D)]), dma=True)
        b.on("gpsimd", lambda e, t=t: e.dma_start(out=XP[:, ts(t, D)], in_=x_part[:, ts(t, D)]), dma=True)
    b.on("gpsimd", lambda e: e.dma_start(out=WQKV[:], in_=wqkvT[:]), dma=True)
    b.on("gpsimd", lambda e: e.dma_start(out=WOUT[:], in_=woutT[:]), dma=True)
    b.on("gpsimd", lambda e: e.dma_start(out=CNB[:, 0:8], in_=cn_b[0][:]), dma=True)
    b.on("gpsimd", lambda e: e.dma_start(out=CNB[:, 8:12], in_=cn_b[1][:]), dma=True)
    b.on("gpsimd", lambda e: e.dma_start(out=CNB[:, 12:14], in_=cn_b[2][:]), dma=True)
    b.on("gpsimd", lambda e: e.dma_start(out=CNB[:, 16:20], in_=gc_b[0][:]), dma=True)
    b.on("gpsimd", lambda e: e.dma_start(out=CNB[:, 20:22], in_=gc_b[1][:]), dma=True)
    b.on("gpsimd", lambda e: e.dma_start(out=SC2[0:1, 0:1], in_=cn_b[3][0:1, :]), dma=True)
    b.on("gpsimd", lambda e: e.dma_start(out=SC2[0:1, 1:2], in_=gc_b[2][0:1, :]), dma=True)
    b.on("gpsimd", lambda e: e.dma_start(out=SC2[0:1, 2:3], in_=ar_in[:]), dma=True)
    b.on("gpsimd", lambda e: e.dma_start(out=ROWB[:], in_=bout_in[:]), dma=True)
    b.barrier()
    b.on("gpsimd", lambda e: e.memset(ONES[:], 1.0))
    b.on("vector", lambda e: e.memset(CONSTS[:, 0:1], 0.0))
    b.on("vector", lambda e: e.memset(CONSTS[:, 1:2], 1e-5))
    b.on("vector", lambda e: e.memset(CONSTS[:, 2:3], 1e-6))
    b.on("vector", lambda e: e.memset(CONSTS[:, 3:4], 1.0))
    b.on("vector", lambda e: e.memset(KSUMF[:], 0.0))
    b.barrier()
    b.on("gpsimd", lambda e: e.affine_select(
        IDENT[:], ONES[:], [[1, 128]], A.is_equal, 0.0, base=0, channel_multiplier=-1))
    for g in range(2):
        b.on("tensor", lambda e, g=g: e.matmul(
            PS[g][:], ONES[0:1, :], ROWB[0:1, ts(g, 512)], start=True, stop=True))
    b.barrier()
    for g in range(2):
        b.on("vector", lambda e, g=g: e.tensor_copy(BOUT[:, ts(g, 512)], PS[g][:]))
    b.barrier()

    def x_tile(t):
        return X[:, ts(t, D)] if t < NTO else XP[:, ts(t - NTO, D)]

    def emit_ln_group(g, dst, src_tile=None):
        if src_tile is None:
            src_tile = x_tile
        for i in range(4):
            t = 4 * g + i
            b.on("vector", lambda e, i=i, t=t: e.bn_stats(SM[:, 16 * i:16 * i + 6], src_tile(t)[:, 0:512]))
            b.on("vector", lambda e, i=i, t=t: e.bn_stats(SM[:, 16 * i + 6:16 * i + 12], src_tile(t)[:, 512:1024]))
        b.barrier()
        for i in range(4):
            b.on("vector", lambda e, i=i: e.bn_aggr(SM[:, 16 * i + 12:16 * i + 14], SM[:, 16 * i:16 * i + 12]))
        b.barrier()
        varap = SM[:, 0:64].rearrange("p (i k) -> p i k", k=16)[:, :, 13:14]
        b.on("scalar", lambda e: e.activation(
            SC2[:, 32:36].rearrange("p (i k) -> p i k", k=1), varap, AF.Sqrt, bias=CONSTS[:, 1:2], scale=1.0))
        b.barrier()
        b.on("vector", lambda e: e.reciprocal(SC2[:, 36:40], SC2[:, 32:36]))
        b.barrier()
        for i in range(4):
            t = 4 * g + i
            b.on("vector", lambda e, i=i, t=t: e.tensor_scalar(
                dst[:, ts(i, D)], src_tile(t), SM[:, 16 * i + 12:16 * i + 13], SC2[:, 36 + i:37 + i],
                A.subtract, A.mult))
        b.barrier()

    def emit_transpose_group(src, dst):
        for i in range(4):
            for c in range(8):
                b.on("tensor", lambda e, i=i, c=c: e.matmul(
                    PSB[i][:, ts(c)], src[:, i * D + 128 * c:i * D + 128 * (c + 1)],
                    IDENT[:], is_transpose=True, start=(c == 0), stop=(c == 7)))
        b.barrier()
        for i in range(4):
            dst_ap = dst.rearrange("p (c w) -> p c w", c=8)[:, :, i * 128:(i + 1) * 128]
            b.on("vector", lambda e, i=i, dst_ap=dst_ap: e.tensor_copy(
                dst_ap, PSB[i][:].rearrange("p (s w) -> p s w", s=8)))
        b.barrier()

    def emit_elu1_evac(ps_ap, dst_ap, n):
        b.on("vector", lambda e: e.tensor_scalar(EW[:, 0:n], ps_ap, 0.0, None, A.min))
        b.on("vector", lambda e: e.tensor_scalar(EW[:, 512:512 + n], ps_ap, 0.0, None, A.max))
        b.barrier()
        b.on("scalar", lambda e: e.activation(XN[:, 0:n], EW[:, 0:n], AF.Exp))
        b.barrier()
        b.on("vector", lambda e: e.tensor_tensor(dst_ap, XN[:, 0:n], EW[:, 512:512 + n], A.add))
        b.barrier()

    ps_first = {2: True, 3: True}
    for g in range(NGL):
        emit_ln_group(g, XN)
        emit_transpose_group(XN, XT)
        for i in range(4):
            for nsl in range(2):
                for half, base in enumerate([D, 2 * D]):
                    for c in range(8):
                        b.on("tensor", lambda e, i=i, c=c, base=base, half=half, nsl=nsl: e.matmul(
                            PS[half][:],
                            XT[:, c * 512 + i * 128:c * 512 + (i + 1) * 128],
                            WQKV[:, c * 3 * D + base + nsl * 512:c * 3 * D + base + (nsl + 1) * 512],
                            start=(c == 0), stop=(c == 7)))
                b.barrier()
                emit_elu1_evac(PS[0][:], KT[:, i * D + nsl * 512:i * D + (nsl + 1) * 512], 512)
                b.on("vector", lambda e, nsl=nsl, i=i: e.tensor_copy(
                    VT[:, i * D + nsl * 512:i * D + (nsl + 1) * 512], PS[1][:]))
                b.barrier()
        for i in range(4):
            first = (g == 0 and i == 0)
            last = (g == NGL - 1 and i == 3)
            for h in range(H):
                bank = 2 + h // 8
                col0 = 64 * (h % 8)
                hp = h & ~1
                b.on("tensor", lambda e, h=h, hp=hp, i=i, bank=bank, col0=col0,
                     st=(first and (h % 8) == 0), sp=(last and (h % 8) == 7): e.matmul(
                    PS[bank][:, col0:col0 + 64],
                    KT[:, i * D + 64 * hp:i * D + 64 * (hp + 2)],
                    VT[:, i * D + 64 * h:i * D + 64 * (h + 1)],
                    start=st, stop=sp))
            for c in range(8):
                b.on("tensor", lambda e, i=i, c=c: e.matmul(
                    PS[0][:, 256 + c:257 + c],
                    KT[:, i * D + 128 * c:i * D + 128 * (c + 1)],
                    ONES[:, 0:1],
                    start=(i == 0 and c == 0), stop=(i == 3 and c == 7)))
        b.barrier()
        b.on("vector", lambda e: e.tensor_tensor(KSUMF[:], KSUMF[:], PS[0][:, 256:264], A.add))
        b.barrier()
    b.on("vector", lambda e: e.memset(KVS[:], 0.0))
    b.on("vector", lambda e: e.memset(KSUMZ[:], 0.0))
    b.barrier()
    for h in range(H):
        bank, col0 = 2 + h // 8, 64 * (h % 8)
        p0 = 64 * (h % 2)
        base = 512 * (h // 8) + 64 * (h % 8)
        b.on("vector", lambda e, bank=bank, col0=col0, p0=p0, base=base: e.tensor_copy(
            KVS[p0:p0 + 64, base:base + 64], PS[bank][p0:p0 + 64, col0:col0 + 64]))
        b.on("vector", lambda e, h=h, p0=p0: e.tensor_copy(
            KSUMZ[p0:p0 + 64, h:h + 1], KSUMF[p0:p0 + 64, h // 2:h // 2 + 1]))
    b.barrier()

    def kvs_ap(h):
        base = 512 * (h // 8) + 64 * (h % 8)
        return KVS[:, base:base + 64]

    def ksum_ap(h):
        return KSUMZ[:, h:h + 1]

    P2SEC = 99
    for g in range(NGL if PASS2 else 0):
        emit_ln_group(g, XN)
        emit_transpose_group(XN, XT)
        if P2SEC < 1:
            continue
        for ckg in range(4):
            for ck in range(2 * ckg, 2 * ckg + 2):
                bank = ck % 2
                for c in range(8):
                    b.on("tensor", lambda e, ck=ck, c=c, bank=bank: e.matmul(
                        PS[bank][:],
                        WQKV[:, c * 3 * D + ck * 128:c * 3 * D + (ck + 1) * 128],
                        XT[:, c * 512:(c + 1) * 512],
                        start=(c == 0), stop=(c == 7)))
            b.barrier()
            for ck in range(2 * ckg, 2 * ckg + 2):
                emit_elu1_evac(PS[ck % 2][:], QT[:, ck * 512:(ck + 1) * 512], 512)
            b.barrier()
        if P2SEC < 2:
            continue
        for i in range(4):
            for h in range(H):
                ck = h // 2
                b.on("tensor", lambda e, i=i, h=h, ck=ck: e.matmul(
                    PS[2][:, i * 16 + h:i * 16 + h + 1],
                    QT[:, ck * 512 + i * 128:ck * 512 + (i + 1) * 128],
                    ksum_ap(h), start=(i == 0 and h == 0), stop=(i == 3 and h == H - 1)))
        b.barrier()
        b.on("vector", lambda e: e.tensor_scalar(ZT[:, 0:64], PS[2][:, 0:64], 1e-6, None, A.add))
        b.barrier()
        b.on("vector", lambda e: e.reciprocal(ZT[:, 0:64], ZT[:, 0:64]))
        b.barrier()
        if P2SEC < 3:
            continue
        for i in range(4):
            for h in range(H):
                ck = h // 2
                bank, col0 = h // 8, 64 * (h % 8)
                b.on("tensor", lambda e, i=i, h=h, ck=ck, bank=bank, col0=col0: e.matmul(
                    PS[bank][:, col0:col0 + 64],
                    QT[:, ck * 512 + i * 128:ck * 512 + (i + 1) * 128],
                    kvs_ap(h), start=(h % 8 == 0), stop=(h % 8 == 7)))
            b.barrier()
            for h in range(H):
                bank, col0 = h // 8, 64 * (h % 8)
                b.on("vector", lambda e, i=i, h=h, bank=bank, col0=col0: e.tensor_scalar(
                    OTM[:, i * D + 64 * h:i * D + 64 * (h + 1)],
                    PS[bank][:, col0:col0 + 64],
                    ZT[:, i * 16 + h:i * 16 + h + 1], None, A.mult))
            b.barrier()
        if P2SEC < 4:
            continue
        emit_transpose_group(OTM, AOT)
        if P2SEC < 5:
            continue
        for i in range(4):
            t = 4 * g + i
            for nsl in range(2):
                for c in range(8):
                    b.on("tensor", lambda e, i=i, nsl=nsl, c=c: e.matmul(
                        PS[nsl][:],
                        AOT[:, c * 512 + i * 128:c * 512 + (i + 1) * 128],
                        WOUT[:, c * D + nsl * 512:c * D + (nsl + 1) * 512],
                        start=(c == 0), stop=(c == 7)))
            b.barrier()
            for nsl in range(2):
                b.on("vector", lambda e, nsl=nsl: e.tensor_tensor(
                    EW[:, nsl * 512:(nsl + 1) * 512], PS[nsl][:], BOUT[:, ts(nsl, 512)], A.add))
            b.barrier()
            dst = x_tile(t)
            b.on("vector", lambda e, dst=dst: e.tensor_tensor(dst, dst, EW[:, 0:1024], A.add))
            b.barrier()

    if stage == "l0attn":
        b.on("sync", lambda e: e.dma_start(out=out_own[:], in_=X[:]), dma=True)
        b.barrier()
        b.build_block()
        for c in reversed(ctxs):
            c.__exit__(None, None, None)
        arena.free()
        return nc

    for w, t in [(cn_w[0], WCN1), (cn_w[1], WCN2), (cn_w[2], WCN3), (cn_w[3], WCN4),
                 (gc_w[0], WGC1), (gc_w[1], WGC2), (gc_w[2], WGC3)]:
        b.on("gpsimd", lambda e, w=w, t=t: e.dma_start(out=t[:], in_=w[:]), dma=True)
    for t in range(NTO):
        b.on("sync", lambda e, t=t: e.dma_start(out=xsp1[:, ts(t, D)], in_=X[:, ts(t, D)]), dma=True)
    b.barrier()
    b.on("vector", lambda e: e.tensor_copy(EW[0:1, 0:3], SC2[0:1, 0:3]))
    b.barrier()
    b.on("tensor", lambda e: e.matmul(PS[2][:, 0:3], ONES[0:1, :], EW[0:1, 0:3], start=True, stop=True))
    b.barrier()
    b.on("vector", lambda e: e.tensor_copy(SC2[:, 4:7], PS[2][:, 0:3]))
    b.barrier()
    b.on("vector", lambda e: e.tensor_scalar(SC2[:, 52:53], SC2[:, 6:7], 2.0, None, A.mult))
    b.barrier()
    def ys_tile(t):
        return YS[:, t * D:(t + 1) * D]
    for g in range(NG):
        emit_ln_group(g, XN)
        for i in range(4):
            t = 4 * g + i
            b.on("vector", lambda e, i=i, t=t: e.tensor_copy(ys_tile(t), XN[:, ts(i, D)]))
        b.barrier()
    for t in range(NT):
        for c in range(8):
            b.on("tensor", lambda e, t=t, c=c: e.matmul(
                PS[2][:, 8 + c:9 + c], YS[:, t * D + 128 * c:t * D + 128 * (c + 1)],
                ONES[:, 0:1], start=(t == 0 and c == 0), stop=(t == NT - 1 and c == 7)))
    b.barrier()
    b.on("vector", lambda e: e.tensor_scalar(SC2[:, 8:16], PS[2][:, 8:16], 1.0 / NTOK, None, A.mult))
    b.barrier()
    GCB = EW
    b.on("vector", lambda e: e.tensor_copy(EW[:, 16:24], SC2[:, 8:16]))
    b.barrier()
    for co in range(4):
        for ci in range(8):
            b.on("tensor", lambda e, co=co, ci=ci: e.matmul(
                PS[2][:, 16 + co:17 + co], WGC1[:, ci * 512 + co * 128:ci * 512 + co * 128 + 128],
                EW[:, 16 + ci:17 + ci], start=(ci == 0), stop=(ci == 7)))
    b.barrier()
    for co in range(4):
        b.on("scalar", lambda e, co=co: e.activation(
            EW[:, 24 + co:25 + co], PS[2][:, 16 + co:17 + co], AF.Gelu,
            bias=CNB[:, 16 + co:17 + co], scale=1.0))
    b.barrier()
    for co in range(2):
        for ci in range(4):
            b.on("tensor", lambda e, co=co, ci=ci: e.matmul(
                PS[2][:, 20 + co:21 + co], WGC2[:, ci * 256 + co * 128:ci * 256 + co * 128 + 128],
                EW[:, 24 + ci:25 + ci], start=(ci == 0), stop=(ci == 3)))
    b.barrier()
    for co in range(2):
        b.on("scalar", lambda e, co=co: e.activation(
            EW[:, 28 + co:29 + co], PS[2][:, 20 + co:21 + co], AF.Gelu,
            bias=CNB[:, 20 + co:21 + co], scale=1.0))
    b.barrier()
    for ci in range(2):
        b.on("tensor", lambda e, ci=ci: e.matmul(
            PS[2][:, 22:23], WGC3[:, ci * 128:ci * 128 + 128],
            EW[:, 28 + ci:29 + ci], start=(ci == 0), stop=(ci == 1)))
    b.barrier()
    b.on("scalar", lambda e: e.activation(SC2[:, 33:34], PS[2][:, 22:23], AF.Sigmoid,
                                          bias=SC2[:, 5:6], scale=1.0))
    b.barrier()
    b.on("vector", lambda e: e.tensor_copy(EW[0:1, 30:31], SC2[0:1, 33:34]))
    b.barrier()
    b.on("tensor", lambda e: e.matmul(PS[2][:, 23:24], ONES[0:1, :], EW[0:1, 30:31], start=True, stop=True))
    b.barrier()
    b.on("vector", lambda e: e.tensor_scalar(SC2[:, 35:36], PS[2][:, 23:24], 0.7, None, A.mult))
    b.barrier()

    NCYC = 3
    for cyc in range(NCYC):
        for g in range(NG):
            def y_src(t):
                return ys_tile(t)
            for i in range(4):
                for c in range(8):
                    b.on("tensor", lambda e, g=g, i=i, c=c: e.matmul(
                        PSB[i][:, ts(c)], ys_tile(4 * g + i)[:, 128 * c:128 * (c + 1)],
                        IDENT[:], is_transpose=True, start=(c == 0), stop=(c == 7)))
            b.barrier()
            for i in range(4):
                dst_ap = YTT.rearrange("p (c w) -> p c w", c=8)[:, :, i * 128:(i + 1) * 128]
                b.on("vector", lambda e, i=i, dst_ap=dst_ap: e.tensor_copy(
                    dst_ap, PSB[i][:].rearrange("p (s w) -> p s w", s=8)))
            b.barrier()
            for cg in range(4):
                for co in (2 * cg, 2 * cg + 1):
                    for ci in range(8):
                        b.on("tensor", lambda e, co=co, ci=ci: e.matmul(
                            PS[co % 2][:], WCN1[:, ci * D + co * 128:ci * D + (co + 1) * 128],
                            YTT[:, ci * 512:(ci + 1) * 512], start=(ci == 0), stop=(ci == 7)))
                b.barrier()
                for co in (2 * cg, 2 * cg + 1):
                    b.on("scalar", lambda e, co=co: e.activation(
                        H1T[:, co * 512:(co + 1) * 512], PS[co % 2][:], AF.Gelu,
                        bias=CNB[:, co:co + 1], scale=1.0))
                b.barrier()
            for cg in range(2):
                for co in (2 * cg, 2 * cg + 1):
                    for ci in range(8):
                        b.on("tensor", lambda e, co=co, ci=ci: e.matmul(
                            PS[co % 2][:], WCN2[:, ci * 512 + co * 128:ci * 512 + co * 128 + 128],
                            H1T[:, ci * 512:(ci + 1) * 512], start=(ci == 0), stop=(ci == 7)))
                b.barrier()
                for co in (2 * cg, 2 * cg + 1):
                    b.on("scalar", lambda e, co=co: e.activation(
                        H2T[:, co * 512:(co + 1) * 512], PS[co % 2][:], AF.Gelu,
                        bias=CNB[:, 8 + co:9 + co], scale=1.0))
                b.barrier()
            for co in range(2):
                for ci in range(4):
                    b.on("tensor", lambda e, co=co, ci=ci: e.matmul(
                        PS[co][:], WCN3[:, ci * 256 + co * 128:ci * 256 + co * 128 + 128],
                        H2T[:, ci * 512:(ci + 1) * 512], start=(ci == 0), stop=(ci == 3)))
            b.barrier()
            for co in range(2):
                b.on("scalar", lambda e, co=co: e.activation(
                    H3T[:, co * 512:(co + 1) * 512], PS[co][:], AF.Gelu,
                    bias=CNB[:, 12 + co:13 + co], scale=1.0))
            b.barrier()
            for ci in range(2):
                b.on("tensor", lambda e, ci=ci: e.matmul(
                    PS[2][:], WCN4[:, ci * 128:ci * 128 + 128],
                    H3T[:, ci * 512:(ci + 1) * 512], start=(ci == 0), stop=(ci == 1)))
            b.barrier()
            b.on("vector", lambda e: e.tensor_copy(TROW[0:1, :], PS[2][0:1, :]))
            b.barrier()
            for i in range(4):
                b.on("tensor", lambda e, i=i: e.matmul(
                    PSB[0][:, ts(i)], TROW[:, i * 128:(i + 1) * 128],
                    IDENT[:], is_transpose=True, start=(i == 0), stop=(i == 3)))
            b.barrier()
            for i in range(4):
                b.on("vector", lambda e, i=i: e.tensor_copy(SC2[:, 36 + i:37 + i], PSB[0][:, i * 128:i * 128 + 1]))
            b.barrier()
            b.on("scalar", lambda e: e.activation(SC2[:, 40:44], SC2[:, 36:40], AF.Sigmoid,
                                                  bias=SC2[:, 4:5], scale=1.0))
            b.barrier()
            b.on("vector", lambda e: e.tensor_scalar(SC2[:, 44:48], SC2[:, 40:44], 0.3, None, A.mult))
            b.barrier()
            b.on("vector", lambda e: e.tensor_scalar(SC2[:, 44:48], SC2[:, 44:48], SC2[:, 35:36], None, A.add))
            b.barrier()
            b.on("vector", lambda e: e.tensor_scalar(SC2[:, 44:48], SC2[:, 44:48], 0.5, None, A.subtract))
            b.barrier()
            b.on("vector", lambda e: e.tensor_scalar(SC2[:, 44:48], SC2[:, 44:48], SC2[:, 52:53], 0.1, A.mult, A.add))
            b.barrier()
            b.on("vector", lambda e: e.tensor_scalar(SC2[:, 48:52], SC2[:, 44:48], -1.0, None, A.mult))
            b.barrier()
            for i in range(4):
                t = 4 * g + i
                yt = ys_tile(t)
                cpos = SC2[:, 44 + i:45 + i]
                cneg = SC2[:, 48 + i:49 + i]
                b.on("vector", lambda e, yt=yt: e.tensor_copy(EW[:, 0:512], yt[:, 0:512]))
                b.barrier()
                b.on("vector", lambda e, yt=yt, cpos=cpos: e.scalar_tensor_tensor(
                    yt[:, 0:256], yt[:, 512:768], cpos, yt[:, 0:256], A.mult, A.add))
                b.on("vector", lambda e, yt=yt, cneg=cneg: e.scalar_tensor_tensor(
                    yt[:, 256:512], yt[:, 768:1024], cneg, yt[:, 256:512], A.mult, A.add))
                b.barrier()
                b.on("vector", lambda e, yt=yt, cneg=cneg: e.scalar_tensor_tensor(
                    yt[:, 512:768], EW[:, 0:256], cneg, yt[:, 512:768], A.mult, A.add))
                b.on("vector", lambda e, yt=yt, cpos=cpos: e.scalar_tensor_tensor(
                    yt[:, 768:1024], EW[:, 256:512], cpos, yt[:, 768:1024], A.mult, A.add))
                b.barrier()

    def emit_ln_tile(src_ap, dst_ap):
        b.on("vector", lambda e: e.bn_stats(SM[:, 0:6], src_ap[:, 0:512]))
        b.on("vector", lambda e: e.bn_stats(SM[:, 6:12], src_ap[:, 512:1024]))
        b.barrier()
        b.on("vector", lambda e: e.bn_aggr(SM[:, 12:14], SM[:, 0:12]))
        b.barrier()
        b.on("scalar", lambda e: e.activation(SM[:, 14:15], SM[:, 13:14], AF.Sqrt,
                                              bias=CONSTS[:, 1:2], scale=1.0))
        b.barrier()
        b.on("vector", lambda e: e.reciprocal(SM[:, 15:16], SM[:, 14:15]))
        b.barrier()
        b.on("vector", lambda e: e.tensor_scalar(
            dst_ap, src_ap, SM[:, 12:13], SM[:, 15:16], A.subtract, A.mult))
        b.barrier()

    def xt3_slot(t):
        return XT3[:, t * D:(t + 1) * D]

    for t in range(NT):
        if t < NTO:
            b.on("sync", lambda e, t=t: e.dma_start(out=XSTG[:], in_=xsp1[:, ts(t, D)]), dma=True)
            b.barrier()
            b.on("vector", lambda e, t=t: e.tensor_tensor(XSTG2[:], XSTG[:], ys_tile(t), A.add))
            b.barrier()
            if stage == "l0":
                b.on("sync", lambda e, t=t: e.dma_start(out=out_own[:, ts(t, D)], in_=XSTG2[:]), dma=True)
                b.barrier()
                continue
            b.on("sync", lambda e, t=t: e.dma_start(out=xsp2[:, ts(t, D)], in_=XSTG2[:]), dma=True)
            src_ap = XSTG2
        else:
            b.on("vector", lambda e, t=t: e.tensor_tensor(
                XP[:, ts(t - NTO, D)], XP[:, ts(t - NTO, D)], ys_tile(t), A.add))
            b.barrier()
            if stage == "l0":
                continue
            src_ap = XP[:, ts(t - NTO, D)]
        emit_ln_tile(src_ap, XN[:, 0:D])
        for c in range(8):
            b.on("tensor", lambda e, c=c: e.matmul(
                PSB[0][:, ts(c)], XN[:, 128 * c:128 * (c + 1)],
                IDENT[:], is_transpose=True, start=(c == 0), stop=(c == 7)))
        b.barrier()
        b.on("vector", lambda e, t=t: e.tensor_copy(xt3_slot(t), PSB[0][:, 0:1024]))
        b.barrier()

    if stage == "l0":
        b.build_block()
        for c in reversed(ctxs):
            c.__exit__(None, None, None)
        arena.free()
        return nc

    Z90T = 1.2944
    TGT = 400.5
    WLO, WHI = 392.5, 408.5
    NSEL = 6

    for w, t in [(wq_in, WQ1), (wk_in, WK1), (wv_in, WV1)]:
        b.on("gpsimd", lambda e, w=w, t=t: e.dma_start(out=t[:], in_=w[:]), dma=True)
    b.on("gpsimd", lambda e: e.dma_start(out=BQ1[:], in_=bq_in[:]), dma=True)
    b.on("gpsimd", lambda e: e.dma_start(out=BK1[:], in_=bk_in[:]), dma=True)
    b.on("gpsimd", lambda e: e.dma_start(out=BOUT[0:1, :], in_=bv_in[:]), dma=True)
    b.on("vector", lambda e: e.memset(OACC[:], 0.0))
    b.on("gpsimd", lambda e: e.iota(IOTA16[:], [[1, 16]], base=0, channel_multiplier=0, allow_small_or_imprecise_dtypes=True))
    b.barrier()
    b.on("vector", lambda e: e.tensor_scalar(BQ1[:], BQ1[:], 0.125, None, A.mult))
    b.barrier()
    for t in range(NT):
        for c in range(8):
            b.on("tensor", lambda e, t=t, c=c: e.matmul(
                PS[2][:, 8 + c:9 + c], XT3[:, t * D + 128 * c:t * D + 128 * (c + 1)],
                ONES[:, 0:1], start=(t == 0 and c == 0), stop=(t == NT - 1 and c == 7)))
    b.barrier()
    b.on("vector", lambda e: e.tensor_copy(EX32[:, 0:8], PS[2][:, 8:16]))
    b.barrier()
    XS3B = M16
    b.on("vector", lambda e: e.tensor_copy(M16[:, 0:8], EX32[:, 0:8]))
    b.barrier()

    NPAIRS = 8
    NQT = NTO
    def _pair(p):
            hA, hB = 2 * p, 2 * p + 1
            for kg in range(8):
                rhs_g = XT3.rearrange("p (t cw) -> p t cw", cw=D)
                for c in range(8):
                    rhs = rhs_g[:, 4 * kg:4 * kg + 4, c * 128:(c + 1) * 128]
                    b.on("tensor", lambda e, c=c, rhs=rhs, kg=kg: e.matmul(
                        PS[kg % 2][:], WK1[:, c * D + p * 128:c * D + p * 128 + 128],
                        rhs, start=(c == 0), stop=(c == 7)))
                b.barrier()
                b.on("scalar", lambda e, kg=kg: e.activation(
                    KTP[:, kg * 512:(kg + 1) * 512], PS[kg % 2][:], AF.Identity,
                    bias=BK1[:, p:p + 1], scale=1.0))
                b.barrier()
            for qg in range(4):
                rhs_g = XT3.rearrange("p (t cw) -> p t cw", cw=D)
                for c in range(8):
                    rhs = rhs_g[:, 4 * qg:4 * qg + 4, c * 128:(c + 1) * 128]
                    b.on("tensor", lambda e, c=c, rhs=rhs, qg=qg: e.matmul(
                        PS[qg % 2][:], WQ1[:, c * D + p * 128:c * D + p * 128 + 128],
                        rhs, start=(c == 0), stop=(c == 7)))
                b.barrier()
                b.on("scalar", lambda e, qg=qg: e.activation(
                    QTP[:, qg * 512:(qg + 1) * 512], PS[qg % 2][:], AF.Identity,
                    bias=BQ1[:, p:p + 1], scale=0.125))
                b.barrier()
            for t in range(NT):
                for c in range(8):
                    b.on("tensor", lambda e, t=t, c=c: e.matmul(
                        PS[t % 2][:, 0:128], XT3[:, t * D + 128 * c:t * D + 128 * (c + 1)],
                        WK1[:, c * D + p * 128:c * D + p * 128 + 128],
                        start=(c == 0), stop=(c == 7)))
                if t % 2 == 1:
                    b.barrier()
                    for tt in (t - 1, t):
                        b.on("vector", lambda e, tt=tt: e.tensor_copy(
                            KTM[:, tt * 128:(tt + 1) * 128], PS[tt % 2][:, 0:128]))
                    b.barrier()
            for c in range(8):
                b.on("tensor", lambda e, c=c: e.matmul(
                    PS[2][:, 0:1], WK1[:, c * D + p * 128:c * D + p * 128 + 128],
                    M16[:, c:c + 1], start=(c == 0), stop=(c == 7)))
            b.barrier()
            b.on("vector", lambda e: e.tensor_scalar(EX32[:, 16:17], BK1[:, p:p + 1], 4096.0, None, A.mult))
            b.barrier()
            b.on("vector", lambda e: e.tensor_tensor(EX32[:, 17:18], PS[2][:, 0:1], EX32[:, 16:17], A.add))
            b.barrier()
            b.on("vector", lambda e: e.memset(KSZ[:], 0.0))
            b.on("vector", lambda e: e.memset(M16[:, 8:10], 0.0))
            b.barrier()
            for hh in range(2):
                b.on("vector", lambda e, hh=hh: e.tensor_copy(
                    KSZ[64 * hh:64 * hh + 64, hh:hh + 1], EX32[64 * hh:64 * hh + 64, 17:18]))
                b.on("vector", lambda e, hh=hh: e.tensor_copy(
                    M16[64 * hh:64 * hh + 64, 8 + hh:9 + hh], BQ1[64 * hh:64 * hh + 64, p:p + 1])) if False else None
                b.on("vector", lambda e, hh=hh: e.tensor_copy(
                    M16[64 * hh:64 * hh + 64, 8 + hh:9 + hh], BK1[64 * hh:64 * hh + 64, p:p + 1]))
            b.barrier()
            b.on("vector", lambda e: e.memset(KKZ[:], 0.0))
            b.barrier()
            for hh in range(2):
                for t in range(NT):
                    b.on("tensor", lambda e, hh=hh, t=t: e.matmul(
                        PS[2 + hh][:, 0:64], KTM[:, t * 128:(t + 1) * 128],
                        KTM[:, t * 128 + 64 * hh:t * 128 + 64 * hh + 64],
                        start=(t == 0), stop=(t == NT - 1)))
            b.barrier()
            for hh in range(2):
                b.on("vector", lambda e, hh=hh: e.tensor_copy(
                    KKZ[64 * hh:64 * hh + 64, 64 * hh:64 * hh + 64],
                    PS[2 + hh][64 * hh:64 * hh + 64, 0:64]))
            b.barrier()
            for t in range(NT):
                for c in range(8):
                    b.on("tensor", lambda e, t=t, c=c: e.matmul(
                        PS[t % 2][:, 0:128], XT3[:, t * D + 128 * c:t * D + 128 * (c + 1)],
                        WV1[:, c * D + p * 128:c * D + p * 128 + 128],
                        start=(c == 0), stop=False))
                b.on("tensor", lambda e, t=t: e.matmul(
                    PS[t % 2][:, 0:128], ONES[0:1, :],
                    BOUT[0:1, p * 128:(p + 1) * 128], start=False, stop=True))
                if t % 2 == 1:
                    b.barrier()
                    for tt in (t - 1, t):
                        b.on("vector", lambda e, tt=tt: e.tensor_copy(
                            VTM[:, tt * 128:(tt + 1) * 128], PS[tt % 2][:, 0:128]))
                    b.barrier()

            def _qtile(p, qt):
                    for rnd in range(4):
                        for n in (2 * rnd, 2 * rnd + 1):
                            for hh in range(2):
                                bank = 2 * (n % 2) + hh
                                b.on("tensor", lambda e, n=n, hh=hh, bank=bank: e.matmul(
                                    PS[bank][:],
                                    QTP[64 * hh:64 * hh + 64, qt * 128:(qt + 1) * 128],
                                    KTP[64 * hh:64 * hh + 64, n * 512:(n + 1) * 512],
                                    start=True, stop=True,
                                    tile_position=(64 * hh, 0)))
                        b.barrier()
                        for n in (2 * rnd, 2 * rnd + 1):
                            for hh in range(2):
                                bank = 2 * (n % 2) + hh
                                b.on("vector", lambda e, n=n, hh=hh, bank=bank: e.tensor_copy(
                                    SPAIR[:, hh * NTOK + n * 512:hh * NTOK + (n + 1) * 512], PS[bank][:]))
                        b.barrier()
                    for hh in range(2):
                        b.on("tensor", lambda e, hh=hh: e.matmul(
                            PS[2][:, hh:hh + 1], QTP[:, qt * 128:(qt + 1) * 128],
                            KSZ[:, hh:hh + 1], start=(hh == 0), stop=False))
                    for hh in range(2):
                        b.on("tensor", lambda e, hh=hh: e.matmul(
                            PS[2][:, 2 + hh:3 + hh], QTP[:, qt * 128:(qt + 1) * 128],
                            M16[:, 8 + hh:9 + hh], start=False, stop=(hh == 1)))
                    for hh in range(2):
                        b.on("tensor", lambda e, hh=hh: e.matmul(
                            PS[3][:, 64 * hh:64 * hh + 64], QTP[:, qt * 128:(qt + 1) * 128],
                            KKZ[:, 64 * hh:64 * hh + 64], start=(hh == 0), stop=(hh == 1)))
                    b.on("tensor", lambda e: e.matmul(
                        PSB[0][:, 0:128], QTP[:, qt * 128:(qt + 1) * 128], IDENT[:],
                        is_transpose=True, start=True, stop=True))
                    b.barrier()
                    b.on("vector", lambda e: e.tensor_copy(QTMS[:], PSB[0][:, 0:128]))
                    b.on("vector", lambda e: e.tensor_scalar(SST[:, 0:2], PS[2][:, 0:2], 1.0 / NTOK, None, A.mult))
                    b.on("vector", lambda e: e.tensor_copy(SST[:, 32:34], PS[2][:, 2:4]))
                    b.barrier()
                    b.on("vector", lambda e: e.tensor_tensor(SST[:, 34:36], SST[:, 0:2], SST[:, 32:34], A.subtract))
                    b.barrier()
                    for hh in range(2):
                        b.on("vector", lambda e, hh=hh: e.tensor_tensor(
                            EW[:, 0:64], QTMS[:, 64 * hh:64 * hh + 64], PS[3][:, 64 * hh:64 * hh + 64], A.mult))
                        b.barrier()
                        b.on("vector", lambda e, hh=hh: e.reduce_sum(SST[:, 2 + hh:3 + hh], EW[:, 0:64], axis=AX.X))
                        b.barrier()
                    b.on("vector", lambda e: e.tensor_scalar(SST[:, 4:6], SST[:, 2:4], 1.0 / NTOK, None, A.mult))
                    b.barrier()
                    b.on("vector", lambda e: e.tensor_tensor(SST[:, 6:8], SST[:, 34:36], SST[:, 34:36], A.mult))
                    b.barrier()
                    b.on("vector", lambda e: e.tensor_tensor(SST[:, 4:6], SST[:, 4:6], SST[:, 6:8], A.subtract))
                    b.barrier()
                    b.on("vector", lambda e: e.tensor_scalar(SST[:, 4:6], SST[:, 4:6], 1e-6, None, A.max))
                    b.barrier()
                    b.on("scalar", lambda e: e.activation(SST[:, 6:8], SST[:, 4:6], AF.Sqrt,
                                                          bias=CONSTS[:, 2:3], scale=1.0))
                    b.barrier()
                    b.on("vector", lambda e: e.tensor_scalar(SST[:, 26:28], SST[:, 6:8], -3.34, None, A.mult))
                    b.on("vector", lambda e: e.tensor_scalar(SST[:, 18:20], SST[:, 6:8], 3.17, None, A.mult))
                    b.on("vector", lambda e: e.tensor_scalar(SST[:, 8:10], SST[:, 6:8], Z90T, None, A.mult))
                    b.barrier()
                    b.on("vector", lambda e: e.tensor_tensor(SST[:, 26:28], SST[:, 26:28], SST[:, 0:2], A.add))
                    b.on("vector", lambda e: e.tensor_tensor(SST[:, 18:20], SST[:, 18:20], SST[:, 0:2], A.add))
                    b.barrier()
                    b.on("vector", lambda e: e.tensor_tensor(SST[:, 8:10], SST[:, 8:10], SST[:, 0:2], A.add))
                    b.on("vector", lambda e: e.tensor_copy(SST[:, 14:16], SST[:, 26:28]))
                    b.barrier()
                    b.on("vector", lambda e: e.memset(SST[:, 16:18], 4096.0))
                    b.on("vector", lambda e: e.memset(SST[:, 20:22], 0.0))
                    b.on("vector", lambda e: e.tensor_copy(SST[:, 22:24], SST[:, 8:10]))
                    b.on("vector", lambda e: e.memset(SST[:, 24:26], 400.0))
                    b.barrier()
                    for it in range(NSEL):
                        b.on("vector", lambda e: e.tensor_scalar(SST[:, 10:12], SST[:, 8:10], -1.0, None, A.mult))
                        b.barrier()
                        for hh in range(2):
                            b.on("scalar", lambda e, hh=hh: e.activation(
                                YSEL[:], SPAIR[:, hh * NTOK:(hh + 1) * NTOK], AF.Sign,
                                bias=SST[:, 10 + hh:11 + hh], scale=1.0,
                                accum_out=SST[:, 12 + hh:13 + hh]))
                            b.barrier()
                        b.on("vector", lambda e: e.tensor_scalar(SST[:, 12:14], SST[:, 12:14], 4096.0, 0.5, A.add, A.mult))
                        b.barrier()
                        b.on("vector", lambda e: e.tensor_scalar(MSK[:, 0:2], SST[:, 12:14], WLO, None, A.is_gt))
                        b.on("vector", lambda e: e.tensor_scalar(MSK[:, 2:4], SST[:, 12:14], WHI, None, A.is_lt))
                        b.barrier()
                        b.on("vector", lambda e: e.tensor_tensor(MSK[:, 4:6], MSK[:, 0:2], MSK[:, 2:4], A.mult))
                        b.barrier()
                        b.on("vector", lambda e: e.copy_predicated(SST[:, 22:24], MSK[:, 4:6], SST[:, 8:10]))
                        b.on("vector", lambda e: e.copy_predicated(SST[:, 24:26], MSK[:, 4:6], SST[:, 12:14]))
                        b.barrier()
                        if it == NSEL - 1:
                            break
                        b.on("vector", lambda e: e.tensor_scalar(MSK[:, 0:2], SST[:, 12:14], TGT, None, A.is_ge))
                        b.on("vector", lambda e: e.tensor_scalar(MSK[:, 2:4], SST[:, 12:14], TGT, None, A.is_lt))
                        b.barrier()
                        b.on("vector", lambda e: e.copy_predicated(SST[:, 14:16], MSK[:, 0:2], SST[:, 8:10]))
                        b.on("vector", lambda e: e.copy_predicated(SST[:, 16:18], MSK[:, 0:2], SST[:, 12:14]))
                        b.on("vector", lambda e: e.copy_predicated(SST[:, 18:20], MSK[:, 2:4], SST[:, 8:10]))
                        b.on("vector", lambda e: e.copy_predicated(SST[:, 20:22], MSK[:, 2:4], SST[:, 12:14]))
                        b.barrier()
                        b.on("vector", lambda e: e.tensor_scalar(SST[:, 28:30], SST[:, 16:18], TGT, None, A.subtract))
                        b.on("vector", lambda e: e.tensor_tensor(SST[:, 30:32], SST[:, 16:18], SST[:, 20:22], A.subtract))
                        b.barrier()
                        b.on("vector", lambda e: e.reciprocal(SST[:, 30:32], SST[:, 30:32]))
                        b.barrier()
                        b.on("vector", lambda e: e.tensor_tensor(SST[:, 28:30], SST[:, 28:30], SST[:, 30:32], A.mult))
                        b.barrier()
                        b.on("vector", lambda e: e.tensor_scalar(SST[:, 28:30], SST[:, 28:30], 0.12, 0.88, A.max, A.min))
                        b.barrier()
                        b.on("vector", lambda e: e.tensor_tensor(SST[:, 30:32], SST[:, 18:20], SST[:, 14:16], A.subtract))
                        b.barrier()
                        b.on("vector", lambda e: e.tensor_tensor(SST[:, 28:30], SST[:, 28:30], SST[:, 30:32], A.mult))
                        b.barrier()
                        b.on("vector", lambda e: e.tensor_tensor(SST[:, 8:10], SST[:, 14:16], SST[:, 28:30], A.add))
                        b.barrier()
                    b.on("vector", lambda e: e.tensor_scalar(SST[:, 28:30], SST[:, 6:8], 6.0, None, A.mult))
                    b.barrier()
                    b.on("vector", lambda e: e.tensor_tensor(SST[:, 28:30], SST[:, 28:30], SST[:, 22:24], A.add))
                    b.barrier()
                    b.on("vector", lambda e: e.tensor_scalar(SST[:, 28:30], SST[:, 28:30], 0.0, None, A.max))
                    b.barrier()
                    b.on("vector", lambda e: e.tensor_scalar(SST[:, 30:32], SST[:, 28:30], -1.0, None, A.mult))
                    b.barrier()
                    for hh in range(2):
                        sh = SPAIR[:, hh * NTOK:(hh + 1) * NTOK]
                        b.on("vector", lambda e, sh=sh, hh=hh: e.tensor_scalar(
                            KTM[:], sh, SST[:, 26 + hh:27 + hh], 0.0, A.subtract, A.max))
                        b.on("vector", lambda e, sh=sh, hh=hh: e.tensor_scalar(
                            YSEL[:], sh, SST[:, 22 + hh:23 + hh], None, A.is_le))
                        b.barrier()
                        b.on("vector", lambda e: e.tensor_tensor(YSEL[:], YSEL[:], KTM[:], A.mult))
                        b.barrier()
                        b.on("vector", lambda e: e.max(out=M16[:, 0:8], in_=YSEL[:]))
                        b.barrier()
                        b.on("vector", lambda e: e.match_replace(out=KTM[:], in_to_replace=M16[:, 0:8],
                                                                 in_values=YSEL[:], imm_value=0.0))
                        b.barrier()
                        b.on("vector", lambda e: e.max(out=M16[:, 8:16], in_=KTM[:]))
                        b.barrier()
                        b.on("vector", lambda e, hh=hh: e.tensor_scalar(
                            EX32[:, 20:21], SST[:, 24 + hh:25 + hh], -1.0, 408.0, A.mult, A.add))
                        b.barrier()
                        b.on("vector", lambda e: e.tensor_scalar(
                            EX32[:, 0:16], IOTA16[:], EX32[:, 20:21], None, A.is_equal))
                        b.barrier()
                        b.on("vector", lambda e: e.tensor_tensor(EX32[:, 0:16], EX32[:, 0:16], M16[:], A.mult))
                        b.barrier()
                        b.on("vector", lambda e: e.reduce_sum(EX32[:, 21:22], EX32[:, 0:16], axis=AX.X))
                        b.barrier()
                        b.on("vector", lambda e, hh=hh: e.tensor_tensor(
                            EX32[:, 22:23], EX32[:, 21:22], SST[:, 26 + hh:27 + hh], A.add))
                        b.barrier()
                        if False:
                            b.build_block()
                            for c_ in reversed(ctxs):
                                c_.__exit__(None, None, None)
                            arena.free()
                            return nc
                        b.on("vector", lambda e, sh=sh: e.tensor_scalar(
                            YSEL[:], sh, EX32[:, 22:23], None, A.is_ge))
                        b.barrier()
                        b.on("vector", lambda e, sh=sh: e.tensor_tensor(KTM[:], YSEL[:], sh, A.mult))
                        b.barrier()
                        b.on("scalar", lambda e, sh=sh, hh=hh: e.activation(
                            sh, KTM[:], AF.Exp, bias=SST[:, 30 + hh:31 + hh], scale=1.0,
                            accum_out=EX32[:, 24 + hh:25 + hh]))
                        b.barrier()
                    b.on("vector", lambda e: e.reciprocal(EX32[:, 26:28], EX32[:, 24:26]))
                    b.barrier()
                    for hh in range(2):
                        sh = SPAIR[:, hh * NTOK:(hh + 1) * NTOK]
                        b.on("vector", lambda e, sh=sh, hh=hh: e.tensor_scalar(
                            sh, sh, EX32[:, 26 + hh:27 + hh], None, A.mult))
                        b.barrier()
                        for kc in range(32):
                            b.on("tensor", lambda e, sh=sh, kc=kc: e.matmul(
                                PSB[kc // 8][:, ts(kc % 8)], sh[:, kc * 128:(kc + 1) * 128],
                                IDENT[:], is_transpose=True, start=(kc % 8 == 0), stop=(kc % 8 == 7)))
                        b.barrier()
                        for bk_ in range(4):
                            b.on("vector", lambda e, bk_=bk_: e.tensor_copy(
                                YSEL[:, bk_ * 1024:(bk_ + 1) * 1024], PSB[bk_][:]))
                        b.barrier()
                        for kc in range(32):
                            b.on("tensor", lambda e, kc=kc, hh=hh: e.matmul(
                                PS[0][:, 128 * hh:128 * hh + 128],
                                VTM[:, kc * 128:(kc + 1) * 128],
                                YSEL[:, kc * 128:(kc + 1) * 128],
                                start=(kc == 0), stop=(kc == 31)))
                        b.barrier()
                        b.on("vector", lambda e, hh=hh: e.tensor_copy(
                            OACC[64 * hh:64 * hh + 64, p * NOWN + qt * 128:p * NOWN + (qt + 1) * 128],
                            PS[0][64 * hh:64 * hh + 64, 128 * hh:128 * hh + 128]))
                        b.barrier()

            for qt in range(NQT):
                _qtile(p, qt)
    for p in range(NPAIRS):
        _pair(p)
    b.on("gpsimd", lambda e: e.dma_start(out=WO1[:], in_=wo_in[:]), dma=True)
    b.on("gpsimd", lambda e: e.dma_start(out=EW[0:1, :], in_=bo_in[:]), dma=True)
    b.barrier()
    for g in range(2):
        b.on("tensor", lambda e, g=g: e.matmul(
            PS[g][:], ONES[0:1, :], EW[0:1, ts(g, 512)], start=True, stop=True))
    b.barrier()
    for g in range(2):
        b.on("vector", lambda e, g=g: e.tensor_copy(BOUT[:, ts(g, 512)], PS[g][:]))
    b.barrier()
    for t in range(NTO):
        b.on("sync", lambda e, t=t: e.dma_start(out=XSTG[:], in_=xsp2[:, ts(t, D)]), dma=True)
        for nsl in range(2):
            for c in range(8):
                b.on("tensor", lambda e, t=t, nsl=nsl, c=c: e.matmul(
                    PS[2 + nsl][:], OACC[:, c * NOWN + t * 128:c * NOWN + (t + 1) * 128],
                    WO1[:, c * D + nsl * 512:c * D + (nsl + 1) * 512],
                    start=(c == 0), stop=(c == 7)))
        b.barrier()
        for nsl in range(2):
            b.on("vector", lambda e, nsl=nsl: e.tensor_tensor(
                EW[:, nsl * 512:(nsl + 1) * 512], PS[2 + nsl][:], BOUT[:, ts(nsl, 512)], A.add))
        b.barrier()
        b.on("vector", lambda e, t=t: e.tensor_tensor(X3[:, ts(t, D)], XSTG[:], EW[:, 0:1024], A.add))
        b.barrier()

    if stage == "l1attn":
        b.on("sync", lambda e: e.dma_start(out=out_own[:], in_=X3[:]), dma=True)
        b.barrier()
        b.build_block()
        for c in reversed(ctxs):
            c.__exit__(None, None, None)
        arena.free()
        return nc

    b.on("vector", lambda e: e.memset(MRV[:, 1:8], 1e38))
    b.on("gpsimd", lambda e: e.dma_start(out=GATEW[:], in_=gatew_in[:]), dma=True)
    b.on("gpsimd", lambda e: e.dma_start(out=BOUT[0:8, :], in_=b2mat_in[:]), dma=True)
    b.on("gpsimd", lambda e: e.dma_start(out=TROW[0:1, 0:8], in_=gbrow_in[:]), dma=True)
    b.on("gpsimd", lambda e: e.dma_start(out=NSC[:], in_=nsc_in[:]), dma=True)
    b.barrier()

    def _moe_tile(t):
        xt = X3[:, ts(t, D)]
        emit_ln_tile(xt, MEVAC[:, 0:1024])
        b.on("scalar", lambda e: e.activation(XN[:, 0:D], MEVAC[:, 0:1024], AF.Square,
                                              accum_out=SM[:, 0:1]))
        b.barrier()
        b.on("vector", lambda e: e.tensor_scalar(SM[:, 1:2], SM[:, 0:1], 1.0 / D, None, A.mult))
        b.barrier()
        b.on("scalar", lambda e: e.activation(SM[:, 2:3], SM[:, 1:2], AF.Sqrt,
                                              bias=CONSTS[:, 1:2], scale=1.0))
        b.barrier()
        b.on("vector", lambda e: e.reciprocal(SM[:, 3:4], SM[:, 2:3]))
        b.barrier()
        b.on("vector", lambda e: e.tensor_scalar(XN[:, 0:D], MEVAC[:, 0:1024], SM[:, 3:4], None, A.mult))
        b.on("vector", lambda e: e.tensor_tensor(xt, xt, MEVAC[:, 0:1024], A.add))
        b.barrier()
        for c in range(8):
            b.on("tensor", lambda e, c=c: e.matmul(
                PSB[0][:, ts(c)], XN[:, 128 * c:128 * (c + 1)],
                IDENT[:], is_transpose=True, start=(c == 0), stop=(c == 7)))
        b.barrier()
        for c in range(8):
            b.on("vector", lambda e, t=t, c=c: e.tensor_scalar(
                TT2[:, c * NOWN + t * 128:c * NOWN + (t + 1) * 128],
                PSB[0][:, ts(c)], NSC[:, c:c + 1], None, A.mult))
        b.barrier()
        for c in range(8):
            b.on("tensor", lambda e, t=t, c=c: e.matmul(
                PS[2][:, 0:8], TT2[:, c * NOWN + t * 128:c * NOWN + (t + 1) * 128],
                GATEW[:, c * 8:(c + 1) * 8], start=(c == 0), stop=False))
        b.on("tensor", lambda e: e.matmul(
            PS[2][:, 0:8], ONES[0:1, :], TROW[0:1, 0:8], start=False, stop=True))
        b.barrier()
        b.on("vector", lambda e: e.tensor_copy(EX32[:, 0:8], PS[2][:, 0:8]))
        b.barrier()
        b.on("vector", lambda e: e.max(out=EX32[:, 8:16], in_=EX32[:, 0:8]))
        b.barrier()
        b.on("vector", lambda e: e.tensor_copy(MRV[:, 0:1], EX32[:, 8:9]))
        b.barrier()
        b.on("vector", lambda e: e.match_replace(out=EX32[:, 16:24], in_to_replace=MRV[:, 0:8],
                                                 in_values=EX32[:, 0:8], imm_value=-1e30))
        b.barrier()
        b.on("vector", lambda e: e.tensor_copy(MRV[:, 0:1], EX32[:, 9:10]))
        b.barrier()
        b.on("vector", lambda e: e.match_replace(out=EX32[:, 24:32], in_to_replace=MRV[:, 0:8],
                                                 in_values=EX32[:, 16:24], imm_value=-1e30))
        b.barrier()
        b.on("vector", lambda e: e.tensor_tensor(SM[:, 8:16], EX32[:, 0:8], EX32[:, 16:24], A.not_equal))
        b.on("vector", lambda e: e.tensor_tensor(SM[:, 16:24], EX32[:, 16:24], EX32[:, 24:32], A.not_equal))
        b.barrier()
        b.on("vector", lambda e: e.tensor_tensor(SM[:, 24:25], EX32[:, 9:10], EX32[:, 8:9], A.subtract))
        b.barrier()
        b.on("scalar", lambda e: e.activation(SM[:, 25:26], SM[:, 24:25], AF.Exp))
        b.barrier()
        b.on("vector", lambda e: e.tensor_scalar(SM[:, 26:27], SM[:, 25:26], 1.0, None, A.add))
        b.barrier()
        b.on("vector", lambda e: e.reciprocal(SM[:, 27:28], SM[:, 26:27]))
        b.barrier()
        b.on("vector", lambda e: e.tensor_tensor(SM[:, 28:29], SM[:, 25:26], SM[:, 27:28], A.mult))
        b.barrier()
        b.on("vector", lambda e: e.tensor_scalar(SM[:, 8:16], SM[:, 8:16], SM[:, 27:28], None, A.mult))
        b.on("vector", lambda e: e.tensor_scalar(SM[:, 16:24], SM[:, 16:24], SM[:, 28:29], None, A.mult))
        b.barrier()
        b.on("vector", lambda e, t=t: e.tensor_tensor(
            WTOK[:, t * 8:(t + 1) * 8], SM[:, 8:16], SM[:, 16:24], A.add))
        b.barrier()


    for t in range(NTO):
        _moe_tile(t)

    b.on("sync", lambda e: e.dma_start(out=B1G[:], in_=b1g_in[:]), dma=True)
    b.on("sync", lambda e: e.dma_start(out=B1L[:], in_=b1l_in[:]), dma=True)
    b.barrier()

    def _expert(e_):
        b.on("gpsimd", lambda e, e_=e_: e.dma_start(out=W1E[:], in_=w1e_in[e_][:]), dma=True)
        b.on("gpsimd", lambda e, e_=e_: e.dma_start(out=W2E[:], in_=w2e_in[e_][:]), dma=True)
        b.barrier()
        for grp in range(8):
            t0 = 2 * grp
            b.on("vector", lambda e, t0=t0: e.tensor_copy(WTB[:], WTOK[:, t0 * 8:(t0 + 2) * 8]))
            b.barrier()
            for i2 in range(2):
                b.on("tensor", lambda e, i2=i2: e.matmul(
                    PSB[1 + i2][0:8, 0:128], WTB[:, 8 * i2:8 * i2 + 8],
                    IDENT[:], is_transpose=True, start=True, stop=True))
            b.barrier()
            for i2 in range(2):
                b.on("vector", lambda e, i2=i2: e.tensor_copy(
                    W8TG[0:8, i2 * 128:(i2 + 1) * 128], PSB[1 + i2][0:8, 0:128]))
            b.barrier()
            for hc in range(16):
                for half, bank in ((0, 0), (1, 1)):
                    for c in range(8):
                        b.on("tensor", lambda e, hc=hc, half=half, bank=bank, c=c, t0=t0: e.matmul(
                            PS[bank][:, 0:256],
                            W1E[:, c * 4096 + half * 2048 + hc * 128:c * 4096 + half * 2048 + hc * 128 + 128],
                            TT2[:, c * NOWN + t0 * 128:c * NOWN + (t0 + 2) * 128],
                            start=(c == 0), stop=(c == 7)))
                b.barrier()
                b.on("vector", lambda e, hc=hc, e_=e_: e.tensor_scalar(
                    PS[1][:, 0:256], PS[1][:, 0:256], B1L[:, e_ * 16 + hc:e_ * 16 + hc + 1], 7.0, A.add, A.min))
                b.on("vector", lambda e, hc=hc, e_=e_: e.tensor_scalar(
                    PS[0][:, 0:256], PS[0][:, 0:256], B1G[:, e_ * 16 + hc:e_ * 16 + hc + 1], 7.0, A.add, A.min))
                b.barrier()
                b.on("scalar", lambda e: e.activation(PS[2][:, 0:256], PS[0][:, 0:256], AF.Gelu_apprx_sigmoid))
                b.on("vector", lambda e: e.tensor_scalar(
                    HLF[:, 0:256], PS[1][:, 0:256], -7.0, 1.0, A.max, A.add))
                b.barrier()
                b.on("vector", lambda e, hc=hc: e.tensor_tensor(
                    SWIF[:, hc * 256:(hc + 1) * 256], HLF[:, 0:256], PS[2][:, 0:256], A.mult))
                b.barrier()
            combos = [(i, nsl) for i in range(2) for nsl in range(2)]
            for ci, (i, nsl) in enumerate(combos):
                bank = ci
                t = t0 + i
                for hc in range(16):
                    b.on("tensor", lambda e, i=i, nsl=nsl, hc=hc, bank=bank: e.matmul(
                        PS[bank][:],
                        SWIF[:, hc * 256 + i * 128:hc * 256 + i * 128 + 128],
                        W2E[:, hc * D + nsl * 512:hc * D + (nsl + 1) * 512],
                        start=(hc == 0), stop=False))
                b.on("tensor", lambda e, i=i, nsl=nsl, bank=bank: e.matmul(
                    PS[bank][:], W8TG[0:8, i * 128:(i + 1) * 128],
                    BOUT[0:8, nsl * 512:(nsl + 1) * 512],
                    start=False, stop=True))
            b.barrier()
            for ci, (i, nsl) in enumerate(combos):
                bank = ci
                t = t0 + i
                b.on("vector", lambda e, t=t, nsl=nsl, bank=bank, ci=ci, e_=e_: e.tensor_scalar(
                    MEVAC[:, ci * 512:(ci + 1) * 512],
                    PS[bank][:], WTOK[:, t * 8 + e_:t * 8 + e_ + 1], None, A.mult))
            b.barrier()
            for ci, (i, nsl) in enumerate(combos):
                t = t0 + i
                b.on("vector", lambda e, t=t, nsl=nsl, ci=ci: e.tensor_tensor(
                    X3[:, t * D + nsl * 512:t * D + (nsl + 1) * 512],
                    X3[:, t * D + nsl * 512:t * D + (nsl + 1) * 512],
                    MEVAC[:, ci * 512:(ci + 1) * 512], A.add))
            b.barrier()

    for e_ in range(8):
        _expert(e_)

    b.on("sync", lambda e: e.dma_start(out=out_own[:], in_=X3[:]), dma=True)
    b.barrier()

    b.build_block()
    for c in reversed(ctxs):
        c.__exit__(None, None, None)
    arena.free()
    return nc


def chunk_major(w):
    w = np.asarray(w, dtype=np.float32)
    c = w.shape[0] // 128
    return np.ascontiguousarray(w.reshape(c, 128, w.shape[1]).transpose(1, 0, 2).reshape(128, -1))


def tok_major(x):
    x = np.asarray(x, dtype=np.float32)
    t = x.shape[0] // 128
    return np.ascontiguousarray(x.reshape(t, 128, x.shape[1]).transpose(1, 0, 2).reshape(128, -1))


def feat_cols(bvec):
    b_ = np.asarray(bvec, dtype=np.float32).reshape(-1)
    c = b_.shape[0] // 128
    return np.ascontiguousarray(b_.reshape(c, 128).T)


def stage_inputs(ins, core=0):
    bidx, h = core // 2, core % 2
    x = np.asarray(ins["L0_attn_x"])[bidx]
    own = x[h * 2048:(h + 1) * 2048]
    part = x[(1 - h) * 2048:(2 - h) * 2048]
    m = {
        "x_own": tok_major(own),
        "x_part": tok_major(part),
        "wqkvT": chunk_major(np.asarray(ins["L0_attn_w_qkv"]).T),
        "woutT": chunk_major(np.asarray(ins["L0_attn_w_out"]).T),
        "bout": np.asarray(ins["L0_attn_b_out"], dtype=np.float32).reshape(1, -1),
        "ar": np.asarray(ins["L0_ffn_adaptive_range"], dtype=np.float32).reshape(1, 1),
    }
    for i in range(1, 5):
        w = np.asarray(ins[f"L0_ffn_cn_w{i}"], dtype=np.float32)
        if i == 4:
            w = np.pad(w, ((0, 0), (0, 127)))
        m[f"cn_w{i}"] = chunk_major(w)
        bv = np.asarray(ins[f"L0_ffn_cn_b{i}"], dtype=np.float32)
        m[f"cn_b{i}"] = feat_cols(bv) if i < 4 else np.tile(bv.reshape(1, 1), (128, 1))
    for i in range(1, 4):
        w = np.asarray(ins[f"L0_ffn_gc_w{i}"], dtype=np.float32)
        if i == 3:
            w = np.pad(w, ((0, 0), (0, 127)))
        m[f"gc_w{i}"] = chunk_major(w)
        bv = np.asarray(ins[f"L0_ffn_gc_b{i}"], dtype=np.float32)
        m[f"gc_b{i}"] = feat_cols(bv) if i < 3 else np.tile(bv.reshape(1, 1), (128, 1))
    m["wq"] = chunk_major(np.asarray(ins["L1_attn_Wq"]))
    m["wk"] = chunk_major(np.asarray(ins["L1_attn_Wk"]))
    m["wv"] = chunk_major(np.asarray(ins["L1_attn_Wv"]))
    m["wo"] = chunk_major(np.asarray(ins["L1_attn_Wo"]))
    m["bq"] = feat_cols(ins["L1_attn_bq"])
    m["bk"] = feat_cols(ins["L1_attn_bk"])
    m["bvrow"] = np.asarray(ins["L1_attn_bv"], dtype=np.float32).reshape(1, -1)
    m["borow"] = np.asarray(ins["L1_attn_bo"], dtype=np.float32).reshape(1, -1)
    for e in range(8):
        w1 = np.asarray(ins["L1_ffn_mlp1_w"], dtype=np.float32)[e]
        w1d = np.concatenate([w1[:, 0::2], w1[:, 1::2]], axis=1)
        m[f"w1e{e}"] = chunk_major(w1d)
        m[f"w2e{e}"] = chunk_major(np.asarray(ins["L1_ffn_mlp2_w"], dtype=np.float32)[e])
    b1 = np.asarray(ins["L1_ffn_mlp1_b"], dtype=np.float32)
    cols = []
    for e in range(8):
        g = b1[e, 0::2].reshape(16, 128).T
        l = b1[e, 1::2].reshape(16, 128).T
        cols.append(np.concatenate([g, l], axis=1))
    m["b1g"] = np.ascontiguousarray(np.concatenate([c[:, 0:16] for c in cols], axis=1))
    m["b1l"] = np.ascontiguousarray(np.concatenate([c[:, 16:32] for c in cols], axis=1))
    m["b2mat"] = np.ascontiguousarray(np.asarray(ins["L1_ffn_mlp2_b"], dtype=np.float32))
    gw = np.asarray(ins["L1_ffn_gate_w"], dtype=np.float32)
    m["gatew"] = np.ascontiguousarray(gw.reshape(8, 128, 8).transpose(1, 0, 2).reshape(128, 64))
    m["gbrow"] = np.asarray(ins["L1_ffn_gate_b"], dtype=np.float32).reshape(1, -1)
    m["nsc"] = feat_cols(ins["L1_ffn_norm_scale"])
    return m

def kernel(**inputs):
    nc = build_program("full")
    in_maps = [stage_inputs(inputs, core=c) for c in range(8)]
    r = run_bass_kernel_spmd(nc, in_maps, core_ids=list(range(8)))
    out = np.empty((4, 4096, 1024), dtype=np.float32)
    for c in range(8):
        got = np.asarray(r.results[c]["out_own"], dtype=np.float32)
        got = got.reshape(128, 16, 1024).transpose(1, 0, 2).reshape(2048, 1024)
        out[c // 2, (c % 2) * 2048:(c % 2 + 1) * 2048] = got
    return out


# version 03
# speedup vs baseline: 1.4109x; 1.4109x over previous
import time
import numpy as np
import concourse.bass as bass
import concourse.mybir as mybir
from concourse.bass import AutoArena
from concourse.bass_utils import run_bass_kernel_spmd

F32 = mybir.dt.float32
BF16 = mybir.dt.bfloat16
U8 = mybir.dt.uint8
A = mybir.AluOpType
AF = mybir.ActivationFunctionType
AX = mybir.AxisListType

D = 1024
NTOK = 4096
NT = 32
NG = 8
H = 16
HD = 64
NOWN = 2048
NTO = 16
KB = 1024


def ts(i, n=128):
    return slice(i * n, (i + 1) * n)


class B:
    ENGINES = ("sync", "tensor", "vector", "scalar", "gpsimd")

    def __init__(self, nc):
        self.nc = nc
        self.ops = {e: [] for e in self.ENGINES}
        self.dirty = {e: False for e in self.ENGINES}
        self.used = {e: False for e in self.ENGINES}
        self.ph_cnt = 0
        self.phd_cnt = {"sync": 0, "gpsimd": 0, "scalar": 0, "vector": 0}

    def on(self, eng, fn, dma=False):
        if dma:
            self.ops[eng].append(["dma", fn, 16])
            self.phd_cnt[eng] += 16
        else:
            self.ops[eng].append(["op", fn, 1])
            self.dirty[eng] = True
        self.used[eng] = True

    def barrier(self):
        for eng in self.ENGINES:
            if self.dirty[eng]:
                for i in range(len(self.ops[eng]) - 1, -1, -1):
                    if self.ops[eng][i][0] == "op":
                        self.ops[eng][i][0] = "op_inc"
                        self.ph_cnt += 1
                        break
                self.dirty[eng] = False
        v = (self.ph_cnt, self.phd_cnt["sync"], self.phd_cnt["gpsimd"])
        for eng in self.ENGINES:
            self.ops[eng].append(["barrier", v, 0])

    def build_block(self):
        nc = self.nc
        c1 = nc.semaphore("ph"); ph = c1.__enter__()
        c2 = nc.semaphore("phds"); phds = c2.__enter__()
        c3 = nc.semaphore("phdg"); phdg = c3.__enter__()
        with nc.Block() as block:
            for eng in self.ENGINES:
                if not self.used[eng]:
                    continue
                lst = self.ops[eng]

                def make(lst, eng=None):
                    keep = [False] * len(lst)
                    nxt = False
                    for i in range(len(lst) - 1, -1, -1):
                        if lst[i][0] == "barrier":
                            keep[i] = nxt
                            nxt = False
                        else:
                            nxt = True
                    for i in range(len(lst) - 1, -1, -1):
                        if lst[i][0] == "barrier":
                            keep[i] = True
                            break

                    def body(e):
                        last_wait = (-1, -1, -1)
                        for idx, it in enumerate(lst):
                            kind, fn = it[0], it[1]
                            if kind == "barrier":
                                if not keep[idx]:
                                    continue
                                if fn[0] > last_wait[0]:
                                    e.wait_ge(ph, fn[0])
                                if fn[1] > last_wait[1]:
                                    e.wait_ge(phds, fn[1])
                                if fn[2] > last_wait[2]:
                                    e.wait_ge(phdg, fn[2])
                                last_wait = fn
                            elif kind == "op":
                                fn(e)
                            elif kind == "op_inc":
                                fn(e).then_inc(ph, 1)
                            elif kind == "dma":
                                fn(e).then_inc(phds if eng == "sync" else phdg, 16)
                    return body
                getattr(block, eng)(make(lst, eng))
        c3.__exit__(None, None, None)
        c2.__exit__(None, None, None)
        c1.__exit__(None, None, None)


def build_program(stage):
    NGL = NG
    PASS2 = True
    nc = bass.Bass("TRN2", target_bir_lowering=False)

    def din(name, shape, dtype=F32):
        return nc.dram_tensor(name, list(shape), dtype, kind="ExternalInput").ap()

    def dout(name, shape, dtype=F32):
        return nc.dram_tensor(name, list(shape), dtype, kind="ExternalOutput").ap()

    x_own = din("x_own", [128, NTO * D])
    x_part = din("x_part", [128, NTO * D])
    wqkvT = din("wqkvT", [128, 8 * 3 * D])
    woutT = din("woutT", [128, 8 * D])
    bout_in = din("bout", [1, D])
    cn_w = [din(f"cn_w{i}", s) for i, s in enumerate(
        [(128, 8 * D), (128, 8 * 512), (128, 4 * 256), (128, 2 * 128)], 1)]
    cn_b = [din(f"cn_b{i}", (128, s)) for i, s in enumerate([8, 4, 2, 1], 1)]
    gc_w = [din(f"gc_w{i}", s) for i, s in enumerate(
        [(128, 8 * 512), (128, 4 * 256), (128, 2 * 128)], 1)]
    gc_b = [din(f"gc_b{i}", (128, s)) for i, s in enumerate([4, 2, 1], 1)]
    ar_in = din("ar", [1, 1])
    wq_in = din("wq", [128, 8 * D])
    wk_in = din("wk", [128, 8 * D])
    wv_in = din("wv", [128, 8 * D])
    wo_in = din("wo", [128, 8 * D])
    bq_in = din("bq", [128, 8])
    bk_in = din("bk", [128, 8])
    bv_in = din("bvrow", [1, D])
    bo_in = din("borow", [1, D])
    w1e_in = [din(f"w1e{e}", [128, 8 * 4096]) for e in range(8)]
    w2e_in = [din(f"w2e{e}", [128, 16 * D]) for e in range(8)]
    b1g_in = din("b1g", [128, 128])
    b1l_in = din("b1l", [128, 128])
    b2mat_in = din("b2mat", [8, D])
    gatew_in = din("gatew", [128, 64])
    gbrow_in = din("gbrow", [1, 8])
    nsc_in = din("nsc", [128, 8])
    out_own = dout("out_own", [128, NTO * D])
    xsp1 = nc.dram_tensor("xsp1", [128, NTO * D], F32, kind="Internal").ap()
    xsp2 = nc.dram_tensor("xsp2", [128, NTO * D], F32, kind="Internal").ap()

    arena = AutoArena(nc)
    ARENA_SZ = 212480
    arena.size = ARENA_SZ
    arena.allocate()
    placed = {}

    def place(name, shape, dtype, off):
        nbytes = int(np.prod(shape[1:])) * mybir.dt.size(dtype)
        assert off + nbytes <= ARENA_SZ, (name, off, nbytes)
        hnd = nc.virtual_sbuf_tensor(name, list(shape), dtype)
        nc._tensor(hnd.name, hnd.shape, hnd.dtype, type="SB")
        mloc = nc.lookup_mloc(hnd)
        mloc.allocated = True
        mloc.addr = arena.offset + off
        placed[name] = (off, nbytes)
        return hnd.ap()

    R0, R1, R2, R3, R4, ST = 0, 65536, 98304, 163840, 188416, 206848
    X = place("X", [128, NTO * D], F32, R0)
    YS = place("YS", [128, NT * D], BF16, R0)
    XP = place("XP", [128, NTO * D], BF16, R1)
    WQKV = place("WQKV", [128, 8 * 3 * D], BF16, R2)
    WOUT = place("WOUT", [128, 8 * D], BF16, R2 + 49152)
    WCN1 = place("WCN1", [128, 8 * D], BF16, R2)
    WCN2 = place("WCN2", [128, 8 * 512], BF16, R2 + 16384)
    WCN3 = place("WCN3", [128, 4 * 256], BF16, R2 + 24576)
    WCN4 = place("WCN4", [128, 2 * 128], BF16, R2 + 26624)
    WGC1 = place("WGC1", [128, 8 * 512], BF16, R2 + 27136)
    WGC2 = place("WGC2", [128, 4 * 256], BF16, R2 + 35328)
    WGC3 = place("WGC3", [128, 2 * 128], BF16, R2 + 37376)
    KT = place("KT", [128, 4 * D], BF16, R3)
    VT = place("VT", [128, 4 * D], BF16, R3 + 8192)
    QT = place("QT", [128, 8 * 512], BF16, R3)
    AOT = place("AOT", [128, 8 * 512], BF16, R3 + 8192)
    OTM = place("OTM", [128, 4 * D], BF16, R3 + 16384)
    H1T = place("H1T", [128, 8 * 512], BF16, R3)
    H2T = place("H2T", [128, 4 * 512], BF16, R3 + 8192)
    H3T = place("H3T", [128, 2 * 512], BF16, R3 + 12288)
    YTT = place("YTT", [128, 8 * 512], BF16, R3 + 16384)
    XN = place("XN", [128, 4 * D], BF16, R4)
    XSTG = place("XSTG", [128, D], F32, R3)
    XSTG2 = place("XSTG2", [128, D], F32, R3 + 4096)
    TROW = place("TROW", [128, 512], BF16, R3 + 12288 + 2048)
    XT = place("XT", [128, 8 * 512], BF16, R4 + 8192)
    EW = place("EW", [128, 1024], BF16, R4 + 16384)
    IDENT = place("IDENT", [128, 128], BF16, ST)
    ONES = place("ONES", [128, 128], BF16, ST + 256)
    BOUT = place("BOUT", [128, D], BF16, ST + 512)
    SM = place("SM", [128, 64], F32, ST + 2560)
    ZT = place("ZT", [128, 64], F32, ST + 2816)
    CNB = place("CNB", [128, 32], F32, ST + 3072)
    SC2 = place("SC2", [128, 56], F32, ST + 3200)
    KVS = place("KVS", [128, 1024], BF16, ST + 3456)
    XT3 = place("XT3", [128, NT * D], BF16, R0)
    X3 = place("X3", [128, NTO * D], F32, R0)
    OACC = place("OACC", [128, 8 * NOWN], BF16, R1)
    WQ1 = place("WQ1", [128, 8 * D], BF16, R2)
    WK1 = place("WK1", [128, 8 * D], BF16, R2 + 16384)
    WV1 = place("WV1", [128, 8 * D], BF16, R2 + 32768)
    WO1 = place("WO1", [128, 8 * D], BF16, R2)
    SPAIR = place("SPAIR", [128, 2 * NTOK], BF16, R2 + 49152)
    KTP = place("KTP", [128, NTOK], BF16, R3)
    KTM = place("KTM", [128, NT * 128], BF16, R3 + 8192)
    VTM = place("VTM", [128, NT * 128], BF16, R3 + 16384)
    QTP = place("QTP", [128, NOWN], BF16, R4)
    YSEL = place("YSEL", [128, NTOK], BF16, R4 + 8192)
    SST = place("SST", [128, 64], F32, ST + 2816)
    DIAG = place("DIAG", [128, 128], BF16, ST + 3456)
    KKZ = place("KKZ", [128, 128], BF16, ST + 3712)
    QTMS = place("QTMS", [128, 128], BF16, ST + 3968)
    IOTA16 = place("IOTA16", [128, 16], F32, ST + 4224)
    KSZ = place("KSZ", [128, 2], BF16, ST + 4288)
    M16 = place("M16", [128, 16], BF16, ST + 4320)
    EX32 = place("EX32", [128, 32], F32, ST + 4352)
    BQ1 = place("BQ1", [128, 8], F32, ST + 4480)
    BK1 = place("BK1", [128, 8], F32, ST + 4512)
    TT2 = place("TT2", [128, 8 * NOWN], BF16, R1)
    W1E = place("W1E", [128, 8 * 4096], BF16, R2)
    W2E = place("W2E", [128, 16 * D], BF16, R3)
    SWIF = place("SWIF", [128, 16 * 256], BF16, R4 + 8192)
    GATEW = place("GATEW", [128, 64], BF16, ST + 3456)
    B1G = place("B1G", [128, 128], F32, ST + 2560)
    B1L = place("B1L", [128, 128], F32, ST + 5056)
    MEVAC = place("MEVAC", [128, 2048], F32, R4 + 8192)
    HLF = place("HLF", [128, 512], F32, R4 + 16384)
    WTOK = place("WTOK", [128, 128], F32, ST + 3712)
    NSC = place("NSC", [128, 8], F32, ST + 4224)
    W8TG = place("W8TG", [8, 256], BF16, ST + 4544)
    WTB = place("WTB", [128, 16], BF16, ST + 4288)
    MSK = place("MSK", [128, 12], U8, ST + 4544)
    MRV = place("MRV", [128, 8], F32, ST + 4480)
    KSUM = place("KSUM", [128, 8], BF16, ST + 5504)
    KSUMF = place("KSUMF", [128, 8], F32, ST + 5536)
    KSUMZ = place("KSUMZ", [128, 16], BF16, ST + 5568)
    ROWB = place("ROWB", [1, 1024], BF16, R4 + 16384)
    CONSTS = place("CONSTS", [128, 8], F32, ST + 3424)

    ctxs = []

    def psum_t(name, shape, dtype=F32):
        cm = nc.psum_tensor(name, list(shape), dtype)
        t = cm.__enter__()
        ctxs.append(cm)
        return t

    PS = [psum_t(f"PS{i}", (128, 512)) for i in range(4)]
    PSB = [psum_t(f"PSB{i}", (128, 1024), BF16) for i in range(4)]

    b = B(nc)

    for t in range(NTO):
        b.on("sync", lambda e, t=t: e.dma_start(out=X[:, ts(t, D)], in_=x_own[:, ts(t, D)]), dma=True)
        b.on("gpsimd", lambda e, t=t: e.dma_start(out=XP[:, ts(t, D)], in_=x_part[:, ts(t, D)]), dma=True)
    b.on("gpsimd", lambda e: e.dma_start(out=WQKV[:], in_=wqkvT[:]), dma=True)
    b.on("gpsimd", lambda e: e.dma_start(out=WOUT[:], in_=woutT[:]), dma=True)
    b.on("gpsimd", lambda e: e.dma_start(out=CNB[:, 0:8], in_=cn_b[0][:]), dma=True)
    b.on("gpsimd", lambda e: e.dma_start(out=CNB[:, 8:12], in_=cn_b[1][:]), dma=True)
    b.on("gpsimd", lambda e: e.dma_start(out=CNB[:, 12:14], in_=cn_b[2][:]), dma=True)
    b.on("gpsimd", lambda e: e.dma_start(out=CNB[:, 16:20], in_=gc_b[0][:]), dma=True)
    b.on("gpsimd", lambda e: e.dma_start(out=CNB[:, 20:22], in_=gc_b[1][:]), dma=True)
    b.on("gpsimd", lambda e: e.dma_start(out=SC2[0:1, 0:1], in_=cn_b[3][0:1, :]), dma=True)
    b.on("gpsimd", lambda e: e.dma_start(out=SC2[0:1, 1:2], in_=gc_b[2][0:1, :]), dma=True)
    b.on("gpsimd", lambda e: e.dma_start(out=SC2[0:1, 2:3], in_=ar_in[:]), dma=True)
    b.on("gpsimd", lambda e: e.dma_start(out=ROWB[:], in_=bout_in[:]), dma=True)
    b.barrier()
    b.on("gpsimd", lambda e: e.memset(ONES[:], 1.0))
    b.on("vector", lambda e: e.memset(CONSTS[:, 0:1], 0.0))
    b.on("vector", lambda e: e.memset(CONSTS[:, 1:2], 1e-5))
    b.on("vector", lambda e: e.memset(CONSTS[:, 2:3], 1e-6))
    b.on("vector", lambda e: e.memset(CONSTS[:, 3:4], 1.0))
    b.on("vector", lambda e: e.memset(KSUMF[:], 0.0))
    b.barrier()
    b.on("gpsimd", lambda e: e.affine_select(
        IDENT[:], ONES[:], [[1, 128]], A.is_equal, 0.0, base=0, channel_multiplier=-1))
    for g in range(2):
        b.on("tensor", lambda e, g=g: e.matmul(
            PS[g][:], ONES[0:1, :], ROWB[0:1, ts(g, 512)], start=True, stop=True))
    b.barrier()
    for g in range(2):
        b.on("vector", lambda e, g=g: e.tensor_copy(BOUT[:, ts(g, 512)], PS[g][:]))
    b.barrier()

    def x_tile(t):
        return X[:, ts(t, D)] if t < NTO else XP[:, ts(t - NTO, D)]

    def emit_ln_group(g, dst, src_tile=None):
        if src_tile is None:
            src_tile = x_tile
        for i in range(4):
            t = 4 * g + i
            b.on("vector", lambda e, i=i, t=t: e.bn_stats(SM[:, 16 * i:16 * i + 6], src_tile(t)[:, 0:512]))
            b.on("vector", lambda e, i=i, t=t: e.bn_stats(SM[:, 16 * i + 6:16 * i + 12], src_tile(t)[:, 512:1024]))
        b.barrier()
        for i in range(4):
            b.on("vector", lambda e, i=i: e.bn_aggr(SM[:, 16 * i + 12:16 * i + 14], SM[:, 16 * i:16 * i + 12]))
        b.barrier()
        varap = SM[:, 0:64].rearrange("p (i k) -> p i k", k=16)[:, :, 13:14]
        b.on("scalar", lambda e: e.activation(
            SC2[:, 32:36].rearrange("p (i k) -> p i k", k=1), varap, AF.Sqrt, bias=CONSTS[:, 1:2], scale=1.0))
        b.barrier()
        b.on("vector", lambda e: e.reciprocal(SC2[:, 36:40], SC2[:, 32:36]))
        b.barrier()
        for i in range(4):
            t = 4 * g + i
            b.on("vector", lambda e, i=i, t=t: e.tensor_scalar(
                dst[:, ts(i, D)], src_tile(t), SM[:, 16 * i + 12:16 * i + 13], SC2[:, 36 + i:37 + i],
                A.subtract, A.mult))
        b.barrier()

    def emit_transpose_group(src, dst):
        for i in range(4):
            for c in range(8):
                b.on("tensor", lambda e, i=i, c=c: e.matmul(
                    PSB[i][:, ts(c)], src[:, i * D + 128 * c:i * D + 128 * (c + 1)],
                    IDENT[:], is_transpose=True, start=(c == 0), stop=(c == 7)))
        b.barrier()
        for i in range(4):
            dst_ap = dst.rearrange("p (c w) -> p c w", c=8)[:, :, i * 128:(i + 1) * 128]
            b.on("vector", lambda e, i=i, dst_ap=dst_ap: e.tensor_copy(
                dst_ap, PSB[i][:].rearrange("p (s w) -> p s w", s=8)))
        b.barrier()

    def emit_elu1_evac(ps_ap, dst_ap, n):
        b.on("vector", lambda e: e.tensor_scalar(EW[:, 0:n], ps_ap, 0.0, None, A.min))
        b.on("vector", lambda e: e.tensor_scalar(EW[:, 512:512 + n], ps_ap, 0.0, None, A.max))
        b.barrier()
        b.on("scalar", lambda e: e.activation(XN[:, 0:n], EW[:, 0:n], AF.Exp))
        b.barrier()
        b.on("vector", lambda e: e.tensor_tensor(dst_ap, XN[:, 0:n], EW[:, 512:512 + n], A.add))
        b.barrier()

    ps_first = {2: True, 3: True}
    for g in range(NGL):
        emit_ln_group(g, XN)
        emit_transpose_group(XN, XT)
        for i in range(4):
            for nsl in range(2):
                for half, base in enumerate([D, 2 * D]):
                    for c in range(8):
                        b.on("tensor", lambda e, i=i, c=c, base=base, half=half, nsl=nsl: e.matmul(
                            PS[half][:],
                            XT[:, c * 512 + i * 128:c * 512 + (i + 1) * 128],
                            WQKV[:, c * 3 * D + base + nsl * 512:c * 3 * D + base + (nsl + 1) * 512],
                            start=(c == 0), stop=(c == 7)))
                b.barrier()
                emit_elu1_evac(PS[0][:], KT[:, i * D + nsl * 512:i * D + (nsl + 1) * 512], 512)
                b.on("vector", lambda e, nsl=nsl, i=i: e.tensor_copy(
                    VT[:, i * D + nsl * 512:i * D + (nsl + 1) * 512], PS[1][:]))
                b.barrier()
        for i in range(4):
            first = (g == 0 and i == 0)
            last = (g == NGL - 1 and i == 3)
            for h in range(H):
                bank = 2 + h // 8
                col0 = 64 * (h % 8)
                hp = h & ~1
                b.on("tensor", lambda e, h=h, hp=hp, i=i, bank=bank, col0=col0,
                     st=(first and (h % 8) == 0), sp=(last and (h % 8) == 7): e.matmul(
                    PS[bank][:, col0:col0 + 64],
                    KT[:, i * D + 64 * hp:i * D + 64 * (hp + 2)],
                    VT[:, i * D + 64 * h:i * D + 64 * (h + 1)],
                    start=st, stop=sp))
            for c in range(8):
                b.on("tensor", lambda e, i=i, c=c: e.matmul(
                    PS[0][:, 256 + c:257 + c],
                    KT[:, i * D + 128 * c:i * D + 128 * (c + 1)],
                    ONES[:, 0:1],
                    start=(i == 0 and c == 0), stop=(i == 3 and c == 7)))
        b.barrier()
        b.on("vector", lambda e: e.tensor_tensor(KSUMF[:], KSUMF[:], PS[0][:, 256:264], A.add))
        b.barrier()
    b.on("vector", lambda e: e.memset(KVS[:], 0.0))
    b.on("vector", lambda e: e.memset(KSUMZ[:], 0.0))
    b.barrier()
    for h in range(H):
        bank, col0 = 2 + h // 8, 64 * (h % 8)
        p0 = 64 * (h % 2)
        base = 512 * (h // 8) + 64 * (h % 8)
        b.on("vector", lambda e, bank=bank, col0=col0, p0=p0, base=base: e.tensor_copy(
            KVS[p0:p0 + 64, base:base + 64], PS[bank][p0:p0 + 64, col0:col0 + 64]))
        b.on("vector", lambda e, h=h, p0=p0: e.tensor_copy(
            KSUMZ[p0:p0 + 64, h:h + 1], KSUMF[p0:p0 + 64, h // 2:h // 2 + 1]))
    b.barrier()

    def kvs_ap(h):
        base = 512 * (h // 8) + 64 * (h % 8)
        return KVS[:, base:base + 64]

    def ksum_ap(h):
        return KSUMZ[:, h:h + 1]

    P2SEC = 99
    for g in range(NGL if PASS2 else 0):
        emit_ln_group(g, XN)
        emit_transpose_group(XN, XT)
        if P2SEC < 1:
            continue
        for ckg in range(4):
            for ck in range(2 * ckg, 2 * ckg + 2):
                bank = ck % 2
                for c in range(8):
                    b.on("tensor", lambda e, ck=ck, c=c, bank=bank: e.matmul(
                        PS[bank][:],
                        WQKV[:, c * 3 * D + ck * 128:c * 3 * D + (ck + 1) * 128],
                        XT[:, c * 512:(c + 1) * 512],
                        start=(c == 0), stop=(c == 7)))
            b.barrier()
            for ck in range(2 * ckg, 2 * ckg + 2):
                emit_elu1_evac(PS[ck % 2][:], QT[:, ck * 512:(ck + 1) * 512], 512)
            b.barrier()
        if P2SEC < 2:
            continue
        for i in range(4):
            for h in range(H):
                ck = h // 2
                b.on("tensor", lambda e, i=i, h=h, ck=ck: e.matmul(
                    PS[2][:, i * 16 + h:i * 16 + h + 1],
                    QT[:, ck * 512 + i * 128:ck * 512 + (i + 1) * 128],
                    ksum_ap(h), start=(i == 0 and h == 0), stop=(i == 3 and h == H - 1)))
        b.barrier()
        b.on("vector", lambda e: e.tensor_scalar(ZT[:, 0:64], PS[2][:, 0:64], 1e-6, None, A.add))
        b.barrier()
        b.on("vector", lambda e: e.reciprocal(ZT[:, 0:64], ZT[:, 0:64]))
        b.barrier()
        if P2SEC < 3:
            continue
        for i in range(4):
            for h in range(H):
                ck = h // 2
                bank, col0 = h // 8, 64 * (h % 8)
                b.on("tensor", lambda e, i=i, h=h, ck=ck, bank=bank, col0=col0: e.matmul(
                    PS[bank][:, col0:col0 + 64],
                    QT[:, ck * 512 + i * 128:ck * 512 + (i + 1) * 128],
                    kvs_ap(h), start=(h % 8 == 0), stop=(h % 8 == 7)))
            b.barrier()
            for h in range(H):
                bank, col0 = h // 8, 64 * (h % 8)
                b.on("vector", lambda e, i=i, h=h, bank=bank, col0=col0: e.tensor_scalar(
                    OTM[:, i * D + 64 * h:i * D + 64 * (h + 1)],
                    PS[bank][:, col0:col0 + 64],
                    ZT[:, i * 16 + h:i * 16 + h + 1], None, A.mult))
            b.barrier()
        if P2SEC < 4:
            continue
        emit_transpose_group(OTM, AOT)
        if P2SEC < 5:
            continue
        for i in range(4):
            t = 4 * g + i
            for nsl in range(2):
                for c in range(8):
                    b.on("tensor", lambda e, i=i, nsl=nsl, c=c: e.matmul(
                        PS[nsl][:],
                        AOT[:, c * 512 + i * 128:c * 512 + (i + 1) * 128],
                        WOUT[:, c * D + nsl * 512:c * D + (nsl + 1) * 512],
                        start=(c == 0), stop=(c == 7)))
            b.barrier()
            for nsl in range(2):
                b.on("vector", lambda e, nsl=nsl: e.tensor_tensor(
                    EW[:, nsl * 512:(nsl + 1) * 512], PS[nsl][:], BOUT[:, ts(nsl, 512)], A.add))
            b.barrier()
            dst = x_tile(t)
            b.on("vector", lambda e, dst=dst: e.tensor_tensor(dst, dst, EW[:, 0:1024], A.add))
            b.barrier()

    if stage == "l0attn":
        b.on("sync", lambda e: e.dma_start(out=out_own[:], in_=X[:]), dma=True)
        b.barrier()
        b.build_block()
        for c in reversed(ctxs):
            c.__exit__(None, None, None)
        arena.free()
        return nc

    for w, t in [(cn_w[0], WCN1), (cn_w[1], WCN2), (cn_w[2], WCN3), (cn_w[3], WCN4),
                 (gc_w[0], WGC1), (gc_w[1], WGC2), (gc_w[2], WGC3)]:
        b.on("gpsimd", lambda e, w=w, t=t: e.dma_start(out=t[:], in_=w[:]), dma=True)
    for t in range(NTO):
        b.on("sync", lambda e, t=t: e.dma_start(out=xsp1[:, ts(t, D)], in_=X[:, ts(t, D)]), dma=True)
    b.barrier()
    b.on("vector", lambda e: e.tensor_copy(EW[0:1, 0:3], SC2[0:1, 0:3]))
    b.barrier()
    b.on("tensor", lambda e: e.matmul(PS[2][:, 0:3], ONES[0:1, :], EW[0:1, 0:3], start=True, stop=True))
    b.barrier()
    b.on("vector", lambda e: e.tensor_copy(SC2[:, 4:7], PS[2][:, 0:3]))
    b.barrier()
    b.on("vector", lambda e: e.tensor_scalar(SC2[:, 52:53], SC2[:, 6:7], 2.0, None, A.mult))
    b.barrier()
    def ys_tile(t):
        return YS[:, t * D:(t + 1) * D]
    for g in range(NG):
        emit_ln_group(g, XN)
        for i in range(4):
            t = 4 * g + i
            b.on("vector", lambda e, i=i, t=t: e.tensor_copy(ys_tile(t), XN[:, ts(i, D)]))
        b.barrier()
    for t in range(NT):
        for c in range(8):
            b.on("tensor", lambda e, t=t, c=c: e.matmul(
                PS[2][:, 8 + c:9 + c], YS[:, t * D + 128 * c:t * D + 128 * (c + 1)],
                ONES[:, 0:1], start=(t == 0 and c == 0), stop=(t == NT - 1 and c == 7)))
    b.barrier()
    b.on("vector", lambda e: e.tensor_scalar(SC2[:, 8:16], PS[2][:, 8:16], 1.0 / NTOK, None, A.mult))
    b.barrier()
    GCB = EW
    b.on("vector", lambda e: e.tensor_copy(EW[:, 16:24], SC2[:, 8:16]))
    b.barrier()
    for co in range(4):
        for ci in range(8):
            b.on("tensor", lambda e, co=co, ci=ci: e.matmul(
                PS[2][:, 16 + co:17 + co], WGC1[:, ci * 512 + co * 128:ci * 512 + co * 128 + 128],
                EW[:, 16 + ci:17 + ci], start=(ci == 0), stop=(ci == 7)))
    b.barrier()
    for co in range(4):
        b.on("scalar", lambda e, co=co: e.activation(
            EW[:, 24 + co:25 + co], PS[2][:, 16 + co:17 + co], AF.Gelu,
            bias=CNB[:, 16 + co:17 + co], scale=1.0))
    b.barrier()
    for co in range(2):
        for ci in range(4):
            b.on("tensor", lambda e, co=co, ci=ci: e.matmul(
                PS[2][:, 20 + co:21 + co], WGC2[:, ci * 256 + co * 128:ci * 256 + co * 128 + 128],
                EW[:, 24 + ci:25 + ci], start=(ci == 0), stop=(ci == 3)))
    b.barrier()
    for co in range(2):
        b.on("scalar", lambda e, co=co: e.activation(
            EW[:, 28 + co:29 + co], PS[2][:, 20 + co:21 + co], AF.Gelu,
            bias=CNB[:, 20 + co:21 + co], scale=1.0))
    b.barrier()
    for ci in range(2):
        b.on("tensor", lambda e, ci=ci: e.matmul(
            PS[2][:, 22:23], WGC3[:, ci * 128:ci * 128 + 128],
            EW[:, 28 + ci:29 + ci], start=(ci == 0), stop=(ci == 1)))
    b.barrier()
    b.on("scalar", lambda e: e.activation(SC2[:, 33:34], PS[2][:, 22:23], AF.Sigmoid,
                                          bias=SC2[:, 5:6], scale=1.0))
    b.barrier()
    b.on("vector", lambda e: e.tensor_copy(EW[0:1, 30:31], SC2[0:1, 33:34]))
    b.barrier()
    b.on("tensor", lambda e: e.matmul(PS[2][:, 23:24], ONES[0:1, :], EW[0:1, 30:31], start=True, stop=True))
    b.barrier()
    b.on("vector", lambda e: e.tensor_scalar(SC2[:, 35:36], PS[2][:, 23:24], 0.7, None, A.mult))
    b.barrier()

    NCYC = 3
    for cyc in range(NCYC):
        for g in range(NG):
            def y_src(t):
                return ys_tile(t)
            for i in range(4):
                for c in range(8):
                    b.on("tensor", lambda e, g=g, i=i, c=c: e.matmul(
                        PSB[i][:, ts(c)], ys_tile(4 * g + i)[:, 128 * c:128 * (c + 1)],
                        IDENT[:], is_transpose=True, start=(c == 0), stop=(c == 7)))
            b.barrier()
            for i in range(4):
                dst_ap = YTT.rearrange("p (c w) -> p c w", c=8)[:, :, i * 128:(i + 1) * 128]
                b.on("vector", lambda e, i=i, dst_ap=dst_ap: e.tensor_copy(
                    dst_ap, PSB[i][:].rearrange("p (s w) -> p s w", s=8)))
            b.barrier()
            for cg in range(4):
                for co in (2 * cg, 2 * cg + 1):
                    for ci in range(8):
                        b.on("tensor", lambda e, co=co, ci=ci: e.matmul(
                            PS[co % 2][:], WCN1[:, ci * D + co * 128:ci * D + (co + 1) * 128],
                            YTT[:, ci * 512:(ci + 1) * 512], start=(ci == 0), stop=(ci == 7)))
                b.barrier()
                for co in (2 * cg, 2 * cg + 1):
                    b.on("scalar", lambda e, co=co: e.activation(
                        H1T[:, co * 512:(co + 1) * 512], PS[co % 2][:], AF.Gelu,
                        bias=CNB[:, co:co + 1], scale=1.0))
                b.barrier()
            for cg in range(2):
                for co in (2 * cg, 2 * cg + 1):
                    for ci in range(8):
                        b.on("tensor", lambda e, co=co, ci=ci: e.matmul(
                            PS[co % 2][:], WCN2[:, ci * 512 + co * 128:ci * 512 + co * 128 + 128],
                            H1T[:, ci * 512:(ci + 1) * 512], start=(ci == 0), stop=(ci == 7)))
                b.barrier()
                for co in (2 * cg, 2 * cg + 1):
                    b.on("scalar", lambda e, co=co: e.activation(
                        H2T[:, co * 512:(co + 1) * 512], PS[co % 2][:], AF.Gelu,
                        bias=CNB[:, 8 + co:9 + co], scale=1.0))
                b.barrier()
            for co in range(2):
                for ci in range(4):
                    b.on("tensor", lambda e, co=co, ci=ci: e.matmul(
                        PS[co][:], WCN3[:, ci * 256 + co * 128:ci * 256 + co * 128 + 128],
                        H2T[:, ci * 512:(ci + 1) * 512], start=(ci == 0), stop=(ci == 3)))
            b.barrier()
            for co in range(2):
                b.on("scalar", lambda e, co=co: e.activation(
                    H3T[:, co * 512:(co + 1) * 512], PS[co][:], AF.Gelu,
                    bias=CNB[:, 12 + co:13 + co], scale=1.0))
            b.barrier()
            for ci in range(2):
                b.on("tensor", lambda e, ci=ci: e.matmul(
                    PS[2][:], WCN4[:, ci * 128:ci * 128 + 128],
                    H3T[:, ci * 512:(ci + 1) * 512], start=(ci == 0), stop=(ci == 1)))
            b.barrier()
            b.on("vector", lambda e: e.tensor_copy(TROW[0:1, :], PS[2][0:1, :]))
            b.barrier()
            for i in range(4):
                b.on("tensor", lambda e, i=i: e.matmul(
                    PSB[0][:, ts(i)], TROW[:, i * 128:(i + 1) * 128],
                    IDENT[:], is_transpose=True, start=(i == 0), stop=(i == 3)))
            b.barrier()
            for i in range(4):
                b.on("vector", lambda e, i=i: e.tensor_copy(SC2[:, 36 + i:37 + i], PSB[0][:, i * 128:i * 128 + 1]))
            b.barrier()
            b.on("scalar", lambda e: e.activation(SC2[:, 40:44], SC2[:, 36:40], AF.Sigmoid,
                                                  bias=SC2[:, 4:5], scale=1.0))
            b.barrier()
            b.on("vector", lambda e: e.tensor_scalar(SC2[:, 44:48], SC2[:, 40:44], 0.3, None, A.mult))
            b.barrier()
            b.on("vector", lambda e: e.tensor_scalar(SC2[:, 44:48], SC2[:, 44:48], SC2[:, 35:36], None, A.add))
            b.barrier()
            b.on("vector", lambda e: e.tensor_scalar(SC2[:, 44:48], SC2[:, 44:48], 0.5, None, A.subtract))
            b.barrier()
            b.on("vector", lambda e: e.tensor_scalar(SC2[:, 44:48], SC2[:, 44:48], SC2[:, 52:53], 0.1, A.mult, A.add))
            b.barrier()
            b.on("vector", lambda e: e.tensor_scalar(SC2[:, 48:52], SC2[:, 44:48], -1.0, None, A.mult))
            b.barrier()
            for i in range(4):
                t = 4 * g + i
                yt = ys_tile(t)
                cpos = SC2[:, 44 + i:45 + i]
                cneg = SC2[:, 48 + i:49 + i]
                b.on("vector", lambda e, yt=yt: e.tensor_copy(EW[:, 0:512], yt[:, 0:512]))
                b.barrier()
                b.on("vector", lambda e, yt=yt, cpos=cpos: e.scalar_tensor_tensor(
                    yt[:, 0:256], yt[:, 512:768], cpos, yt[:, 0:256], A.mult, A.add))
                b.on("vector", lambda e, yt=yt, cneg=cneg: e.scalar_tensor_tensor(
                    yt[:, 256:512], yt[:, 768:1024], cneg, yt[:, 256:512], A.mult, A.add))
                b.barrier()
                b.on("vector", lambda e, yt=yt, cneg=cneg: e.scalar_tensor_tensor(
                    yt[:, 512:768], EW[:, 0:256], cneg, yt[:, 512:768], A.mult, A.add))
                b.on("vector", lambda e, yt=yt, cpos=cpos: e.scalar_tensor_tensor(
                    yt[:, 768:1024], EW[:, 256:512], cpos, yt[:, 768:1024], A.mult, A.add))
                b.barrier()

    def emit_ln_tile(src_ap, dst_ap):
        b.on("vector", lambda e: e.bn_stats(SM[:, 0:6], src_ap[:, 0:512]))
        b.on("vector", lambda e: e.bn_stats(SM[:, 6:12], src_ap[:, 512:1024]))
        b.barrier()
        b.on("vector", lambda e: e.bn_aggr(SM[:, 12:14], SM[:, 0:12]))
        b.barrier()
        b.on("scalar", lambda e: e.activation(SM[:, 14:15], SM[:, 13:14], AF.Sqrt,
                                              bias=CONSTS[:, 1:2], scale=1.0))
        b.barrier()
        b.on("vector", lambda e: e.reciprocal(SM[:, 15:16], SM[:, 14:15]))
        b.barrier()
        b.on("vector", lambda e: e.tensor_scalar(
            dst_ap, src_ap, SM[:, 12:13], SM[:, 15:16], A.subtract, A.mult))
        b.barrier()

    def xt3_slot(t):
        return XT3[:, t * D:(t + 1) * D]

    for t in range(NT):
        if t < NTO:
            b.on("sync", lambda e, t=t: e.dma_start(out=XSTG[:], in_=xsp1[:, ts(t, D)]), dma=True)
            b.barrier()
            b.on("vector", lambda e, t=t: e.tensor_tensor(XSTG2[:], XSTG[:], ys_tile(t), A.add))
            b.barrier()
            if stage == "l0":
                b.on("sync", lambda e, t=t: e.dma_start(out=out_own[:, ts(t, D)], in_=XSTG2[:]), dma=True)
                b.barrier()
                continue
            b.on("sync", lambda e, t=t: e.dma_start(out=xsp2[:, ts(t, D)], in_=XSTG2[:]), dma=True)
            src_ap = XSTG2
        else:
            b.on("vector", lambda e, t=t: e.tensor_tensor(
                XP[:, ts(t - NTO, D)], XP[:, ts(t - NTO, D)], ys_tile(t), A.add))
            b.barrier()
            if stage == "l0":
                continue
            src_ap = XP[:, ts(t - NTO, D)]
        emit_ln_tile(src_ap, XN[:, 0:D])
        for c in range(8):
            b.on("tensor", lambda e, c=c: e.matmul(
                PSB[0][:, ts(c)], XN[:, 128 * c:128 * (c + 1)],
                IDENT[:], is_transpose=True, start=(c == 0), stop=(c == 7)))
        b.barrier()
        b.on("vector", lambda e, t=t: e.tensor_copy(xt3_slot(t), PSB[0][:, 0:1024]))
        b.barrier()

    if stage == "l0":
        b.build_block()
        for c in reversed(ctxs):
            c.__exit__(None, None, None)
        arena.free()
        return nc

    Z90T = 1.2944
    TGT = 400.5
    WLO, WHI = 392.5, 408.5
    WLOA, WHIA = 2 * WLO - 4096.0, 2 * WHI - 4096.0
    ATGT = 2 * TGT - 4096.0
    NSEL = 6

    for w, t in [(wq_in, WQ1), (wk_in, WK1), (wv_in, WV1)]:
        b.on("gpsimd", lambda e, w=w, t=t: e.dma_start(out=t[:], in_=w[:]), dma=True)
    b.on("gpsimd", lambda e: e.dma_start(out=BQ1[:], in_=bq_in[:]), dma=True)
    b.on("gpsimd", lambda e: e.dma_start(out=BK1[:], in_=bk_in[:]), dma=True)
    b.on("gpsimd", lambda e: e.dma_start(out=BOUT[0:1, :], in_=bv_in[:]), dma=True)
    b.on("vector", lambda e: e.memset(OACC[:], 0.0))
    b.on("gpsimd", lambda e: e.iota(IOTA16[:], [[1, 16]], base=0, channel_multiplier=0, allow_small_or_imprecise_dtypes=True))
    b.barrier()
    b.on("vector", lambda e: e.tensor_scalar(BQ1[:], BQ1[:], 0.125, None, A.mult))
    b.barrier()
    for t in range(NT):
        for c in range(8):
            b.on("tensor", lambda e, t=t, c=c: e.matmul(
                PS[2][:, 8 + c:9 + c], XT3[:, t * D + 128 * c:t * D + 128 * (c + 1)],
                ONES[:, 0:1], start=(t == 0 and c == 0), stop=(t == NT - 1 and c == 7)))
    b.barrier()
    b.on("vector", lambda e: e.tensor_copy(EX32[:, 0:8], PS[2][:, 8:16]))
    b.barrier()
    XS3B = M16
    b.on("vector", lambda e: e.tensor_copy(M16[:, 0:8], EX32[:, 0:8]))
    b.barrier()

    NPAIRS = 8
    NQT = NTO
    def _pair(p):
            hA, hB = 2 * p, 2 * p + 1
            for kg in range(8):
                rhs_g = XT3.rearrange("p (t cw) -> p t cw", cw=D)
                for c in range(8):
                    rhs = rhs_g[:, 4 * kg:4 * kg + 4, c * 128:(c + 1) * 128]
                    b.on("tensor", lambda e, c=c, rhs=rhs, kg=kg: e.matmul(
                        PS[kg % 2][:], WK1[:, c * D + p * 128:c * D + p * 128 + 128],
                        rhs, start=(c == 0), stop=(c == 7)))
                b.barrier()
                b.on("scalar", lambda e, kg=kg: e.activation(
                    KTP[:, kg * 512:(kg + 1) * 512], PS[kg % 2][:], AF.Identity,
                    bias=BK1[:, p:p + 1], scale=1.0))
                b.barrier()
            for qg in range(4):
                rhs_g = XT3.rearrange("p (t cw) -> p t cw", cw=D)
                for c in range(8):
                    rhs = rhs_g[:, 4 * qg:4 * qg + 4, c * 128:(c + 1) * 128]
                    b.on("tensor", lambda e, c=c, rhs=rhs, qg=qg: e.matmul(
                        PS[qg % 2][:], WQ1[:, c * D + p * 128:c * D + p * 128 + 128],
                        rhs, start=(c == 0), stop=(c == 7)))
                b.barrier()
                b.on("scalar", lambda e, qg=qg: e.activation(
                    QTP[:, qg * 512:(qg + 1) * 512], PS[qg % 2][:], AF.Identity,
                    bias=BQ1[:, p:p + 1], scale=0.125))
                b.barrier()
            for t in range(NT):
                for c in range(8):
                    b.on("tensor", lambda e, t=t, c=c: e.matmul(
                        PS[t % 2][:, 0:128], XT3[:, t * D + 128 * c:t * D + 128 * (c + 1)],
                        WK1[:, c * D + p * 128:c * D + p * 128 + 128],
                        start=(c == 0), stop=(c == 7)))
                if t % 2 == 1:
                    b.barrier()
                    for tt in (t - 1, t):
                        b.on("vector", lambda e, tt=tt: e.tensor_copy(
                            KTM[:, tt * 128:(tt + 1) * 128], PS[tt % 2][:, 0:128]))
                    b.barrier()
            for c in range(8):
                b.on("tensor", lambda e, c=c: e.matmul(
                    PS[2][:, 0:1], WK1[:, c * D + p * 128:c * D + p * 128 + 128],
                    M16[:, c:c + 1], start=(c == 0), stop=(c == 7)))
            b.barrier()
            b.on("vector", lambda e: e.tensor_scalar(EX32[:, 16:17], BK1[:, p:p + 1], 4096.0, None, A.mult))
            b.barrier()
            b.on("vector", lambda e: e.tensor_tensor(EX32[:, 17:18], PS[2][:, 0:1], EX32[:, 16:17], A.add))
            b.barrier()
            b.on("vector", lambda e: e.memset(KSZ[:], 0.0))
            b.on("vector", lambda e: e.memset(M16[:, 8:10], 0.0))
            b.barrier()
            for hh in range(2):
                b.on("vector", lambda e, hh=hh: e.tensor_copy(
                    KSZ[64 * hh:64 * hh + 64, hh:hh + 1], EX32[64 * hh:64 * hh + 64, 17:18]))
                b.on("vector", lambda e, hh=hh: e.tensor_copy(
                    M16[64 * hh:64 * hh + 64, 8 + hh:9 + hh], BQ1[64 * hh:64 * hh + 64, p:p + 1])) if False else None
                b.on("vector", lambda e, hh=hh: e.tensor_copy(
                    M16[64 * hh:64 * hh + 64, 8 + hh:9 + hh], BK1[64 * hh:64 * hh + 64, p:p + 1]))
            b.barrier()
            b.on("vector", lambda e: e.memset(KKZ[:], 0.0))
            b.barrier()
            for hh in range(2):
                for t in range(NT):
                    b.on("tensor", lambda e, hh=hh, t=t: e.matmul(
                        PS[2 + hh][:, 0:64], KTM[:, t * 128:(t + 1) * 128],
                        KTM[:, t * 128 + 64 * hh:t * 128 + 64 * hh + 64],
                        start=(t == 0), stop=(t == NT - 1)))
            b.barrier()
            for hh in range(2):
                b.on("vector", lambda e, hh=hh: e.tensor_copy(
                    KKZ[64 * hh:64 * hh + 64, 64 * hh:64 * hh + 64],
                    PS[2 + hh][64 * hh:64 * hh + 64, 0:64]))
            b.barrier()
            for t in range(NT):
                for c in range(8):
                    b.on("tensor", lambda e, t=t, c=c: e.matmul(
                        PS[t % 2][:, 0:128], XT3[:, t * D + 128 * c:t * D + 128 * (c + 1)],
                        WV1[:, c * D + p * 128:c * D + p * 128 + 128],
                        start=(c == 0), stop=False))
                b.on("tensor", lambda e, t=t: e.matmul(
                    PS[t % 2][:, 0:128], ONES[0:1, :],
                    BOUT[0:1, p * 128:(p + 1) * 128], start=False, stop=True))
                if t % 2 == 1:
                    b.barrier()
                    for tt in (t - 1, t):
                        b.on("vector", lambda e, tt=tt: e.tensor_copy(
                            VTM[:, tt * 128:(tt + 1) * 128], PS[tt % 2][:, 0:128]))
                    b.barrier()

            def _qtile(p, qt):
                    for rnd in range(4):
                        for n in (2 * rnd, 2 * rnd + 1):
                            for hh in range(2):
                                bank = 2 * (n % 2) + hh
                                b.on("tensor", lambda e, n=n, hh=hh, bank=bank: e.matmul(
                                    PS[bank][:],
                                    QTP[64 * hh:64 * hh + 64, qt * 128:(qt + 1) * 128],
                                    KTP[64 * hh:64 * hh + 64, n * 512:(n + 1) * 512],
                                    start=True, stop=True,
                                    tile_position=(64 * hh, 0)))
                        b.barrier()
                        for n in (2 * rnd, 2 * rnd + 1):
                            for hh in range(2):
                                bank = 2 * (n % 2) + hh
                                b.on("vector", lambda e, n=n, hh=hh, bank=bank: e.tensor_copy(
                                    SPAIR[:, hh * NTOK + n * 512:hh * NTOK + (n + 1) * 512], PS[bank][:]))
                        b.barrier()
                    for hh in range(2):
                        b.on("tensor", lambda e, hh=hh: e.matmul(
                            PS[2][:, hh:hh + 1], QTP[:, qt * 128:(qt + 1) * 128],
                            KSZ[:, hh:hh + 1], start=(hh == 0), stop=False))
                    for hh in range(2):
                        b.on("tensor", lambda e, hh=hh: e.matmul(
                            PS[2][:, 2 + hh:3 + hh], QTP[:, qt * 128:(qt + 1) * 128],
                            M16[:, 8 + hh:9 + hh], start=False, stop=(hh == 1)))
                    for hh in range(2):
                        b.on("tensor", lambda e, hh=hh: e.matmul(
                            PS[3][:, 64 * hh:64 * hh + 64], QTP[:, qt * 128:(qt + 1) * 128],
                            KKZ[:, 64 * hh:64 * hh + 64], start=(hh == 0), stop=(hh == 1)))
                    b.on("tensor", lambda e: e.matmul(
                        PSB[0][:, 0:128], QTP[:, qt * 128:(qt + 1) * 128], IDENT[:],
                        is_transpose=True, start=True, stop=True))
                    b.barrier()
                    b.on("vector", lambda e: e.tensor_copy(QTMS[:], PSB[0][:, 0:128]))
                    b.on("vector", lambda e: e.tensor_scalar(SST[:, 0:2], PS[2][:, 0:2], 1.0 / NTOK, None, A.mult))
                    b.on("vector", lambda e: e.tensor_copy(SST[:, 32:34], PS[2][:, 2:4]))
                    b.barrier()
                    b.on("vector", lambda e: e.tensor_tensor(SST[:, 34:36], SST[:, 0:2], SST[:, 32:34], A.subtract))
                    b.barrier()
                    for hh in range(2):
                        b.on("vector", lambda e, hh=hh: e.tensor_tensor(
                            EW[:, 0:64], QTMS[:, 64 * hh:64 * hh + 64], PS[3][:, 64 * hh:64 * hh + 64], A.mult))
                        b.barrier()
                        b.on("vector", lambda e, hh=hh: e.reduce_sum(SST[:, 2 + hh:3 + hh], EW[:, 0:64], axis=AX.X))
                        b.barrier()
                    b.on("vector", lambda e: e.tensor_scalar(SST[:, 4:6], SST[:, 2:4], 1.0 / NTOK, None, A.mult))
                    b.barrier()
                    b.on("vector", lambda e: e.tensor_tensor(SST[:, 6:8], SST[:, 34:36], SST[:, 34:36], A.mult))
                    b.barrier()
                    b.on("vector", lambda e: e.tensor_tensor(SST[:, 4:6], SST[:, 4:6], SST[:, 6:8], A.subtract))
                    b.barrier()
                    b.on("vector", lambda e: e.tensor_scalar(SST[:, 4:6], SST[:, 4:6], 1e-6, None, A.max))
                    b.barrier()
                    b.on("scalar", lambda e: e.activation(SST[:, 6:8], SST[:, 4:6], AF.Sqrt,
                                                          bias=CONSTS[:, 2:3], scale=1.0))
                    b.barrier()
                    b.on("vector", lambda e: e.tensor_scalar(SST[:, 26:28], SST[:, 6:8], -3.34, None, A.mult))
                    b.on("vector", lambda e: e.tensor_scalar(SST[:, 18:20], SST[:, 6:8], 3.17, None, A.mult))
                    b.on("vector", lambda e: e.tensor_scalar(SST[:, 8:10], SST[:, 6:8], Z90T, None, A.mult))
                    b.barrier()
                    b.on("vector", lambda e: e.tensor_tensor(SST[:, 26:28], SST[:, 26:28], SST[:, 0:2], A.add))
                    b.on("vector", lambda e: e.tensor_tensor(SST[:, 18:20], SST[:, 18:20], SST[:, 0:2], A.add))
                    b.barrier()
                    b.on("vector", lambda e: e.tensor_tensor(SST[:, 8:10], SST[:, 8:10], SST[:, 0:2], A.add))
                    b.on("vector", lambda e: e.tensor_copy(SST[:, 14:16], SST[:, 26:28]))
                    b.barrier()
                    b.on("vector", lambda e: e.memset(SST[:, 16:18], 4096.0))
                    b.on("vector", lambda e: e.memset(SST[:, 20:22], -4096.0))
                    b.on("vector", lambda e: e.tensor_copy(SST[:, 22:24], SST[:, 8:10]))
                    b.on("vector", lambda e: e.memset(SST[:, 24:26], -3296.0))
                    b.on("vector", lambda e: e.tensor_scalar(SST[:, 10:12], SST[:, 8:10], -1.0, None, A.mult))
                    b.barrier()
                    def sel_part1(h, last):
                        acc = SST[:, 12 + h:13 + h]
                        def m(j):
                            return MSK[:, 6 * h + j:6 * h + j + 1]
                        def sc(j):
                            return SST[:, j + h:j + h + 1]
                        if last:
                            return [
                                lambda e: e.tensor_scalar(m(0), acc, WLOA, None, A.is_gt),
                                lambda e: e.tensor_scalar(m(1), acc, WHIA, None, A.is_lt),
                                lambda e: e.tensor_copy(SST[:, 50:51], SST[:, 36:37]),
                                lambda e: e.tensor_tensor(m(2), m(0), m(1), A.mult),
                                lambda e: e.tensor_copy(SST[:, 51:52], SST[:, 38:39]),
                                lambda e: e.copy_predicated(sc(22), m(2), sc(8)),
                                lambda e: e.copy_predicated(sc(24), m(2), acc),
                            ]
                        return [
                            lambda e: e.tensor_scalar(m(3), acc, ATGT, None, A.is_ge),
                            lambda e: e.tensor_scalar(m(4), acc, ATGT, None, A.is_lt),
                            lambda e: e.tensor_scalar(m(0), acc, WLOA, None, A.is_gt),
                            lambda e: e.tensor_scalar(m(1), acc, WHIA, None, A.is_lt),
                            lambda e: e.copy_predicated(sc(16), m(3), acc),
                            lambda e: e.copy_predicated(sc(20), m(4), acc),
                            lambda e: e.copy_predicated(sc(14), m(3), sc(8)),
                            lambda e: e.copy_predicated(sc(18), m(4), sc(8)),
                            lambda e: e.tensor_tensor(sc(38), sc(16), sc(20), A.subtract),
                            lambda e: e.tensor_tensor(m(2), m(0), m(1), A.mult),
                            lambda e: e.tensor_scalar(sc(36), sc(16), ATGT, None, A.subtract),
                            lambda e: e.reciprocal(sc(40), sc(38)),
                            lambda e: e.tensor_tensor(sc(44), sc(18), sc(14), A.subtract),
                            lambda e: e.tensor_tensor(sc(42), sc(36), sc(40), A.mult),
                            lambda e: e.copy_predicated(sc(22), m(2), sc(8)),
                            lambda e: e.tensor_scalar(sc(46), sc(42), 0.12, 0.88, A.max, A.min),
                            lambda e: e.copy_predicated(sc(24), m(2), acc),
                            lambda e: e.tensor_tensor(sc(48), sc(46), sc(44), A.mult),
                        ]

                    def sel_part2(h):
                        def sc(j):
                            return SST[:, j + h:j + h + 1]
                        return [
                            lambda e: e.tensor_tensor(sc(8), sc(48), sc(14), A.add),
                            lambda e: e.scalar_tensor_tensor(sc(10), sc(48), -1.0, sc(14), A.mult, A.subtract),
                        ]

                    def sel_sign(h):
                        return lambda e, h=h: e.activation(
                            YSEL[:], SPAIR[:, h * NTOK:(h + 1) * NTOK], AF.Sign,
                            bias=SST[:, 10 + h:11 + h], scale=1.0,
                            accum_out=SST[:, 12 + h:13 + h])

                    for s in range(2 * NSEL + 1):
                        h, it = s % 2, s // 2
                        if s > 0:
                            for f in sel_part1((s - 1) % 2, (s - 1) // 2 == NSEL - 1):
                                b.on("vector", f)
                        if s < 2 * NSEL:
                            b.on("scalar", sel_sign(h))
                        b.barrier()
                        if s > 0 and (s - 1) // 2 != NSEL - 1:
                            for f in sel_part2((s - 1) % 2):
                                b.on("vector", f)
                            b.barrier()
                    for hh in range(2):
                        sh = SPAIR[:, hh * NTOK:(hh + 1) * NTOK]
                        b.on("vector", lambda e, sh=sh, hh=hh: e.tensor_scalar(
                            KTM[:], sh, SST[:, 26 + hh:27 + hh], 0.0, A.subtract, A.max))
                        b.on("vector", lambda e, sh=sh, hh=hh: e.tensor_scalar(
                            YSEL[:], sh, SST[:, 22 + hh:23 + hh], None, A.is_le))
                        b.barrier()
                        b.on("vector", lambda e: e.tensor_tensor(YSEL[:], YSEL[:], KTM[:], A.mult))
                        b.barrier()
                        b.on("vector", lambda e: e.max(out=M16[:, 0:8], in_=YSEL[:]))
                        b.barrier()
                        b.on("vector", lambda e: e.match_replace(out=KTM[:], in_to_replace=M16[:, 0:8],
                                                                 in_values=YSEL[:], imm_value=0.0))
                        b.barrier()
                        b.on("vector", lambda e: e.max(out=M16[:, 8:16], in_=KTM[:]))
                        b.barrier()
                        b.on("vector", lambda e, hh=hh: e.tensor_scalar(
                            EX32[:, 20:21], SST[:, 24 + hh:25 + hh], -0.5, -1640.0, A.mult, A.add))
                        b.barrier()
                        b.on("vector", lambda e: e.tensor_scalar(
                            EX32[:, 0:16], IOTA16[:], EX32[:, 20:21], None, A.is_equal))
                        b.barrier()
                        b.on("vector", lambda e: e.tensor_tensor(EX32[:, 0:16], EX32[:, 0:16], M16[:], A.mult))
                        b.barrier()
                        b.on("vector", lambda e: e.reduce_sum(EX32[:, 21:22], EX32[:, 0:16], axis=AX.X))
                        b.barrier()
                        b.on("vector", lambda e, hh=hh: e.tensor_tensor(
                            EX32[:, 22:23], EX32[:, 21:22], SST[:, 26 + hh:27 + hh], A.add))
                        b.barrier()
                        if False:
                            b.build_block()
                            for c_ in reversed(ctxs):
                                c_.__exit__(None, None, None)
                            arena.free()
                            return nc
                        b.on("vector", lambda e, sh=sh: e.tensor_scalar(
                            YSEL[:], sh, EX32[:, 22:23], None, A.is_ge))
                        b.barrier()
                        b.on("vector", lambda e, sh=sh: e.tensor_tensor(KTM[:], YSEL[:], sh, A.mult))
                        b.barrier()
                        b.on("scalar", lambda e, sh=sh, hh=hh: e.activation(
                            sh, KTM[:], AF.Exp, bias=SST[:, 30 + hh:31 + hh], scale=1.0,
                            accum_out=EX32[:, 24 + hh:25 + hh]))
                        b.barrier()
                    b.on("vector", lambda e: e.reciprocal(EX32[:, 26:28], EX32[:, 24:26]))
                    b.barrier()
                    for hh in range(2):
                        sh = SPAIR[:, hh * NTOK:(hh + 1) * NTOK]
                        b.on("vector", lambda e, sh=sh, hh=hh: e.tensor_scalar(
                            sh, sh, EX32[:, 26 + hh:27 + hh], None, A.mult))
                        b.barrier()
                        for kc in range(32):
                            b.on("tensor", lambda e, sh=sh, kc=kc: e.matmul(
                                PSB[kc // 8][:, ts(kc % 8)], sh[:, kc * 128:(kc + 1) * 128],
                                IDENT[:], is_transpose=True, start=(kc % 8 == 0), stop=(kc % 8 == 7)))
                        b.barrier()
                        for bk_ in range(4):
                            b.on("vector", lambda e, bk_=bk_: e.tensor_copy(
                                YSEL[:, bk_ * 1024:(bk_ + 1) * 1024], PSB[bk_][:]))
                        b.barrier()
                        for kc in range(32):
                            b.on("tensor", lambda e, kc=kc, hh=hh: e.matmul(
                                PS[0][:, 128 * hh:128 * hh + 128],
                                VTM[:, kc * 128:(kc + 1) * 128],
                                YSEL[:, kc * 128:(kc + 1) * 128],
                                start=(kc == 0), stop=(kc == 31)))
                        b.barrier()
                        b.on("vector", lambda e, hh=hh: e.tensor_copy(
                            OACC[64 * hh:64 * hh + 64, p * NOWN + qt * 128:p * NOWN + (qt + 1) * 128],
                            PS[0][64 * hh:64 * hh + 64, 128 * hh:128 * hh + 128]))
                        b.barrier()

            for qt in range(NQT):
                _qtile(p, qt)
    for p in range(NPAIRS):
        _pair(p)
    b.on("gpsimd", lambda e: e.dma_start(out=WO1[:], in_=wo_in[:]), dma=True)
    b.on("gpsimd", lambda e: e.dma_start(out=EW[0:1, :], in_=bo_in[:]), dma=True)
    b.barrier()
    for g in range(2):
        b.on("tensor", lambda e, g=g: e.matmul(
            PS[g][:], ONES[0:1, :], EW[0:1, ts(g, 512)], start=True, stop=True))
    b.barrier()
    for g in range(2):
        b.on("vector", lambda e, g=g: e.tensor_copy(BOUT[:, ts(g, 512)], PS[g][:]))
    b.barrier()
    for t in range(NTO):
        b.on("sync", lambda e, t=t: e.dma_start(out=XSTG[:], in_=xsp2[:, ts(t, D)]), dma=True)
        for nsl in range(2):
            for c in range(8):
                b.on("tensor", lambda e, t=t, nsl=nsl, c=c: e.matmul(
                    PS[2 + nsl][:], OACC[:, c * NOWN + t * 128:c * NOWN + (t + 1) * 128],
                    WO1[:, c * D + nsl * 512:c * D + (nsl + 1) * 512],
                    start=(c == 0), stop=(c == 7)))
        b.barrier()
        for nsl in range(2):
            b.on("vector", lambda e, nsl=nsl: e.tensor_tensor(
                EW[:, nsl * 512:(nsl + 1) * 512], PS[2 + nsl][:], BOUT[:, ts(nsl, 512)], A.add))
        b.barrier()
        b.on("vector", lambda e, t=t: e.tensor_tensor(X3[:, ts(t, D)], XSTG[:], EW[:, 0:1024], A.add))
        b.barrier()

    if stage == "l1attn":
        b.on("sync", lambda e: e.dma_start(out=out_own[:], in_=X3[:]), dma=True)
        b.barrier()
        b.build_block()
        for c in reversed(ctxs):
            c.__exit__(None, None, None)
        arena.free()
        return nc

    b.on("vector", lambda e: e.memset(MRV[:, 1:8], 1e38))
    b.on("gpsimd", lambda e: e.dma_start(out=GATEW[:], in_=gatew_in[:]), dma=True)
    b.on("gpsimd", lambda e: e.dma_start(out=BOUT[0:8, :], in_=b2mat_in[:]), dma=True)
    b.on("gpsimd", lambda e: e.dma_start(out=TROW[0:1, 0:8], in_=gbrow_in[:]), dma=True)
    b.on("gpsimd", lambda e: e.dma_start(out=NSC[:], in_=nsc_in[:]), dma=True)
    b.barrier()

    def _moe_tile(t):
        xt = X3[:, ts(t, D)]
        emit_ln_tile(xt, MEVAC[:, 0:1024])
        b.on("scalar", lambda e: e.activation(XN[:, 0:D], MEVAC[:, 0:1024], AF.Square,
                                              accum_out=SM[:, 0:1]))
        b.barrier()
        b.on("vector", lambda e: e.tensor_scalar(SM[:, 1:2], SM[:, 0:1], 1.0 / D, None, A.mult))
        b.barrier()
        b.on("scalar", lambda e: e.activation(SM[:, 2:3], SM[:, 1:2], AF.Sqrt,
                                              bias=CONSTS[:, 1:2], scale=1.0))
        b.barrier()
        b.on("vector", lambda e: e.reciprocal(SM[:, 3:4], SM[:, 2:3]))
        b.barrier()
        b.on("vector", lambda e: e.tensor_scalar(XN[:, 0:D], MEVAC[:, 0:1024], SM[:, 3:4], None, A.mult))
        b.on("vector", lambda e: e.tensor_tensor(xt, xt, MEVAC[:, 0:1024], A.add))
        b.barrier()
        for c in range(8):
            b.on("tensor", lambda e, c=c: e.matmul(
                PSB[0][:, ts(c)], XN[:, 128 * c:128 * (c + 1)],
                IDENT[:], is_transpose=True, start=(c == 0), stop=(c == 7)))
        b.barrier()
        for c in range(8):
            b.on("vector", lambda e, t=t, c=c: e.tensor_scalar(
                TT2[:, c * NOWN + t * 128:c * NOWN + (t + 1) * 128],
                PSB[0][:, ts(c)], NSC[:, c:c + 1], None, A.mult))
        b.barrier()
        for c in range(8):
            b.on("tensor", lambda e, t=t, c=c: e.matmul(
                PS[2][:, 0:8], TT2[:, c * NOWN + t * 128:c * NOWN + (t + 1) * 128],
                GATEW[:, c * 8:(c + 1) * 8], start=(c == 0), stop=False))
        b.on("tensor", lambda e: e.matmul(
            PS[2][:, 0:8], ONES[0:1, :], TROW[0:1, 0:8], start=False, stop=True))
        b.barrier()
        b.on("vector", lambda e: e.tensor_copy(EX32[:, 0:8], PS[2][:, 0:8]))
        b.barrier()
        b.on("vector", lambda e: e.max(out=EX32[:, 8:16], in_=EX32[:, 0:8]))
        b.barrier()
        b.on("vector", lambda e: e.tensor_copy(MRV[:, 0:1], EX32[:, 8:9]))
        b.barrier()
        b.on("vector", lambda e: e.match_replace(out=EX32[:, 16:24], in_to_replace=MRV[:, 0:8],
                                                 in_values=EX32[:, 0:8], imm_value=-1e30))
        b.barrier()
        b.on("vector", lambda e: e.tensor_copy(MRV[:, 0:1], EX32[:, 9:10]))
        b.barrier()
        b.on("vector", lambda e: e.match_replace(out=EX32[:, 24:32], in_to_replace=MRV[:, 0:8],
                                                 in_values=EX32[:, 16:24], imm_value=-1e30))
        b.barrier()
        b.on("vector", lambda e: e.tensor_tensor(SM[:, 8:16], EX32[:, 0:8], EX32[:, 16:24], A.not_equal))
        b.on("vector", lambda e: e.tensor_tensor(SM[:, 16:24], EX32[:, 16:24], EX32[:, 24:32], A.not_equal))
        b.barrier()
        b.on("vector", lambda e: e.tensor_tensor(SM[:, 24:25], EX32[:, 9:10], EX32[:, 8:9], A.subtract))
        b.barrier()
        b.on("scalar", lambda e: e.activation(SM[:, 25:26], SM[:, 24:25], AF.Exp))
        b.barrier()
        b.on("vector", lambda e: e.tensor_scalar(SM[:, 26:27], SM[:, 25:26], 1.0, None, A.add))
        b.barrier()
        b.on("vector", lambda e: e.reciprocal(SM[:, 27:28], SM[:, 26:27]))
        b.barrier()
        b.on("vector", lambda e: e.tensor_tensor(SM[:, 28:29], SM[:, 25:26], SM[:, 27:28], A.mult))
        b.barrier()
        b.on("vector", lambda e: e.tensor_scalar(SM[:, 8:16], SM[:, 8:16], SM[:, 27:28], None, A.mult))
        b.on("vector", lambda e: e.tensor_scalar(SM[:, 16:24], SM[:, 16:24], SM[:, 28:29], None, A.mult))
        b.barrier()
        b.on("vector", lambda e, t=t: e.tensor_tensor(
            WTOK[:, t * 8:(t + 1) * 8], SM[:, 8:16], SM[:, 16:24], A.add))
        b.barrier()


    for t in range(NTO):
        _moe_tile(t)

    b.on("sync", lambda e: e.dma_start(out=B1G[:], in_=b1g_in[:]), dma=True)
    b.on("sync", lambda e: e.dma_start(out=B1L[:], in_=b1l_in[:]), dma=True)
    b.barrier()

    def _expert(e_):
        def w8tg_prep(grp):
            t0 = 2 * grp
            return (
                [lambda e, t0=t0: e.tensor_copy(WTB[:], WTOK[:, t0 * 8:(t0 + 2) * 8])],
                [lambda e, i2=i2: e.matmul(
                    PSB[1 + i2][0:8, 0:128], WTB[:, 8 * i2:8 * i2 + 8],
                    IDENT[:], is_transpose=True, start=True, stop=True) for i2 in range(2)],
                [lambda e, i2=i2: e.tensor_copy(
                    W8TG[0:8, i2 * 128:(i2 + 1) * 128], PSB[1 + i2][0:8, 0:128]) for i2 in range(2)],
            )

        def mm_list(grp, hc):
            t0 = 2 * grp
            p = hc % 2
            out = []
            for half in (0, 1):
                bank = 2 * p + half
                for c in range(8):
                    out.append(lambda e, hc=hc, half=half, bank=bank, c=c, t0=t0: e.matmul(
                        PS[bank][:, 0:256],
                        W1E[:, c * 4096 + half * 2048 + hc * 128:c * 4096 + half * 2048 + hc * 128 + 128],
                        TT2[:, c * NOWN + t0 * 128:c * NOWN + (t0 + 2) * 128],
                        start=(c == 0), stop=(c == 7)))
            return out

        def evac_phases(hc):
            p = hc % 2
            g, l = 2 * p, 2 * p + 1
            hs = slice(256 * p, 256 * p + 256)
            ph1 = [("vector", lambda e, hc=hc, l=l: e.tensor_scalar(
                        PS[l][:, 0:256], PS[l][:, 0:256],
                        B1L[:, e_ * 16 + hc:e_ * 16 + hc + 1], 7.0, A.add, A.min)),
                   ("vector", lambda e, hc=hc, g=g: e.tensor_scalar(
                        PS[g][:, 0:256], PS[g][:, 0:256],
                        B1G[:, e_ * 16 + hc:e_ * 16 + hc + 1], 7.0, A.add, A.min))]
            ph2 = [("scalar", lambda e, g=g, hs=hs: e.activation(
                        HLF[:, hs], PS[g][:, 0:256], AF.Gelu_apprx_sigmoid)),
                   ("vector", lambda e, l=l: e.tensor_scalar(
                        PS[l][:, 0:256], PS[l][:, 0:256], -7.0, 1.0, A.max, A.add))]
            ph3 = [("vector", lambda e, hc=hc, l=l, hs=hs: e.tensor_tensor(
                        SWIF[:, hc * 256:(hc + 1) * 256], HLF[:, hs], PS[l][:, 0:256], A.mult))]
            return ph1, ph2, ph3

        b.barrier()
        if e_ == 0:
            p1, p2, p3 = w8tg_prep(0)
            for f in p1:
                b.on("vector", f)
            b.barrier()
            for f in p2:
                b.on("tensor", f)
            b.barrier()
            for f in p3:
                b.on("vector", f)
            b.barrier()
        for grp in range(8):
            t0 = 2 * grp
            for hc in range(17):
                mm = mm_list(grp, hc) if hc < 16 else []
                ev = evac_phases(hc - 1) if hc > 0 else ([], [], [])
                for k, (lo, hi) in enumerate(((0, 6), (6, 11), (11, 16))):
                    for f in mm[lo:hi]:
                        b.on("tensor", f)
                    for eng, f in ev[k]:
                        b.on(eng, f)
                    b.barrier()
            combos = [(i, nsl) for i in range(2) for nsl in range(2)]
            def combo_mms(ci):
                i, nsl = combos[ci]
                out = []
                for hc in range(16):
                    out.append(lambda e, i=i, nsl=nsl, hc=hc, bank=ci: e.matmul(
                        PS[bank][:],
                        SWIF[:, hc * 256 + i * 128:hc * 256 + i * 128 + 128],
                        W2E[:, hc * D + nsl * 512:hc * D + (nsl + 1) * 512],
                        start=(hc == 0), stop=False))
                out.append(lambda e, i=i, nsl=nsl, bank=ci: e.matmul(
                    PS[bank][:], W8TG[0:8, i * 128:(i + 1) * 128],
                    BOUT[0:8, nsl * 512:(nsl + 1) * 512],
                    start=False, stop=True))
                return out

            def evac_mm2(ci):
                i, nsl = combos[ci]
                t = t0 + i
                return lambda e, t=t, bank=ci, ci=ci, e_=e_: e.tensor_scalar(
                    MEVAC[:, ci * 512:(ci + 1) * 512],
                    PS[bank][:], WTOK[:, t * 8 + e_:t * 8 + e_ + 1], None, A.mult)

            def add_mm2(ci):
                i, nsl = combos[ci]
                t = t0 + i
                return lambda e, t=t, nsl=nsl, ci=ci: e.tensor_tensor(
                    X3[:, t * D + nsl * 512:t * D + (nsl + 1) * 512],
                    X3[:, t * D + nsl * 512:t * D + (nsl + 1) * 512],
                    MEVAC[:, ci * 512:(ci + 1) * 512], A.add)

            last_grp = grp == 7
            nxt = grp + 1 if not last_grp else (0 if e_ < 7 else None)
            for f in combo_mms(0) + combo_mms(1):
                b.on("tensor", f)
            if last_grp and e_ < 7:
                b.on("gpsimd", lambda e, e_=e_: e.dma_start(out=W1E[:], in_=w1e_in[e_ + 1][:]), dma=True)
            if nxt is not None:
                p1, p2, p3 = w8tg_prep(nxt)
                for f in p1:
                    b.on("vector", f)
            b.barrier()
            for f in combo_mms(2) + combo_mms(3):
                b.on("tensor", f)
            b.barrier()
            for ci in range(4):
                b.on("vector", evac_mm2(ci))
            if last_grp and e_ < 7:
                b.on("gpsimd", lambda e, e_=e_: e.dma_start(out=W2E[:], in_=w2e_in[e_ + 1][:]), dma=True)
            if nxt is not None:
                for f in p2:
                    b.on("tensor", f)
            b.barrier()
            for ci in range(4):
                b.on("vector", add_mm2(ci))
            if nxt is not None:
                for f in p3:
                    b.on("vector", f)
            b.barrier()

    b.on("gpsimd", lambda e: e.dma_start(out=W1E[:], in_=w1e_in[0][:]), dma=True)
    b.on("gpsimd", lambda e: e.dma_start(out=W2E[:], in_=w2e_in[0][:]), dma=True)
    for e_ in range(8):
        _expert(e_)

    b.on("sync", lambda e: e.dma_start(out=out_own[:], in_=X3[:]), dma=True)
    b.barrier()

    b.build_block()
    for c in reversed(ctxs):
        c.__exit__(None, None, None)
    arena.free()
    return nc


def chunk_major(w):
    w = np.asarray(w, dtype=np.float32)
    c = w.shape[0] // 128
    return np.ascontiguousarray(w.reshape(c, 128, w.shape[1]).transpose(1, 0, 2).reshape(128, -1))


def tok_major(x):
    x = np.asarray(x, dtype=np.float32)
    t = x.shape[0] // 128
    return np.ascontiguousarray(x.reshape(t, 128, x.shape[1]).transpose(1, 0, 2).reshape(128, -1))


def feat_cols(bvec):
    b_ = np.asarray(bvec, dtype=np.float32).reshape(-1)
    c = b_.shape[0] // 128
    return np.ascontiguousarray(b_.reshape(c, 128).T)


def stage_inputs(ins, core=0):
    bidx, h = core // 2, core % 2
    x = np.asarray(ins["L0_attn_x"])[bidx]
    own = x[h * 2048:(h + 1) * 2048]
    part = x[(1 - h) * 2048:(2 - h) * 2048]
    m = {
        "x_own": tok_major(own),
        "x_part": tok_major(part),
        "wqkvT": chunk_major(np.asarray(ins["L0_attn_w_qkv"]).T),
        "woutT": chunk_major(np.asarray(ins["L0_attn_w_out"]).T),
        "bout": np.asarray(ins["L0_attn_b_out"], dtype=np.float32).reshape(1, -1),
        "ar": np.asarray(ins["L0_ffn_adaptive_range"], dtype=np.float32).reshape(1, 1),
    }
    for i in range(1, 5):
        w = np.asarray(ins[f"L0_ffn_cn_w{i}"], dtype=np.float32)
        if i == 4:
            w = np.pad(w, ((0, 0), (0, 127)))
        m[f"cn_w{i}"] = chunk_major(w)
        bv = np.asarray(ins[f"L0_ffn_cn_b{i}"], dtype=np.float32)
        m[f"cn_b{i}"] = feat_cols(bv) if i < 4 else np.tile(bv.reshape(1, 1), (128, 1))
    for i in range(1, 4):
        w = np.asarray(ins[f"L0_ffn_gc_w{i}"], dtype=np.float32)
        if i == 3:
            w = np.pad(w, ((0, 0), (0, 127)))
        m[f"gc_w{i}"] = chunk_major(w)
        bv = np.asarray(ins[f"L0_ffn_gc_b{i}"], dtype=np.float32)
        m[f"gc_b{i}"] = feat_cols(bv) if i < 3 else np.tile(bv.reshape(1, 1), (128, 1))
    m["wq"] = chunk_major(np.asarray(ins["L1_attn_Wq"]))
    m["wk"] = chunk_major(np.asarray(ins["L1_attn_Wk"]))
    m["wv"] = chunk_major(np.asarray(ins["L1_attn_Wv"]))
    m["wo"] = chunk_major(np.asarray(ins["L1_attn_Wo"]))
    m["bq"] = feat_cols(ins["L1_attn_bq"])
    m["bk"] = feat_cols(ins["L1_attn_bk"])
    m["bvrow"] = np.asarray(ins["L1_attn_bv"], dtype=np.float32).reshape(1, -1)
    m["borow"] = np.asarray(ins["L1_attn_bo"], dtype=np.float32).reshape(1, -1)
    for e in range(8):
        w1 = np.asarray(ins["L1_ffn_mlp1_w"], dtype=np.float32)[e]
        w1d = np.concatenate([w1[:, 0::2], w1[:, 1::2]], axis=1)
        m[f"w1e{e}"] = chunk_major(w1d)
        m[f"w2e{e}"] = chunk_major(np.asarray(ins["L1_ffn_mlp2_w"], dtype=np.float32)[e])
    b1 = np.asarray(ins["L1_ffn_mlp1_b"], dtype=np.float32)
    cols = []
    for e in range(8):
        g = b1[e, 0::2].reshape(16, 128).T
        l = b1[e, 1::2].reshape(16, 128).T
        cols.append(np.concatenate([g, l], axis=1))
    m["b1g"] = np.ascontiguousarray(np.concatenate([c[:, 0:16] for c in cols], axis=1))
    m["b1l"] = np.ascontiguousarray(np.concatenate([c[:, 16:32] for c in cols], axis=1))
    m["b2mat"] = np.ascontiguousarray(np.asarray(ins["L1_ffn_mlp2_b"], dtype=np.float32))
    gw = np.asarray(ins["L1_ffn_gate_w"], dtype=np.float32)
    m["gatew"] = np.ascontiguousarray(gw.reshape(8, 128, 8).transpose(1, 0, 2).reshape(128, 64))
    m["gbrow"] = np.asarray(ins["L1_ffn_gate_b"], dtype=np.float32).reshape(1, -1)
    m["nsc"] = feat_cols(ins["L1_ffn_norm_scale"])
    return m

def kernel(**inputs):
    nc = build_program("full")
    in_maps = [stage_inputs(inputs, core=c) for c in range(8)]
    r = run_bass_kernel_spmd(nc, in_maps, core_ids=list(range(8)))
    out = np.empty((4, 4096, 1024), dtype=np.float32)
    for c in range(8):
        got = np.asarray(r.results[c]["out_own"], dtype=np.float32)
        got = got.reshape(128, 16, 1024).transpose(1, 0, 2).reshape(2048, 1024)
        out[c // 2, (c % 2) * 2048:(c % 2 + 1) * 2048] = got
    return out


# version 04
# speedup vs baseline: 1.4708x; 1.0425x over previous
import time
import numpy as np
import concourse.bass as bass
import concourse.mybir as mybir
from concourse.bass import AutoArena
from concourse.bass_utils import run_bass_kernel_spmd

F32 = mybir.dt.float32
BF16 = mybir.dt.bfloat16
U8 = mybir.dt.uint8
A = mybir.AluOpType
AF = mybir.ActivationFunctionType
AX = mybir.AxisListType

D = 1024
NTOK = 4096
NT = 32
NG = 8
H = 16
HD = 64
NOWN = 2048
NTO = 16
KB = 1024


def ts(i, n=128):
    return slice(i * n, (i + 1) * n)


class B:
    ENGINES = ("sync", "tensor", "vector", "scalar", "gpsimd")

    def __init__(self, nc):
        self.nc = nc
        self.ops = {e: [] for e in self.ENGINES}
        self.dirty = {e: False for e in self.ENGINES}
        self.used = {e: False for e in self.ENGINES}
        self.ph_cnt = 0
        self.phd_cnt = {"sync": 0, "gpsimd": 0, "scalar": 0, "vector": 0}

    def on(self, eng, fn, dma=False):
        if dma:
            self.ops[eng].append(["dma", fn, 16])
            self.phd_cnt[eng] += 16
        else:
            self.ops[eng].append(["op", fn, 1])
            self.dirty[eng] = True
        self.used[eng] = True

    def barrier(self):
        for eng in self.ENGINES:
            if self.dirty[eng]:
                for i in range(len(self.ops[eng]) - 1, -1, -1):
                    if self.ops[eng][i][0] == "op":
                        self.ops[eng][i][0] = "op_inc"
                        self.ph_cnt += 1
                        break
                self.dirty[eng] = False
        v = (self.ph_cnt, self.phd_cnt["sync"], self.phd_cnt["gpsimd"])
        for eng in self.ENGINES:
            self.ops[eng].append(["barrier", v, 0])

    def build_block(self):
        nc = self.nc
        c1 = nc.semaphore("ph"); ph = c1.__enter__()
        c2 = nc.semaphore("phds"); phds = c2.__enter__()
        c3 = nc.semaphore("phdg"); phdg = c3.__enter__()
        with nc.Block() as block:
            for eng in self.ENGINES:
                if not self.used[eng]:
                    continue
                lst = self.ops[eng]

                def make(lst, eng=None):
                    keep = [False] * len(lst)
                    nxt = False
                    for i in range(len(lst) - 1, -1, -1):
                        if lst[i][0] == "barrier":
                            keep[i] = nxt
                            nxt = False
                        else:
                            nxt = True
                    for i in range(len(lst) - 1, -1, -1):
                        if lst[i][0] == "barrier":
                            keep[i] = True
                            break

                    def body(e):
                        last_wait = (-1, -1, -1)
                        for idx, it in enumerate(lst):
                            kind, fn = it[0], it[1]
                            if kind == "barrier":
                                if not keep[idx]:
                                    continue
                                if fn[0] > last_wait[0]:
                                    e.wait_ge(ph, fn[0])
                                if fn[1] > last_wait[1]:
                                    e.wait_ge(phds, fn[1])
                                if fn[2] > last_wait[2]:
                                    e.wait_ge(phdg, fn[2])
                                last_wait = fn
                            elif kind == "op":
                                fn(e)
                            elif kind == "op_inc":
                                fn(e).then_inc(ph, 1)
                            elif kind == "dma":
                                fn(e).then_inc(phds if eng == "sync" else phdg, 16)
                    return body
                getattr(block, eng)(make(lst, eng))
        c3.__exit__(None, None, None)
        c2.__exit__(None, None, None)
        c1.__exit__(None, None, None)


def build_program(stage):
    NGL = NG
    PASS2 = True
    nc = bass.Bass("TRN2", target_bir_lowering=False)

    def din(name, shape, dtype=F32):
        return nc.dram_tensor(name, list(shape), dtype, kind="ExternalInput").ap()

    def dout(name, shape, dtype=F32):
        return nc.dram_tensor(name, list(shape), dtype, kind="ExternalOutput").ap()

    x_own = din("x_own", [128, NTO * D])
    x_part = din("x_part", [128, NTO * D])
    wqkvT = din("wqkvT", [128, 8 * 3 * D])
    woutT = din("woutT", [128, 8 * D])
    bout_in = din("bout", [1, D])
    cn_w = [din(f"cn_w{i}", s) for i, s in enumerate(
        [(128, 8 * D), (128, 8 * 512), (128, 4 * 256), (128, 2 * 128)], 1)]
    cn_b = [din(f"cn_b{i}", (128, s)) for i, s in enumerate([8, 4, 2, 1], 1)]
    gc_w = [din(f"gc_w{i}", s) for i, s in enumerate(
        [(128, 8 * 512), (128, 4 * 256), (128, 2 * 128)], 1)]
    gc_b = [din(f"gc_b{i}", (128, s)) for i, s in enumerate([4, 2, 1], 1)]
    ar_in = din("ar", [1, 1])
    wq_in = din("wq", [128, 8 * D])
    wk_in = din("wk", [128, 8 * D])
    wv_in = din("wv", [128, 8 * D])
    wo_in = din("wo", [128, 8 * D])
    bq_in = din("bq", [128, 8])
    bk_in = din("bk", [128, 8])
    bv_in = din("bvrow", [1, D])
    bo_in = din("borow", [1, D])
    w1e_in = [din(f"w1e{e}", [128, 8 * 4096]) for e in range(8)]
    w2e_in = [din(f"w2e{e}", [128, 16 * D]) for e in range(8)]
    b1g_in = din("b1g", [128, 128])
    b1l_in = din("b1l", [128, 128])
    b2mat_in = din("b2mat", [8, D])
    gatew_in = din("gatew", [128, 64])
    gbrow_in = din("gbrow", [1, 8])
    nsc_in = din("nsc", [128, 8])
    out_own = dout("out_own", [128, NTO * D])
    xsp1 = nc.dram_tensor("xsp1", [128, NTO * D], F32, kind="Internal").ap()
    xsp2 = nc.dram_tensor("xsp2", [128, NTO * D], F32, kind="Internal").ap()

    arena = AutoArena(nc)
    ARENA_SZ = 212480
    arena.size = ARENA_SZ
    arena.allocate()
    placed = {}

    def place(name, shape, dtype, off):
        nbytes = int(np.prod(shape[1:])) * mybir.dt.size(dtype)
        assert off + nbytes <= ARENA_SZ, (name, off, nbytes)
        hnd = nc.virtual_sbuf_tensor(name, list(shape), dtype)
        nc._tensor(hnd.name, hnd.shape, hnd.dtype, type="SB")
        mloc = nc.lookup_mloc(hnd)
        mloc.allocated = True
        mloc.addr = arena.offset + off
        placed[name] = (off, nbytes)
        return hnd.ap()

    R0, R1, R2, R3, R4, ST = 0, 65536, 98304, 163840, 188416, 206848
    X = place("X", [128, NTO * D], F32, R0)
    YS = place("YS", [128, NT * D], BF16, R0)
    XP = place("XP", [128, NTO * D], BF16, R1)
    WQKV = place("WQKV", [128, 8 * 3 * D], BF16, R2)
    WOUT = place("WOUT", [128, 8 * D], BF16, R2 + 49152)
    WCN1 = place("WCN1", [128, 8 * D], BF16, R2)
    WCN2 = place("WCN2", [128, 8 * 512], BF16, R2 + 16384)
    WCN3 = place("WCN3", [128, 4 * 256], BF16, R2 + 24576)
    WCN4 = place("WCN4", [128, 2 * 128], BF16, R2 + 26624)
    WGC1 = place("WGC1", [128, 8 * 512], BF16, R2 + 27136)
    WGC2 = place("WGC2", [128, 4 * 256], BF16, R2 + 35328)
    WGC3 = place("WGC3", [128, 2 * 128], BF16, R2 + 37376)
    KT = place("KT", [128, 4 * D], BF16, R3)
    VT = place("VT", [128, 4 * D], BF16, R3 + 8192)
    QT = place("QT", [128, 8 * 512], BF16, R3)
    AOT = place("AOT", [128, 8 * 512], BF16, R3 + 8192)
    OTM = place("OTM", [128, 4 * D], BF16, R3 + 16384)
    H1T = place("H1T", [128, 8 * 512], BF16, R3)
    H2T = place("H2T", [128, 4 * 512], BF16, R3 + 8192)
    H3T = place("H3T", [128, 2 * 512], BF16, R3 + 12288)
    YTT = place("YTT", [128, 8 * 512], BF16, R3 + 16384)
    XN = place("XN", [128, 4 * D], BF16, R4)
    XSTG = place("XSTG", [128, D], F32, R3)
    XSTG2 = place("XSTG2", [128, D], F32, R3 + 4096)
    TROW = place("TROW", [128, 512], BF16, R3 + 12288 + 2048)
    XT = place("XT", [128, 8 * 512], BF16, R4 + 8192)
    EW = place("EW", [128, 1024], BF16, R4 + 16384)
    IDENT = place("IDENT", [128, 128], BF16, ST)
    ONES = place("ONES", [128, 128], BF16, ST + 256)
    BOUT = place("BOUT", [128, D], BF16, ST + 512)
    SM = place("SM", [128, 64], F32, ST + 2560)
    ZT = place("ZT", [128, 64], F32, ST + 2816)
    CNB = place("CNB", [128, 32], F32, ST + 3072)
    SC2 = place("SC2", [128, 56], F32, ST + 3200)
    KVS = place("KVS", [128, 1024], BF16, ST + 3456)
    XT3 = place("XT3", [128, NT * D], BF16, R0)
    X3 = place("X3", [128, NTO * D], F32, R0)
    OACC = place("OACC", [128, 8 * NOWN], BF16, R1)
    WQ1 = place("WQ1", [128, 8 * D], BF16, R2)
    WK1 = place("WK1", [128, 8 * D], BF16, R2 + 16384)
    WV1 = place("WV1", [128, 8 * D], BF16, R2 + 32768)
    WO1 = place("WO1", [128, 8 * D], BF16, R2)
    SPAIR = place("SPAIR", [128, 2 * NTOK], BF16, R2 + 49152)
    KTP = place("KTP", [128, NTOK], BF16, R3)
    KTM = place("KTM", [128, NT * 128], BF16, R3 + 8192)
    VTM = place("VTM", [128, NT * 128], BF16, R3 + 16384)
    QTP = place("QTP", [128, NOWN], BF16, R4)
    YSEL = place("YSEL", [128, NTOK], BF16, R4 + 8192)
    SST = place("SST", [128, 64], F32, ST + 2816)
    DIAG = place("DIAG", [128, 128], BF16, ST + 3456)
    KKZ = place("KKZ", [128, 128], BF16, ST + 3712)
    QTMS = place("QTMS", [128, 128], BF16, ST + 3968)
    IOTA16 = place("IOTA16", [128, 16], F32, ST + 4224)
    KSZ = place("KSZ", [128, 2], BF16, ST + 4288)
    M16 = place("M16", [128, 16], BF16, ST + 4320)
    EX32 = place("EX32", [128, 32], F32, ST + 4352)
    BQ1 = place("BQ1", [128, 8], F32, ST + 4480)
    BK1 = place("BK1", [128, 8], F32, ST + 4512)
    TT2 = place("TT2", [128, 8 * NOWN], BF16, R1)
    W1E = place("W1E", [128, 8 * 4096], BF16, R2)
    W2E = place("W2E", [128, 16 * D], BF16, R3)
    SWIF = place("SWIF", [128, 16 * 256], BF16, R4 + 8192)
    GATEW = place("GATEW", [128, 64], BF16, ST + 3456)
    B1G = place("B1G", [128, 128], F32, ST + 2560)
    B1L = place("B1L", [128, 128], F32, ST + 5056)
    MEVAC = place("MEVAC", [128, 2048], F32, R4 + 8192)
    HLF = place("HLF", [128, 512], F32, R4 + 16384)
    WTOK = place("WTOK", [128, 128], F32, ST + 3712)
    NSC = place("NSC", [128, 8], F32, ST + 4224)
    W8TG = place("W8TG", [8, 256], BF16, ST + 4544)
    WTB = place("WTB", [128, 16], BF16, ST + 4288)
    MSK = place("MSK", [128, 12], U8, ST + 4544)
    MRV = place("MRV", [128, 8], F32, ST + 4480)
    M16B = place("M16B", [128, 16], BF16, ST + 3200)
    KSUM = place("KSUM", [128, 8], BF16, ST + 5504)
    KSUMF = place("KSUMF", [128, 8], F32, ST + 5536)
    KSUMZ = place("KSUMZ", [128, 16], BF16, ST + 5568)
    ROWB = place("ROWB", [1, 1024], BF16, R4 + 16384)
    CONSTS = place("CONSTS", [128, 8], F32, ST + 3424)

    ctxs = []

    def psum_t(name, shape, dtype=F32):
        cm = nc.psum_tensor(name, list(shape), dtype)
        t = cm.__enter__()
        ctxs.append(cm)
        return t

    PS = [psum_t(f"PS{i}", (128, 512)) for i in range(4)]
    PSB = [psum_t(f"PSB{i}", (128, 1024), BF16) for i in range(4)]

    b = B(nc)

    for t in range(NTO):
        b.on("sync", lambda e, t=t: e.dma_start(out=X[:, ts(t, D)], in_=x_own[:, ts(t, D)]), dma=True)
        b.on("gpsimd", lambda e, t=t: e.dma_start(out=XP[:, ts(t, D)], in_=x_part[:, ts(t, D)]), dma=True)
    b.on("gpsimd", lambda e: e.dma_start(out=WQKV[:], in_=wqkvT[:]), dma=True)
    b.on("gpsimd", lambda e: e.dma_start(out=WOUT[:], in_=woutT[:]), dma=True)
    b.on("gpsimd", lambda e: e.dma_start(out=CNB[:, 0:8], in_=cn_b[0][:]), dma=True)
    b.on("gpsimd", lambda e: e.dma_start(out=CNB[:, 8:12], in_=cn_b[1][:]), dma=True)
    b.on("gpsimd", lambda e: e.dma_start(out=CNB[:, 12:14], in_=cn_b[2][:]), dma=True)
    b.on("gpsimd", lambda e: e.dma_start(out=CNB[:, 16:20], in_=gc_b[0][:]), dma=True)
    b.on("gpsimd", lambda e: e.dma_start(out=CNB[:, 20:22], in_=gc_b[1][:]), dma=True)
    b.on("gpsimd", lambda e: e.dma_start(out=SC2[0:1, 0:1], in_=cn_b[3][0:1, :]), dma=True)
    b.on("gpsimd", lambda e: e.dma_start(out=SC2[0:1, 1:2], in_=gc_b[2][0:1, :]), dma=True)
    b.on("gpsimd", lambda e: e.dma_start(out=SC2[0:1, 2:3], in_=ar_in[:]), dma=True)
    b.on("gpsimd", lambda e: e.dma_start(out=ROWB[:], in_=bout_in[:]), dma=True)
    b.barrier()
    b.on("gpsimd", lambda e: e.memset(ONES[:], 1.0))
    b.on("vector", lambda e: e.memset(CONSTS[:, 0:1], 0.0))
    b.on("vector", lambda e: e.memset(CONSTS[:, 1:2], 1e-5))
    b.on("vector", lambda e: e.memset(CONSTS[:, 2:3], 1e-6))
    b.on("vector", lambda e: e.memset(CONSTS[:, 3:4], 1.0))
    b.on("vector", lambda e: e.memset(KSUMF[:], 0.0))
    b.barrier()
    b.on("gpsimd", lambda e: e.affine_select(
        IDENT[:], ONES[:], [[1, 128]], A.is_equal, 0.0, base=0, channel_multiplier=-1))
    for g in range(2):
        b.on("tensor", lambda e, g=g: e.matmul(
            PS[g][:], ONES[0:1, :], ROWB[0:1, ts(g, 512)], start=True, stop=True))
    b.barrier()
    for g in range(2):
        b.on("vector", lambda e, g=g: e.tensor_copy(BOUT[:, ts(g, 512)], PS[g][:]))
    b.barrier()

    def x_tile(t):
        return X[:, ts(t, D)] if t < NTO else XP[:, ts(t - NTO, D)]

    def emit_ln_group(g, dst, src_tile=None):
        if src_tile is None:
            src_tile = x_tile
        for i in range(4):
            t = 4 * g + i
            b.on("vector", lambda e, i=i, t=t: e.bn_stats(SM[:, 16 * i:16 * i + 6], src_tile(t)[:, 0:512]))
            b.on("vector", lambda e, i=i, t=t: e.bn_stats(SM[:, 16 * i + 6:16 * i + 12], src_tile(t)[:, 512:1024]))
        b.barrier()
        for i in range(4):
            b.on("vector", lambda e, i=i: e.bn_aggr(SM[:, 16 * i + 12:16 * i + 14], SM[:, 16 * i:16 * i + 12]))
        b.barrier()
        varap = SM[:, 0:64].rearrange("p (i k) -> p i k", k=16)[:, :, 13:14]
        b.on("scalar", lambda e: e.activation(
            SC2[:, 32:36].rearrange("p (i k) -> p i k", k=1), varap, AF.Sqrt, bias=CONSTS[:, 1:2], scale=1.0))
        b.barrier()
        b.on("vector", lambda e: e.reciprocal(SC2[:, 36:40], SC2[:, 32:36]))
        b.barrier()
        for i in range(4):
            t = 4 * g + i
            b.on("vector", lambda e, i=i, t=t: e.tensor_scalar(
                dst[:, ts(i, D)], src_tile(t), SM[:, 16 * i + 12:16 * i + 13], SC2[:, 36 + i:37 + i],
                A.subtract, A.mult))
        b.barrier()

    def emit_transpose_group(src, dst):
        for i in range(4):
            for c in range(8):
                b.on("tensor", lambda e, i=i, c=c: e.matmul(
                    PSB[i][:, ts(c)], src[:, i * D + 128 * c:i * D + 128 * (c + 1)],
                    IDENT[:], is_transpose=True, start=(c == 0), stop=(c == 7)))
        b.barrier()
        for i in range(4):
            dst_ap = dst.rearrange("p (c w) -> p c w", c=8)[:, :, i * 128:(i + 1) * 128]
            b.on("vector", lambda e, i=i, dst_ap=dst_ap: e.tensor_copy(
                dst_ap, PSB[i][:].rearrange("p (s w) -> p s w", s=8)))
        b.barrier()

    def emit_elu1_evac(ps_ap, dst_ap, n):
        b.on("vector", lambda e: e.tensor_scalar(EW[:, 0:n], ps_ap, 0.0, None, A.min))
        b.on("vector", lambda e: e.tensor_scalar(EW[:, 512:512 + n], ps_ap, 0.0, None, A.max))
        b.barrier()
        b.on("scalar", lambda e: e.activation(XN[:, 0:n], EW[:, 0:n], AF.Exp))
        b.barrier()
        b.on("vector", lambda e: e.tensor_tensor(dst_ap, XN[:, 0:n], EW[:, 512:512 + n], A.add))
        b.barrier()

    ps_first = {2: True, 3: True}
    for g in range(NGL):
        emit_ln_group(g, XN)
        emit_transpose_group(XN, XT)
        for i in range(4):
            for nsl in range(2):
                for half, base in enumerate([D, 2 * D]):
                    for c in range(8):
                        b.on("tensor", lambda e, i=i, c=c, base=base, half=half, nsl=nsl: e.matmul(
                            PS[half][:],
                            XT[:, c * 512 + i * 128:c * 512 + (i + 1) * 128],
                            WQKV[:, c * 3 * D + base + nsl * 512:c * 3 * D + base + (nsl + 1) * 512],
                            start=(c == 0), stop=(c == 7)))
                b.barrier()
                emit_elu1_evac(PS[0][:], KT[:, i * D + nsl * 512:i * D + (nsl + 1) * 512], 512)
                b.on("vector", lambda e, nsl=nsl, i=i: e.tensor_copy(
                    VT[:, i * D + nsl * 512:i * D + (nsl + 1) * 512], PS[1][:]))
                b.barrier()
        for i in range(4):
            first = (g == 0 and i == 0)
            last = (g == NGL - 1 and i == 3)
            for h in range(H):
                bank = 2 + h // 8
                col0 = 64 * (h % 8)
                hp = h & ~1
                b.on("tensor", lambda e, h=h, hp=hp, i=i, bank=bank, col0=col0,
                     st=(first and (h % 8) == 0), sp=(last and (h % 8) == 7): e.matmul(
                    PS[bank][:, col0:col0 + 64],
                    KT[:, i * D + 64 * hp:i * D + 64 * (hp + 2)],
                    VT[:, i * D + 64 * h:i * D + 64 * (h + 1)],
                    start=st, stop=sp))
            for c in range(8):
                b.on("tensor", lambda e, i=i, c=c: e.matmul(
                    PS[0][:, 256 + c:257 + c],
                    KT[:, i * D + 128 * c:i * D + 128 * (c + 1)],
                    ONES[:, 0:1],
                    start=(i == 0 and c == 0), stop=(i == 3 and c == 7)))
        b.barrier()
        b.on("vector", lambda e: e.tensor_tensor(KSUMF[:], KSUMF[:], PS[0][:, 256:264], A.add))
        b.barrier()
    b.on("vector", lambda e: e.memset(KVS[:], 0.0))
    b.on("vector", lambda e: e.memset(KSUMZ[:], 0.0))
    b.barrier()
    for h in range(H):
        bank, col0 = 2 + h // 8, 64 * (h % 8)
        p0 = 64 * (h % 2)
        base = 512 * (h // 8) + 64 * (h % 8)
        b.on("vector", lambda e, bank=bank, col0=col0, p0=p0, base=base: e.tensor_copy(
            KVS[p0:p0 + 64, base:base + 64], PS[bank][p0:p0 + 64, col0:col0 + 64]))
        b.on("vector", lambda e, h=h, p0=p0: e.tensor_copy(
            KSUMZ[p0:p0 + 64, h:h + 1], KSUMF[p0:p0 + 64, h // 2:h // 2 + 1]))
    b.barrier()

    def kvs_ap(h):
        base = 512 * (h // 8) + 64 * (h % 8)
        return KVS[:, base:base + 64]

    def ksum_ap(h):
        return KSUMZ[:, h:h + 1]

    P2SEC = 99
    for g in range(NGL if PASS2 else 0):
        emit_ln_group(g, XN)
        emit_transpose_group(XN, XT)
        if P2SEC < 1:
            continue
        for ckg in range(4):
            for ck in range(2 * ckg, 2 * ckg + 2):
                bank = ck % 2
                for c in range(8):
                    b.on("tensor", lambda e, ck=ck, c=c, bank=bank: e.matmul(
                        PS[bank][:],
                        WQKV[:, c * 3 * D + ck * 128:c * 3 * D + (ck + 1) * 128],
                        XT[:, c * 512:(c + 1) * 512],
                        start=(c == 0), stop=(c == 7)))
            b.barrier()
            for ck in range(2 * ckg, 2 * ckg + 2):
                emit_elu1_evac(PS[ck % 2][:], QT[:, ck * 512:(ck + 1) * 512], 512)
            b.barrier()
        if P2SEC < 2:
            continue
        for i in range(4):
            for h in range(H):
                ck = h // 2
                b.on("tensor", lambda e, i=i, h=h, ck=ck: e.matmul(
                    PS[2][:, i * 16 + h:i * 16 + h + 1],
                    QT[:, ck * 512 + i * 128:ck * 512 + (i + 1) * 128],
                    ksum_ap(h), start=(i == 0 and h == 0), stop=(i == 3 and h == H - 1)))
        b.barrier()
        b.on("vector", lambda e: e.tensor_scalar(ZT[:, 0:64], PS[2][:, 0:64], 1e-6, None, A.add))
        b.barrier()
        b.on("vector", lambda e: e.reciprocal(ZT[:, 0:64], ZT[:, 0:64]))
        b.barrier()
        if P2SEC < 3:
            continue
        for i in range(4):
            for h in range(H):
                ck = h // 2
                bank, col0 = h // 8, 64 * (h % 8)
                b.on("tensor", lambda e, i=i, h=h, ck=ck, bank=bank, col0=col0: e.matmul(
                    PS[bank][:, col0:col0 + 64],
                    QT[:, ck * 512 + i * 128:ck * 512 + (i + 1) * 128],
                    kvs_ap(h), start=(h % 8 == 0), stop=(h % 8 == 7)))
            b.barrier()
            for h in range(H):
                bank, col0 = h // 8, 64 * (h % 8)
                b.on("vector", lambda e, i=i, h=h, bank=bank, col0=col0: e.tensor_scalar(
                    OTM[:, i * D + 64 * h:i * D + 64 * (h + 1)],
                    PS[bank][:, col0:col0 + 64],
                    ZT[:, i * 16 + h:i * 16 + h + 1], None, A.mult))
            b.barrier()
        if P2SEC < 4:
            continue
        emit_transpose_group(OTM, AOT)
        if P2SEC < 5:
            continue
        for i in range(4):
            t = 4 * g + i
            for nsl in range(2):
                for c in range(8):
                    b.on("tensor", lambda e, i=i, nsl=nsl, c=c: e.matmul(
                        PS[nsl][:],
                        AOT[:, c * 512 + i * 128:c * 512 + (i + 1) * 128],
                        WOUT[:, c * D + nsl * 512:c * D + (nsl + 1) * 512],
                        start=(c == 0), stop=(c == 7)))
            b.barrier()
            for nsl in range(2):
                b.on("vector", lambda e, nsl=nsl: e.tensor_tensor(
                    EW[:, nsl * 512:(nsl + 1) * 512], PS[nsl][:], BOUT[:, ts(nsl, 512)], A.add))
            b.barrier()
            dst = x_tile(t)
            b.on("vector", lambda e, dst=dst: e.tensor_tensor(dst, dst, EW[:, 0:1024], A.add))
            b.barrier()

    if stage == "l0attn":
        b.on("sync", lambda e: e.dma_start(out=out_own[:], in_=X[:]), dma=True)
        b.barrier()
        b.build_block()
        for c in reversed(ctxs):
            c.__exit__(None, None, None)
        arena.free()
        return nc

    for w, t in [(cn_w[0], WCN1), (cn_w[1], WCN2), (cn_w[2], WCN3), (cn_w[3], WCN4),
                 (gc_w[0], WGC1), (gc_w[1], WGC2), (gc_w[2], WGC3)]:
        b.on("gpsimd", lambda e, w=w, t=t: e.dma_start(out=t[:], in_=w[:]), dma=True)
    for t in range(NTO):
        b.on("sync", lambda e, t=t: e.dma_start(out=xsp1[:, ts(t, D)], in_=X[:, ts(t, D)]), dma=True)
    b.barrier()
    b.on("vector", lambda e: e.tensor_copy(EW[0:1, 0:3], SC2[0:1, 0:3]))
    b.barrier()
    b.on("tensor", lambda e: e.matmul(PS[2][:, 0:3], ONES[0:1, :], EW[0:1, 0:3], start=True, stop=True))
    b.barrier()
    b.on("vector", lambda e: e.tensor_copy(SC2[:, 4:7], PS[2][:, 0:3]))
    b.barrier()
    b.on("vector", lambda e: e.tensor_scalar(SC2[:, 52:53], SC2[:, 6:7], 2.0, None, A.mult))
    b.barrier()
    def ys_tile(t):
        return YS[:, t * D:(t + 1) * D]
    for g in range(NG):
        emit_ln_group(g, XN)
        for i in range(4):
            t = 4 * g + i
            b.on("vector", lambda e, i=i, t=t: e.tensor_copy(ys_tile(t), XN[:, ts(i, D)]))
        b.barrier()
    for t in range(NT):
        for c in range(8):
            b.on("tensor", lambda e, t=t, c=c: e.matmul(
                PS[2][:, 8 + c:9 + c], YS[:, t * D + 128 * c:t * D + 128 * (c + 1)],
                ONES[:, 0:1], start=(t == 0 and c == 0), stop=(t == NT - 1 and c == 7)))
    b.barrier()
    b.on("vector", lambda e: e.tensor_scalar(SC2[:, 8:16], PS[2][:, 8:16], 1.0 / NTOK, None, A.mult))
    b.barrier()
    GCB = EW
    b.on("vector", lambda e: e.tensor_copy(EW[:, 16:24], SC2[:, 8:16]))
    b.barrier()
    for co in range(4):
        for ci in range(8):
            b.on("tensor", lambda e, co=co, ci=ci: e.matmul(
                PS[2][:, 16 + co:17 + co], WGC1[:, ci * 512 + co * 128:ci * 512 + co * 128 + 128],
                EW[:, 16 + ci:17 + ci], start=(ci == 0), stop=(ci == 7)))
    b.barrier()
    for co in range(4):
        b.on("scalar", lambda e, co=co: e.activation(
            EW[:, 24 + co:25 + co], PS[2][:, 16 + co:17 + co], AF.Gelu,
            bias=CNB[:, 16 + co:17 + co], scale=1.0))
    b.barrier()
    for co in range(2):
        for ci in range(4):
            b.on("tensor", lambda e, co=co, ci=ci: e.matmul(
                PS[2][:, 20 + co:21 + co], WGC2[:, ci * 256 + co * 128:ci * 256 + co * 128 + 128],
                EW[:, 24 + ci:25 + ci], start=(ci == 0), stop=(ci == 3)))
    b.barrier()
    for co in range(2):
        b.on("scalar", lambda e, co=co: e.activation(
            EW[:, 28 + co:29 + co], PS[2][:, 20 + co:21 + co], AF.Gelu,
            bias=CNB[:, 20 + co:21 + co], scale=1.0))
    b.barrier()
    for ci in range(2):
        b.on("tensor", lambda e, ci=ci: e.matmul(
            PS[2][:, 22:23], WGC3[:, ci * 128:ci * 128 + 128],
            EW[:, 28 + ci:29 + ci], start=(ci == 0), stop=(ci == 1)))
    b.barrier()
    b.on("scalar", lambda e: e.activation(SC2[:, 33:34], PS[2][:, 22:23], AF.Sigmoid,
                                          bias=SC2[:, 5:6], scale=1.0))
    b.barrier()
    b.on("vector", lambda e: e.tensor_copy(EW[0:1, 30:31], SC2[0:1, 33:34]))
    b.barrier()
    b.on("tensor", lambda e: e.matmul(PS[2][:, 23:24], ONES[0:1, :], EW[0:1, 30:31], start=True, stop=True))
    b.barrier()
    b.on("vector", lambda e: e.tensor_scalar(SC2[:, 35:36], PS[2][:, 23:24], 0.7, None, A.mult))
    b.barrier()

    NCYC = 3
    for cyc in range(NCYC):
        for g in range(NG):
            def y_src(t):
                return ys_tile(t)
            for i in range(4):
                for c in range(8):
                    b.on("tensor", lambda e, g=g, i=i, c=c: e.matmul(
                        PSB[i][:, ts(c)], ys_tile(4 * g + i)[:, 128 * c:128 * (c + 1)],
                        IDENT[:], is_transpose=True, start=(c == 0), stop=(c == 7)))
            b.barrier()
            for i in range(4):
                dst_ap = YTT.rearrange("p (c w) -> p c w", c=8)[:, :, i * 128:(i + 1) * 128]
                b.on("vector", lambda e, i=i, dst_ap=dst_ap: e.tensor_copy(
                    dst_ap, PSB[i][:].rearrange("p (s w) -> p s w", s=8)))
            b.barrier()
            for cg in range(4):
                for co in (2 * cg, 2 * cg + 1):
                    for ci in range(8):
                        b.on("tensor", lambda e, co=co, ci=ci: e.matmul(
                            PS[co % 2][:], WCN1[:, ci * D + co * 128:ci * D + (co + 1) * 128],
                            YTT[:, ci * 512:(ci + 1) * 512], start=(ci == 0), stop=(ci == 7)))
                b.barrier()
                for co in (2 * cg, 2 * cg + 1):
                    b.on("scalar", lambda e, co=co: e.activation(
                        H1T[:, co * 512:(co + 1) * 512], PS[co % 2][:], AF.Gelu,
                        bias=CNB[:, co:co + 1], scale=1.0))
                b.barrier()
            for cg in range(2):
                for co in (2 * cg, 2 * cg + 1):
                    for ci in range(8):
                        b.on("tensor", lambda e, co=co, ci=ci: e.matmul(
                            PS[co % 2][:], WCN2[:, ci * 512 + co * 128:ci * 512 + co * 128 + 128],
                            H1T[:, ci * 512:(ci + 1) * 512], start=(ci == 0), stop=(ci == 7)))
                b.barrier()
                for co in (2 * cg, 2 * cg + 1):
                    b.on("scalar", lambda e, co=co: e.activation(
                        H2T[:, co * 512:(co + 1) * 512], PS[co % 2][:], AF.Gelu,
                        bias=CNB[:, 8 + co:9 + co], scale=1.0))
                b.barrier()
            for co in range(2):
                for ci in range(4):
                    b.on("tensor", lambda e, co=co, ci=ci: e.matmul(
                        PS[co][:], WCN3[:, ci * 256 + co * 128:ci * 256 + co * 128 + 128],
                        H2T[:, ci * 512:(ci + 1) * 512], start=(ci == 0), stop=(ci == 3)))
            b.barrier()
            for co in range(2):
                b.on("scalar", lambda e, co=co: e.activation(
                    H3T[:, co * 512:(co + 1) * 512], PS[co][:], AF.Gelu,
                    bias=CNB[:, 12 + co:13 + co], scale=1.0))
            b.barrier()
            for ci in range(2):
                b.on("tensor", lambda e, ci=ci: e.matmul(
                    PS[2][:], WCN4[:, ci * 128:ci * 128 + 128],
                    H3T[:, ci * 512:(ci + 1) * 512], start=(ci == 0), stop=(ci == 1)))
            b.barrier()
            b.on("vector", lambda e: e.tensor_copy(TROW[0:1, :], PS[2][0:1, :]))
            b.barrier()
            for i in range(4):
                b.on("tensor", lambda e, i=i: e.matmul(
                    PSB[0][:, ts(i)], TROW[:, i * 128:(i + 1) * 128],
                    IDENT[:], is_transpose=True, start=(i == 0), stop=(i == 3)))
            b.barrier()
            for i in range(4):
                b.on("vector", lambda e, i=i: e.tensor_copy(SC2[:, 36 + i:37 + i], PSB[0][:, i * 128:i * 128 + 1]))
            b.barrier()
            b.on("scalar", lambda e: e.activation(SC2[:, 40:44], SC2[:, 36:40], AF.Sigmoid,
                                                  bias=SC2[:, 4:5], scale=1.0))
            b.barrier()
            b.on("vector", lambda e: e.tensor_scalar(SC2[:, 44:48], SC2[:, 40:44], 0.3, None, A.mult))
            b.barrier()
            b.on("vector", lambda e: e.tensor_scalar(SC2[:, 44:48], SC2[:, 44:48], SC2[:, 35:36], None, A.add))
            b.barrier()
            b.on("vector", lambda e: e.tensor_scalar(SC2[:, 44:48], SC2[:, 44:48], 0.5, None, A.subtract))
            b.barrier()
            b.on("vector", lambda e: e.tensor_scalar(SC2[:, 44:48], SC2[:, 44:48], SC2[:, 52:53], 0.1, A.mult, A.add))
            b.barrier()
            b.on("vector", lambda e: e.tensor_scalar(SC2[:, 48:52], SC2[:, 44:48], -1.0, None, A.mult))
            b.barrier()
            for i in range(4):
                t = 4 * g + i
                yt = ys_tile(t)
                cpos = SC2[:, 44 + i:45 + i]
                cneg = SC2[:, 48 + i:49 + i]
                b.on("vector", lambda e, yt=yt: e.tensor_copy(EW[:, 0:512], yt[:, 0:512]))
                b.barrier()
                b.on("vector", lambda e, yt=yt, cpos=cpos: e.scalar_tensor_tensor(
                    yt[:, 0:256], yt[:, 512:768], cpos, yt[:, 0:256], A.mult, A.add))
                b.on("vector", lambda e, yt=yt, cneg=cneg: e.scalar_tensor_tensor(
                    yt[:, 256:512], yt[:, 768:1024], cneg, yt[:, 256:512], A.mult, A.add))
                b.barrier()
                b.on("vector", lambda e, yt=yt, cneg=cneg: e.scalar_tensor_tensor(
                    yt[:, 512:768], EW[:, 0:256], cneg, yt[:, 512:768], A.mult, A.add))
                b.on("vector", lambda e, yt=yt, cpos=cpos: e.scalar_tensor_tensor(
                    yt[:, 768:1024], EW[:, 256:512], cpos, yt[:, 768:1024], A.mult, A.add))
                b.barrier()

    def emit_ln_tile(src_ap, dst_ap):
        b.on("vector", lambda e: e.bn_stats(SM[:, 0:6], src_ap[:, 0:512]))
        b.on("vector", lambda e: e.bn_stats(SM[:, 6:12], src_ap[:, 512:1024]))
        b.barrier()
        b.on("vector", lambda e: e.bn_aggr(SM[:, 12:14], SM[:, 0:12]))
        b.barrier()
        b.on("scalar", lambda e: e.activation(SM[:, 14:15], SM[:, 13:14], AF.Sqrt,
                                              bias=CONSTS[:, 1:2], scale=1.0))
        b.barrier()
        b.on("vector", lambda e: e.reciprocal(SM[:, 15:16], SM[:, 14:15]))
        b.barrier()
        b.on("vector", lambda e: e.tensor_scalar(
            dst_ap, src_ap, SM[:, 12:13], SM[:, 15:16], A.subtract, A.mult))
        b.barrier()

    def xt3_slot(t):
        return XT3[:, t * D:(t + 1) * D]

    for t in range(NT):
        if t < NTO:
            b.on("sync", lambda e, t=t: e.dma_start(out=XSTG[:], in_=xsp1[:, ts(t, D)]), dma=True)
            b.barrier()
            b.on("vector", lambda e, t=t: e.tensor_tensor(XSTG2[:], XSTG[:], ys_tile(t), A.add))
            b.barrier()
            if stage == "l0":
                b.on("sync", lambda e, t=t: e.dma_start(out=out_own[:, ts(t, D)], in_=XSTG2[:]), dma=True)
                b.barrier()
                continue
            b.on("sync", lambda e, t=t: e.dma_start(out=xsp2[:, ts(t, D)], in_=XSTG2[:]), dma=True)
            src_ap = XSTG2
        else:
            b.on("vector", lambda e, t=t: e.tensor_tensor(
                XP[:, ts(t - NTO, D)], XP[:, ts(t - NTO, D)], ys_tile(t), A.add))
            b.barrier()
            if stage == "l0":
                continue
            src_ap = XP[:, ts(t - NTO, D)]
        emit_ln_tile(src_ap, XN[:, 0:D])
        for c in range(8):
            b.on("tensor", lambda e, c=c: e.matmul(
                PSB[0][:, ts(c)], XN[:, 128 * c:128 * (c + 1)],
                IDENT[:], is_transpose=True, start=(c == 0), stop=(c == 7)))
        b.barrier()
        b.on("vector", lambda e, t=t: e.tensor_copy(xt3_slot(t), PSB[0][:, 0:1024]))
        b.barrier()

    if stage == "l0":
        b.build_block()
        for c in reversed(ctxs):
            c.__exit__(None, None, None)
        arena.free()
        return nc

    Z90T = 1.2944
    TGT = 400.5
    WLO, WHI = 392.5, 408.5
    WLOA, WHIA = 2 * WLO - 4096.0, 2 * WHI - 4096.0
    ATGT = 2 * TGT - 4096.0
    NSEL = 6

    for w, t in [(wq_in, WQ1), (wk_in, WK1), (wv_in, WV1)]:
        b.on("gpsimd", lambda e, w=w, t=t: e.dma_start(out=t[:], in_=w[:]), dma=True)
    b.on("gpsimd", lambda e: e.dma_start(out=BQ1[:], in_=bq_in[:]), dma=True)
    b.on("gpsimd", lambda e: e.dma_start(out=BK1[:], in_=bk_in[:]), dma=True)
    b.on("gpsimd", lambda e: e.dma_start(out=BOUT[0:1, :], in_=bv_in[:]), dma=True)
    b.on("vector", lambda e: e.memset(OACC[:], 0.0))
    b.on("gpsimd", lambda e: e.iota(IOTA16[:], [[1, 16]], base=0, channel_multiplier=0, allow_small_or_imprecise_dtypes=True))
    b.barrier()
    b.on("vector", lambda e: e.tensor_scalar(BQ1[:], BQ1[:], 0.125, None, A.mult))
    b.barrier()
    for t in range(NT):
        for c in range(8):
            b.on("tensor", lambda e, t=t, c=c: e.matmul(
                PS[2][:, 8 + c:9 + c], XT3[:, t * D + 128 * c:t * D + 128 * (c + 1)],
                ONES[:, 0:1], start=(t == 0 and c == 0), stop=(t == NT - 1 and c == 7)))
    b.barrier()
    b.on("vector", lambda e: e.tensor_copy(EX32[:, 0:8], PS[2][:, 8:16]))
    b.barrier()
    XS3B = M16
    b.on("vector", lambda e: e.tensor_copy(M16[:, 0:8], EX32[:, 0:8]))
    b.barrier()

    NPAIRS = 8
    NQT = NTO
    def _pair(p):
            hA, hB = 2 * p, 2 * p + 1
            for kg in range(8):
                rhs_g = XT3.rearrange("p (t cw) -> p t cw", cw=D)
                for c in range(8):
                    rhs = rhs_g[:, 4 * kg:4 * kg + 4, c * 128:(c + 1) * 128]
                    b.on("tensor", lambda e, c=c, rhs=rhs, kg=kg: e.matmul(
                        PS[kg % 2][:], WK1[:, c * D + p * 128:c * D + p * 128 + 128],
                        rhs, start=(c == 0), stop=(c == 7)))
                b.barrier()
                b.on("scalar", lambda e, kg=kg: e.activation(
                    KTP[:, kg * 512:(kg + 1) * 512], PS[kg % 2][:], AF.Identity,
                    bias=BK1[:, p:p + 1], scale=1.0))
                b.barrier()
            for qg in range(4):
                rhs_g = XT3.rearrange("p (t cw) -> p t cw", cw=D)
                for c in range(8):
                    rhs = rhs_g[:, 4 * qg:4 * qg + 4, c * 128:(c + 1) * 128]
                    b.on("tensor", lambda e, c=c, rhs=rhs, qg=qg: e.matmul(
                        PS[qg % 2][:], WQ1[:, c * D + p * 128:c * D + p * 128 + 128],
                        rhs, start=(c == 0), stop=(c == 7)))
                b.barrier()
                b.on("scalar", lambda e, qg=qg: e.activation(
                    QTP[:, qg * 512:(qg + 1) * 512], PS[qg % 2][:], AF.Identity,
                    bias=BQ1[:, p:p + 1], scale=0.125))
                b.barrier()
            for t in range(NT):
                for c in range(8):
                    b.on("tensor", lambda e, t=t, c=c: e.matmul(
                        PS[t % 2][:, 0:128], XT3[:, t * D + 128 * c:t * D + 128 * (c + 1)],
                        WK1[:, c * D + p * 128:c * D + p * 128 + 128],
                        start=(c == 0), stop=(c == 7)))
                if t % 2 == 1:
                    b.barrier()
                    for tt in (t - 1, t):
                        b.on("vector", lambda e, tt=tt: e.tensor_copy(
                            KTM[:, tt * 128:(tt + 1) * 128], PS[tt % 2][:, 0:128]))
                    b.barrier()
            for c in range(8):
                b.on("tensor", lambda e, c=c: e.matmul(
                    PS[2][:, 0:1], WK1[:, c * D + p * 128:c * D + p * 128 + 128],
                    M16[:, c:c + 1], start=(c == 0), stop=(c == 7)))
            b.barrier()
            b.on("vector", lambda e: e.tensor_scalar(EX32[:, 16:17], BK1[:, p:p + 1], 4096.0, None, A.mult))
            b.barrier()
            b.on("vector", lambda e: e.tensor_tensor(EX32[:, 17:18], PS[2][:, 0:1], EX32[:, 16:17], A.add))
            b.barrier()
            b.on("vector", lambda e: e.memset(KSZ[:], 0.0))
            b.on("vector", lambda e: e.memset(M16[:, 8:10], 0.0))
            b.barrier()
            for hh in range(2):
                b.on("vector", lambda e, hh=hh: e.tensor_copy(
                    KSZ[64 * hh:64 * hh + 64, hh:hh + 1], EX32[64 * hh:64 * hh + 64, 17:18]))
                b.on("vector", lambda e, hh=hh: e.tensor_copy(
                    M16[64 * hh:64 * hh + 64, 8 + hh:9 + hh], BQ1[64 * hh:64 * hh + 64, p:p + 1])) if False else None
                b.on("vector", lambda e, hh=hh: e.tensor_copy(
                    M16[64 * hh:64 * hh + 64, 8 + hh:9 + hh], BK1[64 * hh:64 * hh + 64, p:p + 1]))
            b.barrier()
            b.on("vector", lambda e: e.memset(KKZ[:], 0.0))
            b.barrier()
            for hh in range(2):
                for t in range(NT):
                    b.on("tensor", lambda e, hh=hh, t=t: e.matmul(
                        PS[2 + hh][:, 0:64], KTM[:, t * 128:(t + 1) * 128],
                        KTM[:, t * 128 + 64 * hh:t * 128 + 64 * hh + 64],
                        start=(t == 0), stop=(t == NT - 1)))
            b.barrier()
            for hh in range(2):
                b.on("vector", lambda e, hh=hh: e.tensor_copy(
                    KKZ[64 * hh:64 * hh + 64, 64 * hh:64 * hh + 64],
                    PS[2 + hh][64 * hh:64 * hh + 64, 0:64]))
            b.barrier()
            for t in range(NT):
                for c in range(8):
                    b.on("tensor", lambda e, t=t, c=c: e.matmul(
                        PS[t % 2][:, 0:128], XT3[:, t * D + 128 * c:t * D + 128 * (c + 1)],
                        WV1[:, c * D + p * 128:c * D + p * 128 + 128],
                        start=(c == 0), stop=False))
                b.on("tensor", lambda e, t=t: e.matmul(
                    PS[t % 2][:, 0:128], ONES[0:1, :],
                    BOUT[0:1, p * 128:(p + 1) * 128], start=False, stop=True))
                if t % 2 == 1:
                    b.barrier()
                    for tt in (t - 1, t):
                        b.on("vector", lambda e, tt=tt: e.tensor_copy(
                            VTM[:, tt * 128:(tt + 1) * 128], PS[tt % 2][:, 0:128]))
                    b.barrier()

            def _qtile(p, qt):
                    for rnd in range(4):
                        for n in (2 * rnd, 2 * rnd + 1):
                            for hh in range(2):
                                bank = 2 * (n % 2) + hh
                                b.on("tensor", lambda e, n=n, hh=hh, bank=bank: e.matmul(
                                    PS[bank][:],
                                    QTP[64 * hh:64 * hh + 64, qt * 128:(qt + 1) * 128],
                                    KTP[64 * hh:64 * hh + 64, n * 512:(n + 1) * 512],
                                    start=True, stop=True,
                                    tile_position=(64 * hh, 0)))
                        b.barrier()
                        for n in (2 * rnd, 2 * rnd + 1):
                            for hh in range(2):
                                bank = 2 * (n % 2) + hh
                                b.on("vector", lambda e, n=n, hh=hh, bank=bank: e.tensor_copy(
                                    SPAIR[:, hh * NTOK + n * 512:hh * NTOK + (n + 1) * 512], PS[bank][:]))
                        b.barrier()
                    for hh in range(2):
                        b.on("tensor", lambda e, hh=hh: e.matmul(
                            PS[2][:, hh:hh + 1], QTP[:, qt * 128:(qt + 1) * 128],
                            KSZ[:, hh:hh + 1], start=(hh == 0), stop=False))
                    for hh in range(2):
                        b.on("tensor", lambda e, hh=hh: e.matmul(
                            PS[2][:, 2 + hh:3 + hh], QTP[:, qt * 128:(qt + 1) * 128],
                            M16[:, 8 + hh:9 + hh], start=False, stop=(hh == 1)))
                    for hh in range(2):
                        b.on("tensor", lambda e, hh=hh: e.matmul(
                            PS[3][:, 64 * hh:64 * hh + 64], QTP[:, qt * 128:(qt + 1) * 128],
                            KKZ[:, 64 * hh:64 * hh + 64], start=(hh == 0), stop=(hh == 1)))
                    b.on("tensor", lambda e: e.matmul(
                        PSB[0][:, 0:128], QTP[:, qt * 128:(qt + 1) * 128], IDENT[:],
                        is_transpose=True, start=True, stop=True))
                    b.barrier()
                    b.on("vector", lambda e: e.tensor_copy(QTMS[:], PSB[0][:, 0:128]))
                    b.on("vector", lambda e: e.tensor_scalar(SST[:, 0:2], PS[2][:, 0:2], 1.0 / NTOK, None, A.mult))
                    b.on("vector", lambda e: e.tensor_copy(SST[:, 32:34], PS[2][:, 2:4]))
                    b.barrier()
                    b.on("vector", lambda e: e.tensor_tensor(SST[:, 34:36], SST[:, 0:2], SST[:, 32:34], A.subtract))
                    b.barrier()
                    for hh in range(2):
                        b.on("vector", lambda e, hh=hh: e.tensor_tensor(
                            EW[:, 0:64], QTMS[:, 64 * hh:64 * hh + 64], PS[3][:, 64 * hh:64 * hh + 64], A.mult))
                        b.barrier()
                        b.on("vector", lambda e, hh=hh: e.reduce_sum(SST[:, 2 + hh:3 + hh], EW[:, 0:64], axis=AX.X))
                        b.barrier()
                    b.on("vector", lambda e: e.tensor_scalar(SST[:, 4:6], SST[:, 2:4], 1.0 / NTOK, None, A.mult))
                    b.barrier()
                    b.on("vector", lambda e: e.tensor_tensor(SST[:, 6:8], SST[:, 34:36], SST[:, 34:36], A.mult))
                    b.barrier()
                    b.on("vector", lambda e: e.tensor_tensor(SST[:, 4:6], SST[:, 4:6], SST[:, 6:8], A.subtract))
                    b.barrier()
                    b.on("vector", lambda e: e.tensor_scalar(SST[:, 4:6], SST[:, 4:6], 1e-6, None, A.max))
                    b.barrier()
                    b.on("scalar", lambda e: e.activation(SST[:, 6:8], SST[:, 4:6], AF.Sqrt,
                                                          bias=CONSTS[:, 2:3], scale=1.0))
                    b.barrier()
                    b.on("vector", lambda e: e.tensor_scalar(SST[:, 26:28], SST[:, 6:8], -3.34, None, A.mult))
                    b.on("vector", lambda e: e.tensor_scalar(SST[:, 18:20], SST[:, 6:8], 3.17, None, A.mult))
                    b.on("vector", lambda e: e.tensor_scalar(SST[:, 8:10], SST[:, 6:8], Z90T, None, A.mult))
                    b.barrier()
                    b.on("vector", lambda e: e.tensor_tensor(SST[:, 26:28], SST[:, 26:28], SST[:, 0:2], A.add))
                    b.on("vector", lambda e: e.tensor_tensor(SST[:, 18:20], SST[:, 18:20], SST[:, 0:2], A.add))
                    b.barrier()
                    b.on("vector", lambda e: e.tensor_tensor(SST[:, 8:10], SST[:, 8:10], SST[:, 0:2], A.add))
                    b.on("vector", lambda e: e.tensor_copy(SST[:, 14:16], SST[:, 26:28]))
                    b.barrier()
                    b.on("vector", lambda e: e.memset(SST[:, 16:18], 4096.0))
                    b.on("vector", lambda e: e.memset(SST[:, 20:22], -4096.0))
                    b.on("vector", lambda e: e.tensor_copy(SST[:, 22:24], SST[:, 8:10]))
                    b.on("vector", lambda e: e.memset(SST[:, 24:26], -3296.0))
                    b.on("vector", lambda e: e.tensor_scalar(SST[:, 10:12], SST[:, 8:10], -1.0, None, A.mult))
                    b.barrier()
                    def sel_part1(h, last):
                        acc = SST[:, 12 + h:13 + h]
                        def m(j):
                            return MSK[:, 6 * h + j:6 * h + j + 1]
                        def sc(j):
                            return SST[:, j + h:j + h + 1]
                        if last:
                            return [
                                lambda e: e.tensor_scalar(m(0), acc, WLOA, None, A.is_gt),
                                lambda e: e.tensor_scalar(m(1), acc, WHIA, None, A.is_lt),
                                lambda e: e.tensor_copy(SST[:, 50:51], SST[:, 36:37]),
                                lambda e: e.tensor_tensor(m(2), m(0), m(1), A.mult),
                                lambda e: e.tensor_copy(SST[:, 51:52], SST[:, 38:39]),
                                lambda e: e.copy_predicated(sc(22), m(2), sc(8)),
                                lambda e: e.copy_predicated(sc(24), m(2), acc),
                            ]
                        return [
                            lambda e: e.tensor_scalar(m(3), acc, ATGT, None, A.is_ge),
                            lambda e: e.tensor_scalar(m(4), acc, ATGT, None, A.is_lt),
                            lambda e: e.tensor_scalar(m(0), acc, WLOA, None, A.is_gt),
                            lambda e: e.tensor_scalar(m(1), acc, WHIA, None, A.is_lt),
                            lambda e: e.copy_predicated(sc(16), m(3), acc),
                            lambda e: e.copy_predicated(sc(20), m(4), acc),
                            lambda e: e.copy_predicated(sc(14), m(3), sc(8)),
                            lambda e: e.copy_predicated(sc(18), m(4), sc(8)),
                            lambda e: e.tensor_tensor(sc(38), sc(16), sc(20), A.subtract),
                            lambda e: e.tensor_tensor(m(2), m(0), m(1), A.mult),
                            lambda e: e.tensor_scalar(sc(36), sc(16), ATGT, None, A.subtract),
                            lambda e: e.reciprocal(sc(40), sc(38)),
                            lambda e: e.tensor_tensor(sc(44), sc(18), sc(14), A.subtract),
                            lambda e: e.tensor_tensor(sc(42), sc(36), sc(40), A.mult),
                            lambda e: e.copy_predicated(sc(22), m(2), sc(8)),
                            lambda e: e.tensor_scalar(sc(46), sc(42), 0.12, 0.88, A.max, A.min),
                            lambda e: e.copy_predicated(sc(24), m(2), acc),
                            lambda e: e.tensor_tensor(sc(48), sc(46), sc(44), A.mult),
                        ]

                    def sel_part2(h):
                        def sc(j):
                            return SST[:, j + h:j + h + 1]
                        return [
                            lambda e: e.tensor_tensor(sc(8), sc(48), sc(14), A.add),
                            lambda e: e.scalar_tensor_tensor(sc(10), sc(48), -1.0, sc(14), A.mult, A.subtract),
                        ]

                    def sel_sign(h):
                        return lambda e, h=h: e.activation(
                            YSEL[:], SPAIR[:, h * NTOK:(h + 1) * NTOK], AF.Sign,
                            bias=SST[:, 10 + h:11 + h], scale=1.0,
                            accum_out=SST[:, 12 + h:13 + h])

                    for s in range(2 * NSEL + 1):
                        h, it = s % 2, s // 2
                        if s > 0:
                            hp, itp = (s - 1) % 2, (s - 1) // 2
                            for f in sel_part1(hp, itp == NSEL - 1):
                                b.on("vector", f)
                            if itp != NSEL - 1:
                                b.on("vector", lambda e: e.tensor_copy(SST[:, 52:53], SST[:, 50:51]))
                                for f in sel_part2(hp):
                                    b.on("vector", f)
                        if s < 2 * NSEL:
                            b.on("scalar", sel_sign(h))
                        b.barrier()
                    def ext_band(hh):
                        sh = SPAIR[:, hh * NTOK:(hh + 1) * NTOK]
                        return [
                            lambda e, hh=hh, sh=sh: e.tensor_scalar(
                                KTM[:], sh, SST[:, 26 + hh:27 + hh], 0.0, A.subtract, A.max),
                            lambda e, hh=hh, sh=sh: e.tensor_scalar(
                                YSEL[:], sh, SST[:, 22 + hh:23 + hh], None, A.is_le),
                        ]

                    def ext_rank(hh):
                        m16 = M16 if hh == 0 else M16B
                        ce, cs, cv = (20, 21, 22) if hh == 0 else (28, 29, 30)
                        return [
                            [lambda e, hh=hh, ce=ce: e.tensor_scalar(
                                EX32[:, ce:ce + 1], SST[:, 24 + hh:25 + hh], -0.5, -1640.0, A.mult, A.add)],
                            [lambda e, ce=ce: e.tensor_scalar(
                                EX32[:, 0:16], IOTA16[:], EX32[:, ce:ce + 1], None, A.is_equal)],
                            [lambda e, m16=m16: e.tensor_tensor(EX32[:, 0:16], EX32[:, 0:16], m16[:], A.mult)],
                            [lambda e, cs=cs: e.reduce_sum(EX32[:, cs:cs + 1], EX32[:, 0:16], axis=AX.X)],
                            [lambda e, hh=hh, cs=cs, cv=cv: e.tensor_tensor(
                                EX32[:, cv:cv + 1], EX32[:, cs:cs + 1], SST[:, 26 + hh:27 + hh], A.add)],
                        ]

                    def ext_keep(hh):
                        sh = SPAIR[:, hh * NTOK:(hh + 1) * NTOK]
                        cv = 22 if hh == 0 else 30
                        return ([lambda e, sh=sh, cv=cv: e.tensor_scalar(
                                    YSEL[:], sh, EX32[:, cv:cv + 1], None, A.is_ge)],
                                [lambda e, sh=sh: e.tensor_tensor(KTM[:], YSEL[:], sh, A.mult)])

                    def ext_exp(hh):
                        sh = SPAIR[:, hh * NTOK:(hh + 1) * NTOK]
                        return lambda e, sh=sh, hh=hh: e.activation(
                            sh, KTM[:], AF.Exp, bias=CONSTS[:, 0:1], scale=1.0,
                            accum_out=EX32[:, 24 + hh:25 + hh])

                    def ext_recip(hh):
                        return lambda e, hh=hh: e.reciprocal(
                            EX32[:, 26 + hh:27 + hh], EX32[:, 24 + hh:25 + hh])

                    def ext_scale(hh):
                        sh = SPAIR[:, hh * NTOK:(hh + 1) * NTOK]
                        return lambda e, sh=sh, hh=hh: e.tensor_scalar(
                            sh, sh, EX32[:, 26 + hh:27 + hh], None, A.mult)

                    def ext_transposes(hh):
                        sh = SPAIR[:, hh * NTOK:(hh + 1) * NTOK]
                        return [lambda e, sh=sh, kc=kc: e.matmul(
                            PSB[kc // 8][:, ts(kc % 8)], sh[:, kc * 128:(kc + 1) * 128],
                            IDENT[:], is_transpose=True, start=(kc % 8 == 0), stop=(kc % 8 == 7))
                            for kc in range(32)]

                    def ext_evacs(hh):
                        sh = SPAIR[:, hh * NTOK:(hh + 1) * NTOK]
                        return [lambda e, sh=sh, bk_=bk_: e.tensor_copy(
                            sh[:, bk_ * 1024:(bk_ + 1) * 1024], PSB[bk_][:]) for bk_ in range(4)]

                    def ext_attnv(hh):
                        sh = SPAIR[:, hh * NTOK:(hh + 1) * NTOK]
                        return [lambda e, sh=sh, kc=kc, hh=hh: e.matmul(
                            PS[0][:, 128 * hh:128 * hh + 128],
                            VTM[:, kc * 128:(kc + 1) * 128],
                            sh[:, kc * 128:(kc + 1) * 128],
                            start=(kc == 0), stop=(kc == 31)) for kc in range(32)]

                    def ext_oacc(hh):
                        return lambda e, hh=hh: e.tensor_copy(
                            OACC[64 * hh:64 * hh + 64, p * NOWN + qt * 128:p * NOWN + (qt + 1) * 128],
                            PS[0][64 * hh:64 * hh + 64, 128 * hh:128 * hh + 128])

                    def vphase(ops):
                        for f in ops:
                            b.on("vector", f)
                        b.barrier()

                    vphase(ext_band(0))
                    vphase([lambda e: e.tensor_tensor(YSEL[:], YSEL[:], KTM[:], A.mult)])
                    vphase([lambda e: e.max(out=M16[:, 0:8], in_=YSEL[:])])
                    vphase([lambda e: e.match_replace(out=KTM[:], in_to_replace=M16[:, 0:8],
                                                      in_values=YSEL[:], imm_value=0.0)])
                    vphase([lambda e: e.max(out=M16[:, 8:16], in_=KTM[:])])
                    for ph in ext_rank(0):
                        vphase(ph)
                    k1, k2 = ext_keep(0)
                    vphase(k1)
                    vphase(k2)
                    b.on("scalar", ext_exp(0))
                    b.barrier()
                    vphase([ext_recip(0)])
                    vphase([ext_scale(0)] + ext_band(1))
                    for f in ext_transposes(0):
                        b.on("tensor", f)
                    vphase([lambda e: e.tensor_tensor(YSEL[:], YSEL[:], KTM[:], A.mult)])
                    vphase(ext_evacs(0) + [lambda e: e.max(out=M16B[:, 0:8], in_=YSEL[:])])
                    for f in ext_attnv(0):
                        b.on("tensor", f)
                    vphase([lambda e: e.match_replace(out=KTM[:], in_to_replace=M16B[:, 0:8],
                                                      in_values=YSEL[:], imm_value=0.0)])
                    vphase([ext_oacc(0),
                            lambda e: e.max(out=M16B[:, 8:16], in_=KTM[:])])
                    for ph in ext_rank(1):
                        vphase(ph)
                    k1, k2 = ext_keep(1)
                    vphase(k1)
                    vphase(k2)
                    b.on("scalar", ext_exp(1))
                    b.barrier()
                    vphase([ext_recip(1)])
                    vphase([ext_scale(1)])
                    for f in ext_transposes(1):
                        b.on("tensor", f)
                    b.barrier()
                    vphase(ext_evacs(1))
                    for f in ext_attnv(1):
                        b.on("tensor", f)
                    b.barrier()
                    vphase([ext_oacc(1)])

            for qt in range(NQT):
                _qtile(p, qt)
    for p in range(NPAIRS):
        _pair(p)
    b.on("gpsimd", lambda e: e.dma_start(out=WO1[:], in_=wo_in[:]), dma=True)
    b.on("gpsimd", lambda e: e.dma_start(out=EW[0:1, :], in_=bo_in[:]), dma=True)
    b.barrier()
    for g in range(2):
        b.on("tensor", lambda e, g=g: e.matmul(
            PS[g][:], ONES[0:1, :], EW[0:1, ts(g, 512)], start=True, stop=True))
    b.barrier()
    for g in range(2):
        b.on("vector", lambda e, g=g: e.tensor_copy(BOUT[:, ts(g, 512)], PS[g][:]))
    b.barrier()
    for t in range(NTO):
        b.on("sync", lambda e, t=t: e.dma_start(out=XSTG[:], in_=xsp2[:, ts(t, D)]), dma=True)
        for nsl in range(2):
            for c in range(8):
                b.on("tensor", lambda e, t=t, nsl=nsl, c=c: e.matmul(
                    PS[2 + nsl][:], OACC[:, c * NOWN + t * 128:c * NOWN + (t + 1) * 128],
                    WO1[:, c * D + nsl * 512:c * D + (nsl + 1) * 512],
                    start=(c == 0), stop=(c == 7)))
        b.barrier()
        for nsl in range(2):
            b.on("vector", lambda e, nsl=nsl: e.tensor_tensor(
                EW[:, nsl * 512:(nsl + 1) * 512], PS[2 + nsl][:], BOUT[:, ts(nsl, 512)], A.add))
        b.barrier()
        b.on("vector", lambda e, t=t: e.tensor_tensor(X3[:, ts(t, D)], XSTG[:], EW[:, 0:1024], A.add))
        b.barrier()

    if stage == "l1attn":
        b.on("sync", lambda e: e.dma_start(out=out_own[:], in_=X3[:]), dma=True)
        b.barrier()
        b.build_block()
        for c in reversed(ctxs):
            c.__exit__(None, None, None)
        arena.free()
        return nc

    b.on("vector", lambda e: e.memset(MRV[:, 1:8], 1e38))
    b.on("gpsimd", lambda e: e.dma_start(out=GATEW[:], in_=gatew_in[:]), dma=True)
    b.on("gpsimd", lambda e: e.dma_start(out=BOUT[0:8, :], in_=b2mat_in[:]), dma=True)
    b.on("gpsimd", lambda e: e.dma_start(out=TROW[0:1, 0:8], in_=gbrow_in[:]), dma=True)
    b.on("gpsimd", lambda e: e.dma_start(out=NSC[:], in_=nsc_in[:]), dma=True)
    b.barrier()

    def _moe_tile(t):
        xt = X3[:, ts(t, D)]
        emit_ln_tile(xt, MEVAC[:, 0:1024])
        b.on("scalar", lambda e: e.activation(XN[:, 0:D], MEVAC[:, 0:1024], AF.Square,
                                              accum_out=SM[:, 0:1]))
        b.barrier()
        b.on("vector", lambda e: e.tensor_scalar(SM[:, 1:2], SM[:, 0:1], 1.0 / D, None, A.mult))
        b.barrier()
        b.on("scalar", lambda e: e.activation(SM[:, 2:3], SM[:, 1:2], AF.Sqrt,
                                              bias=CONSTS[:, 1:2], scale=1.0))
        b.barrier()
        b.on("vector", lambda e: e.reciprocal(SM[:, 3:4], SM[:, 2:3]))
        b.barrier()
        b.on("vector", lambda e: e.tensor_scalar(XN[:, 0:D], MEVAC[:, 0:1024], SM[:, 3:4], None, A.mult))
        b.on("vector", lambda e: e.tensor_tensor(xt, xt, MEVAC[:, 0:1024], A.add))
        b.barrier()
        for c in range(8):
            b.on("tensor", lambda e, c=c: e.matmul(
                PSB[0][:, ts(c)], XN[:, 128 * c:128 * (c + 1)],
                IDENT[:], is_transpose=True, start=(c == 0), stop=(c == 7)))
        b.barrier()
        for c in range(8):
            b.on("vector", lambda e, t=t, c=c: e.tensor_scalar(
                TT2[:, c * NOWN + t * 128:c * NOWN + (t + 1) * 128],
                PSB[0][:, ts(c)], NSC[:, c:c + 1], None, A.mult))
        b.barrier()
        for c in range(8):
            b.on("tensor", lambda e, t=t, c=c: e.matmul(
                PS[2][:, 0:8], TT2[:, c * NOWN + t * 128:c * NOWN + (t + 1) * 128],
                GATEW[:, c * 8:(c + 1) * 8], start=(c == 0), stop=False))
        b.on("tensor", lambda e: e.matmul(
            PS[2][:, 0:8], ONES[0:1, :], TROW[0:1, 0:8], start=False, stop=True))
        b.barrier()
        b.on("vector", lambda e: e.tensor_copy(EX32[:, 0:8], PS[2][:, 0:8]))
        b.barrier()
        b.on("vector", lambda e: e.max(out=EX32[:, 8:16], in_=EX32[:, 0:8]))
        b.barrier()
        b.on("vector", lambda e: e.tensor_copy(MRV[:, 0:1], EX32[:, 8:9]))
        b.barrier()
        b.on("vector", lambda e: e.match_replace(out=EX32[:, 16:24], in_to_replace=MRV[:, 0:8],
                                                 in_values=EX32[:, 0:8], imm_value=-1e30))
        b.barrier()
        b.on("vector", lambda e: e.tensor_copy(MRV[:, 0:1], EX32[:, 9:10]))
        b.barrier()
        b.on("vector", lambda e: e.match_replace(out=EX32[:, 24:32], in_to_replace=MRV[:, 0:8],
                                                 in_values=EX32[:, 16:24], imm_value=-1e30))
        b.barrier()
        b.on("vector", lambda e: e.tensor_tensor(SM[:, 8:16], EX32[:, 0:8], EX32[:, 16:24], A.not_equal))
        b.on("vector", lambda e: e.tensor_tensor(SM[:, 16:24], EX32[:, 16:24], EX32[:, 24:32], A.not_equal))
        b.barrier()
        b.on("vector", lambda e: e.tensor_tensor(SM[:, 24:25], EX32[:, 9:10], EX32[:, 8:9], A.subtract))
        b.barrier()
        b.on("scalar", lambda e: e.activation(SM[:, 25:26], SM[:, 24:25], AF.Exp))
        b.barrier()
        b.on("vector", lambda e: e.tensor_scalar(SM[:, 26:27], SM[:, 25:26], 1.0, None, A.add))
        b.barrier()
        b.on("vector", lambda e: e.reciprocal(SM[:, 27:28], SM[:, 26:27]))
        b.barrier()
        b.on("vector", lambda e: e.tensor_tensor(SM[:, 28:29], SM[:, 25:26], SM[:, 27:28], A.mult))
        b.barrier()
        b.on("vector", lambda e: e.tensor_scalar(SM[:, 8:16], SM[:, 8:16], SM[:, 27:28], None, A.mult))
        b.on("vector", lambda e: e.tensor_scalar(SM[:, 16:24], SM[:, 16:24], SM[:, 28:29], None, A.mult))
        b.barrier()
        b.on("vector", lambda e, t=t: e.tensor_tensor(
            WTOK[:, t * 8:(t + 1) * 8], SM[:, 8:16], SM[:, 16:24], A.add))
        b.barrier()


    for t in range(NTO):
        _moe_tile(t)

    b.on("sync", lambda e: e.dma_start(out=B1G[:], in_=b1g_in[:]), dma=True)
    b.on("sync", lambda e: e.dma_start(out=B1L[:], in_=b1l_in[:]), dma=True)
    b.barrier()

    def _expert(e_):
        def w8tg_prep(grp):
            t0 = 2 * grp
            return (
                [lambda e, t0=t0: e.tensor_copy(WTB[:], WTOK[:, t0 * 8:(t0 + 2) * 8])],
                [lambda e, i2=i2: e.matmul(
                    PSB[1 + i2][0:8, 0:128], WTB[:, 8 * i2:8 * i2 + 8],
                    IDENT[:], is_transpose=True, start=True, stop=True) for i2 in range(2)],
                [lambda e, i2=i2: e.tensor_copy(
                    W8TG[0:8, i2 * 128:(i2 + 1) * 128], PSB[1 + i2][0:8, 0:128]) for i2 in range(2)],
            )

        def mm_list(grp, hc):
            t0 = 2 * grp
            p = hc % 2
            out = []
            for half in (0, 1):
                bank = 2 * p + half
                for c in range(8):
                    out.append(lambda e, hc=hc, half=half, bank=bank, c=c, t0=t0: e.matmul(
                        PS[bank][:, 0:256],
                        W1E[:, c * 4096 + half * 2048 + hc * 128:c * 4096 + half * 2048 + hc * 128 + 128],
                        TT2[:, c * NOWN + t0 * 128:c * NOWN + (t0 + 2) * 128],
                        start=(c == 0), stop=(c == 7)))
            return out

        def evac_phases(hc):
            p = hc % 2
            g, l = 2 * p, 2 * p + 1
            hs = slice(256 * p, 256 * p + 256)
            ph1 = [("vector", lambda e, hc=hc, l=l: e.tensor_scalar(
                        PS[l][:, 0:256], PS[l][:, 0:256],
                        B1L[:, e_ * 16 + hc:e_ * 16 + hc + 1], 7.0, A.add, A.min)),
                   ("vector", lambda e, hc=hc, g=g: e.tensor_scalar(
                        PS[g][:, 0:256], PS[g][:, 0:256],
                        B1G[:, e_ * 16 + hc:e_ * 16 + hc + 1], 7.0, A.add, A.min))]
            ph2 = [("scalar", lambda e, g=g, hs=hs: e.activation(
                        HLF[:, hs], PS[g][:, 0:256], AF.Gelu_apprx_sigmoid)),
                   ("vector", lambda e, l=l: e.tensor_scalar(
                        PS[l][:, 0:256], PS[l][:, 0:256], -7.0, 1.0, A.max, A.add))]
            ph3 = [("vector", lambda e, hc=hc, l=l, hs=hs: e.tensor_tensor(
                        SWIF[:, hc * 256:(hc + 1) * 256], HLF[:, hs], PS[l][:, 0:256], A.mult))]
            return ph1, ph2, ph3

        b.barrier()
        if e_ == 0:
            p1, p2, p3 = w8tg_prep(0)
            for f in p1:
                b.on("vector", f)
            b.barrier()
            for f in p2:
                b.on("tensor", f)
            b.barrier()
            for f in p3:
                b.on("vector", f)
            b.barrier()
        for grp in range(8):
            t0 = 2 * grp
            for hc in range(17):
                mm = mm_list(grp, hc) if hc < 16 else []
                ev = evac_phases(hc - 1) if hc > 0 else ([], [], [])
                for k, (lo, hi) in enumerate(((0, 6), (6, 11), (11, 16))):
                    for f in mm[lo:hi]:
                        b.on("tensor", f)
                    for eng, f in ev[k]:
                        b.on(eng, f)
                    b.barrier()
            combos = [(i, nsl) for i in range(2) for nsl in range(2)]
            def combo_mms(ci):
                i, nsl = combos[ci]
                out = []
                for hc in range(16):
                    out.append(lambda e, i=i, nsl=nsl, hc=hc, bank=ci: e.matmul(
                        PS[bank][:],
                        SWIF[:, hc * 256 + i * 128:hc * 256 + i * 128 + 128],
                        W2E[:, hc * D + nsl * 512:hc * D + (nsl + 1) * 512],
                        start=(hc == 0), stop=False))
                out.append(lambda e, i=i, nsl=nsl, bank=ci: e.matmul(
                    PS[bank][:], W8TG[0:8, i * 128:(i + 1) * 128],
                    BOUT[0:8, nsl * 512:(nsl + 1) * 512],
                    start=False, stop=True))
                return out

            def evac_mm2(ci):
                i, nsl = combos[ci]
                t = t0 + i
                return lambda e, t=t, bank=ci, ci=ci, e_=e_: e.tensor_scalar(
                    MEVAC[:, ci * 512:(ci + 1) * 512],
                    PS[bank][:], WTOK[:, t * 8 + e_:t * 8 + e_ + 1], None, A.mult)

            def add_mm2(ci):
                i, nsl = combos[ci]
                t = t0 + i
                return lambda e, t=t, nsl=nsl, ci=ci: e.tensor_tensor(
                    X3[:, t * D + nsl * 512:t * D + (nsl + 1) * 512],
                    X3[:, t * D + nsl * 512:t * D + (nsl + 1) * 512],
                    MEVAC[:, ci * 512:(ci + 1) * 512], A.add)

            last_grp = grp == 7
            nxt = grp + 1 if not last_grp else (0 if e_ < 7 else None)
            for f in combo_mms(0) + combo_mms(1):
                b.on("tensor", f)
            if last_grp and e_ < 7:
                b.on("gpsimd", lambda e, e_=e_: e.dma_start(out=W1E[:], in_=w1e_in[e_ + 1][:]), dma=True)
            if nxt is not None:
                p1, p2, p3 = w8tg_prep(nxt)
                for f in p1:
                    b.on("vector", f)
            b.barrier()
            for f in combo_mms(2) + combo_mms(3):
                b.on("tensor", f)
            b.barrier()
            for ci in range(4):
                b.on("vector", evac_mm2(ci))
            if last_grp and e_ < 7:
                b.on("gpsimd", lambda e, e_=e_: e.dma_start(out=W2E[:], in_=w2e_in[e_ + 1][:]), dma=True)
            if nxt is not None:
                for f in p2:
                    b.on("tensor", f)
            b.barrier()
            for ci in range(4):
                b.on("vector", add_mm2(ci))
            if nxt is not None:
                for f in p3:
                    b.on("vector", f)
            b.barrier()

    b.on("gpsimd", lambda e: e.dma_start(out=W1E[:], in_=w1e_in[0][:]), dma=True)
    b.on("gpsimd", lambda e: e.dma_start(out=W2E[:], in_=w2e_in[0][:]), dma=True)
    for e_ in range(8):
        _expert(e_)

    b.on("sync", lambda e: e.dma_start(out=out_own[:], in_=X3[:]), dma=True)
    b.barrier()

    b.build_block()
    for c in reversed(ctxs):
        c.__exit__(None, None, None)
    arena.free()
    return nc


def chunk_major(w):
    w = np.asarray(w, dtype=np.float32)
    c = w.shape[0] // 128
    return np.ascontiguousarray(w.reshape(c, 128, w.shape[1]).transpose(1, 0, 2).reshape(128, -1))


def tok_major(x):
    x = np.asarray(x, dtype=np.float32)
    t = x.shape[0] // 128
    return np.ascontiguousarray(x.reshape(t, 128, x.shape[1]).transpose(1, 0, 2).reshape(128, -1))


def feat_cols(bvec):
    b_ = np.asarray(bvec, dtype=np.float32).reshape(-1)
    c = b_.shape[0] // 128
    return np.ascontiguousarray(b_.reshape(c, 128).T)


def stage_inputs(ins, core=0):
    bidx, h = core // 2, core % 2
    x = np.asarray(ins["L0_attn_x"])[bidx]
    own = x[h * 2048:(h + 1) * 2048]
    part = x[(1 - h) * 2048:(2 - h) * 2048]
    m = {
        "x_own": tok_major(own),
        "x_part": tok_major(part),
        "wqkvT": chunk_major(np.asarray(ins["L0_attn_w_qkv"]).T),
        "woutT": chunk_major(np.asarray(ins["L0_attn_w_out"]).T),
        "bout": np.asarray(ins["L0_attn_b_out"], dtype=np.float32).reshape(1, -1),
        "ar": np.asarray(ins["L0_ffn_adaptive_range"], dtype=np.float32).reshape(1, 1),
    }
    for i in range(1, 5):
        w = np.asarray(ins[f"L0_ffn_cn_w{i}"], dtype=np.float32)
        if i == 4:
            w = np.pad(w, ((0, 0), (0, 127)))
        m[f"cn_w{i}"] = chunk_major(w)
        bv = np.asarray(ins[f"L0_ffn_cn_b{i}"], dtype=np.float32)
        m[f"cn_b{i}"] = feat_cols(bv) if i < 4 else np.tile(bv.reshape(1, 1), (128, 1))
    for i in range(1, 4):
        w = np.asarray(ins[f"L0_ffn_gc_w{i}"], dtype=np.float32)
        if i == 3:
            w = np.pad(w, ((0, 0), (0, 127)))
        m[f"gc_w{i}"] = chunk_major(w)
        bv = np.asarray(ins[f"L0_ffn_gc_b{i}"], dtype=np.float32)
        m[f"gc_b{i}"] = feat_cols(bv) if i < 3 else np.tile(bv.reshape(1, 1), (128, 1))
    m["wq"] = chunk_major(np.asarray(ins["L1_attn_Wq"]))
    m["wk"] = chunk_major(np.asarray(ins["L1_attn_Wk"]))
    m["wv"] = chunk_major(np.asarray(ins["L1_attn_Wv"]))
    m["wo"] = chunk_major(np.asarray(ins["L1_attn_Wo"]))
    m["bq"] = feat_cols(ins["L1_attn_bq"])
    m["bk"] = feat_cols(ins["L1_attn_bk"])
    m["bvrow"] = np.asarray(ins["L1_attn_bv"], dtype=np.float32).reshape(1, -1)
    m["borow"] = np.asarray(ins["L1_attn_bo"], dtype=np.float32).reshape(1, -1)
    for e in range(8):
        w1 = np.asarray(ins["L1_ffn_mlp1_w"], dtype=np.float32)[e]
        w1d = np.concatenate([w1[:, 0::2], w1[:, 1::2]], axis=1)
        m[f"w1e{e}"] = chunk_major(w1d)
        m[f"w2e{e}"] = chunk_major(np.asarray(ins["L1_ffn_mlp2_w"], dtype=np.float32)[e])
    b1 = np.asarray(ins["L1_ffn_mlp1_b"], dtype=np.float32)
    cols = []
    for e in range(8):
        g = b1[e, 0::2].reshape(16, 128).T
        l = b1[e, 1::2].reshape(16, 128).T
        cols.append(np.concatenate([g, l], axis=1))
    m["b1g"] = np.ascontiguousarray(np.concatenate([c[:, 0:16] for c in cols], axis=1))
    m["b1l"] = np.ascontiguousarray(np.concatenate([c[:, 16:32] for c in cols], axis=1))
    m["b2mat"] = np.ascontiguousarray(np.asarray(ins["L1_ffn_mlp2_b"], dtype=np.float32))
    gw = np.asarray(ins["L1_ffn_gate_w"], dtype=np.float32)
    m["gatew"] = np.ascontiguousarray(gw.reshape(8, 128, 8).transpose(1, 0, 2).reshape(128, 64))
    m["gbrow"] = np.asarray(ins["L1_ffn_gate_b"], dtype=np.float32).reshape(1, -1)
    m["nsc"] = feat_cols(ins["L1_ffn_norm_scale"])
    return m

def kernel(**inputs):
    nc = build_program("full")
    in_maps = [stage_inputs(inputs, core=c) for c in range(8)]
    r = run_bass_kernel_spmd(nc, in_maps, core_ids=list(range(8)))
    out = np.empty((4, 4096, 1024), dtype=np.float32)
    for c in range(8):
        got = np.asarray(r.results[c]["out_own"], dtype=np.float32)
        got = got.reshape(128, 16, 1024).transpose(1, 0, 2).reshape(2048, 1024)
        out[c // 2, (c % 2) * 2048:(c % 2 + 1) * 2048] = got
    return out
